# version 02
# baseline (speedup 1.0000x reference)
import numpy as np
from contextlib import ExitStack
import concourse.bass as bass
import concourse.mybir as mybir
from concourse.bass_utils import run_bass_kernel_spmd
import ml_dtypes

bf16 = ml_dtypes.bfloat16
F32 = mybir.dt.float32
BF16 = mybir.dt.bfloat16
I8 = mybir.dt.int8
I32 = mybir.dt.int32
AF = mybir.ActivationFunctionType
ALU = mybir.AluOpType
AX = mybir.AxisListType
_INLINE_N = [0]

RS128 = float(1.0 / np.sqrt(128.0))
SOFTMAX_C = 40.0

_INLINE_N = [0]
_NAME_N = [0]


def _u(nm):
    _NAME_N[0] += 1
    return f"{nm}_u{_NAME_N[0]}"


class Sem:
    def __init__(self, handle):
        self.h = handle
        self.n = 0

    def inc(self, ins, k=1):
        ins.then_inc(self.h, k)
        self.n += k
        return ins

    def inc_dma(self, ins):
        return self.inc(ins, 16)


class SemView:
    def __init__(self, sem):
        self.s = sem
        self.base = sem.n

    @property
    def h(self):
        return self.s.h

    @property
    def n(self):
        return self.s.n - self.base

    def inc(self, ins, k=1):
        self.s.inc(ins, k)
        return ins

    def inc_dma(self, ins):
        return self.inc(ins, 16)


def mk_sems(nc, stack, names):
    pool = getattr(nc, "_ag_sem_pool", None)
    if pool is None:
        out = {}
        for nm in names:
            h = stack.enter_context(nc.semaphore(nm))
            s = Sem(h)
            s.base = 0
            out[nm] = s
        return out
    gstack = nc._ag_sem_stack
    while len(pool) < len(names):
        i = len(pool)
        pool.append(Sem(gstack.enter_context(nc.semaphore(f"sp{i}"))))
    return {nm: SemView(pool[i]) for i, nm in enumerate(names)}


def wait_all(eng, *sems):
    for s in sems:
        if s.n > 0:
            eng.wait_ge(s.h, getattr(s, "base", 0) + s.n)


def inline_const(nc, name, arr):
    _INLINE_N[0] += 1
    name = f"{name}_{_INLINE_N[0]}"
    if arr.dtype == bf16:
        t = nc.inline_tensor(np.ascontiguousarray(arr.view(np.uint16)), name=name)
        return t, BF16
    t = nc.inline_tensor(np.ascontiguousarray(arr), name=name)
    return t, None


def load_const(eng, sb_ap, dram_t, cast):
    src = dram_t.ap()
    if cast is not None:
        src = src.bitcast(cast)
    return eng.dma_start(out=sb_ap, in_=src)


def dram_io(nc, io, name, shape, dtype, kind):
    if io is not None and name in io:
        return io[name]
    return nc.dram_tensor(name, shape, dtype, kind=kind).ap()


D = 2048
FEAT = 1024
NH = 8
NFC = 16
NI = 8192
NIC = 64
CONVD = 4224
NCC = 33
DIN = 4096
NST = 64
EPS_RMS = 1.1920929e-07
NGX = 8


D = 2048
FEAT = 1024
NH = 8
NFC = D // 128

NGX = 8


D = 2048
FEAT = 1024
NH = 8
NFC = D // 128

NGX = 8


D = 2048
FEAT = 1024
NH = 8
NFC = D // 128


def emit_a1(nc, S, stack, io=None):
    NTT = S // 128
    NTB = S // 512
    NKT = S // 128
    NV = NKT // 4
    NG = NTB * NKT
    sb = lambda *a: stack.enter_context(nc.sbuf_tensor(_u(a[0]), *a[1:]))
    ps = lambda *a: stack.enter_context(nc.psum_tensor(_u(a[0]), *a[1:]))

    x_in = dram_io(nc, io, "x", [S, D], F32, "ExternalInput")
    w_in = {m: nc.dram_tensor("w" + m, [D, FEAT], F32, kind="ExternalInput").ap() for m in "qkv"}
    s_in = {m: nc.dram_tensor("s" + m, [64, FEAT], F32, kind="ExternalInput").ap() for m in "qkv"}
    b_in = {m: nc.dram_tensor("b" + m, [128, NH], F32, kind="ExternalInput").ap() for m in "qkv"}
    bvr_in = dram_io(nc, io, "bvr", [1, FEAT], F32, "ExternalInput")
    attn_out = dram_io(nc, io, "attn", [FEAT, S], BF16, "ExternalOutput")
    xn_out = dram_io(nc, io, "xn", [D, S], BF16, "ExternalOutput")

    ones_1x128_bf, c1 = inline_const(nc, "ones_1x128", np.ones((1, 128), dtype=bf16))
    ones_128x1_bf, c2 = inline_const(nc, "ones_128x1", np.ones((128, 1), dtype=bf16))
    ones_1x128_f, c3 = inline_const(nc, "ones_1x128f", np.ones((1, 128), dtype=np.float32))
    ident_bf, c4 = inline_const(nc, "ident", np.eye(128, dtype=bf16))
    e4 = np.zeros((4, 128), dtype=bf16)
    for r in range(4):
        e4[r, 32 * r:32 * (r + 1)] = 1
    e4_bf, c5 = inline_const(nc, "e4sel", e4)

    x_tile = sb("x_tile", [128, D], F32)
    xn_tm = sb("xn_tm", [128, D], BF16)
    xn0 = [sb(f"xn0_{fi}", [128, S], BF16) for fi in range(NFC)]
    wdeq = {m: sb(f"wdeq_{m}", [128, 16 * 128], BF16) for m in "qkv"}
    codes_sb = [sb(f"codes{i}", [128, 512], F32) for i in range(2)]
    scales_sb = sb("scales", [4, 4 * 128], F32)
    scales_bf = sb("scales_bf", [4, 4 * 128], BF16)
    bias_sb = {m: sb(f"bias_{m}", [128, NH], F32) for m in "qkv"}
    bv_bf = sb("bv_bf", [1, FEAT], BF16)
    K_h = sb("K_h", [128, S], BF16)
    Q_h = sb("Q_h", [128, S], BF16)
    V_tm = sb("V_tm", [128, S], BF16)
    PT = [sb(f"PT{i}", [128, 512], BF16) for i in range(3)]
    sq_s = sb("sq_s", [128, 512], BF16)
    atmp = sb("atmp", [128, 512], F32)
    att_bf = [sb(f"att_bf{i}", [128, 512], BF16) for i in range(2)]
    stat = sb("stat", [128, 8], F32)
    mtmp = sb("mtmp", [1, 512], F32)
    sums_s = sb("sums_s", [1, 512], F32)
    sums_bf = sb("sums_bf", [1, 512], BF16)
    recip_s = sb("recip_s", [1, 512], F32)
    ksq_s = sb("ksq_s", [1, 512], F32)
    km = sb("km", [1, 6], F32)
    mng_bf = sb("mng_bf", [1, 2], BF16)
    bcol = sb("bcol", [128, 2], F32)
    ones1_bf = sb("ones1_bf", [1, 128], BF16)
    ones128_bf = sb("ones128_bf", [128, 1], BF16)
    ones1_f = sb("ones1_f", [1, 128], F32)
    ident_sb = sb("ident_sb", [128, 128], BF16)
    e4_sb = sb("e4_sb", [4, 128], BF16)
    fc = sb("fc", [128, 4], F32)

    p0 = ps("p0", [128, 512], F32)
    p1 = ps("p1", [128, 512], F32)
    p2 = ps("p2", [128, 512], F32)
    p3 = ps("p3", [128, 512], F32)
    pT = ps("pT", [128, 1024], BF16)
    p4 = ps("p4", [128, 512], F32)
    p5 = ps("p5", [128, 512], F32)
    p6 = ps("p6", [128, 512], F32)

    names = ["s_in0", "s_in1", "s_ld0", "s_ld1", "s_cw", "s_bvp", "s_fc",
             "s_sq", "s_rs", "s_stat", "s_rstd", "s_ap", "s_tr", "s_ev", "s_xout",
             "s_sb", "s_scv", "s_deq", "s_qkpe", "s_vpe", "s_qke", "s_ve",
             "s_qs", "s_qsm", "s_qse", "s_kmx", "s_kacc", "s_lnk", "s_mfin",
             "s_mexp", "s_bbf", "s_mng", "s_bcm",
             "s_sc", "s_pt", "s_av", "s_smv", "s_sbf", "s_bv", "s_evA",
             "s_rcl", "s_rcp", "s_rb", "s_nrm", "s_aout0", "s_aout1", "s_v", "s_a"]
    S_ = mk_sems(nc, stack, names)

    def W(eng, sem, v):
        if v > 0:
            eng.wait_ge(S_[sem].h, S_[sem].base + v)

    with nc.Block() as block:
        @block.sync
        def _(sync):
            S_["s_cw"].inc_dma(load_const(sync, ones1_bf[:], ones_1x128_bf, c1))
            S_["s_cw"].inc_dma(load_const(sync, ones128_bf[:], ones_128x1_bf, c2))
            S_["s_cw"].inc_dma(load_const(sync, ones1_f[:], ones_1x128_f, c3))
            S_["s_cw"].inc_dma(load_const(sync, ident_sb[:], ident_bf, c4))
            S_["s_cw"].inc_dma(load_const(sync, e4_sb[:], e4_bf, c5))
            for m in "qkv":
                S_["s_cw"].inc_dma(sync.dma_start(out=bias_sb[m][:], in_=b_in[m]))
            S_["s_cw"].inc_dma(sync.dma_start(out=xn_tm[0:1, :].bitcast(F32), in_=bvr_in))
            for t in range(NTT):
                W(sync, "s_ap", t)
                S_[f"s_in{t % 2}"].inc_dma(sync.dma_start(out=x_tile[:], in_=x_in[t * 128:(t + 1) * 128, :]))
            for h in range(NH):
                for gi in range(12):
                    m = "qkv"[gi // 4]
                    g = gi % 4
                    W(sync, "s_deq", 12 * h + gi - 1)
                    src = w_in[m][512 * g:512 * (g + 1), 128 * h:128 * (h + 1)].rearrange("(a p) c -> p a c", p=128)
                    S_[f"s_ld{gi % 2}"].inc_dma(sync.dma_start(
                        out=codes_sb[gi % 2][:].rearrange("p (a c) -> p a c", a=4), in_=src))
                    W(sync, "s_scv", 12 * h + gi)
                    ssrc = s_in[m][16 * g:16 * (g + 1), 128 * h:128 * (h + 1)].rearrange("(a r) c -> r a c", r=4)
                    S_[f"s_ld{gi % 2}"].inc_dma(sync.dma_start(
                        out=scales_sb[:].rearrange("r (a c) -> r a c", a=4), in_=ssrc))

        CW0 = 16 * 9
        ld_at = lambda h, gi: 32 * ((12 * h + gi) // 2 + 1)

        @block.scalar
        def _(scalar):
            for t in range(NTT):
                W(scalar, f"s_in{t % 2}", 16 * (t // 2 + 1))
                W(scalar, "s_stat", 4 * t)
                W(scalar, "s_ap", t)
                W(scalar, "s_tr", 2 * t)
                if t == 0:
                    W(scalar, "s_bvp", 1)
                    W(scalar, "s_fc", 1)
                S_["s_sq"].inc(scalar.activation(out=xn_tm[:], in_=x_tile[:],
                                                 func=AF.Square, accum_out=stat[:, 1:2]))
                W(scalar, "s_stat", 4 * (t + 1))
                S_["s_a"].inc(scalar.activation(out=stat[:, 6:7], in_=stat[:, 5:6], func=AF.Ln, bias=fc[:, 0:1]))
                W(scalar, "s_a", S_["s_a"].n)
                S_["s_rstd"].inc(scalar.activation(out=stat[:, 7:8], in_=stat[:, 6:7], func=AF.Exp, scale=fc[:, 1:2]), 2)
            for h in range(NH):
                for g in range(NV):
                    W(scalar, "s_vpe", NV * h + g + 1)
                    W(scalar, "s_av", NG * h)
                    S_["s_ve"].inc(scalar.activation(out=V_tm[:, g * 512:(g + 1) * 512], in_=p2[:], func=AF.Copy))
                for tb in range(NTB):
                    idx = 2 * NTB * h + tb
                    W(scalar, "s_qke", NTB * (h + 1))
                    W(scalar, "s_qsm", idx)
                    S_["s_qs"].inc(scalar.activation(out=sq_s[:], in_=K_h[:, tb * 512:(tb + 1) * 512],
                                                     func=AF.Square))
                    W(scalar, "s_qsm", idx + 1)
                    W(scalar, "s_kmx", NTB * h + tb)
                    S_["s_qse"].inc(scalar.activation(out=ksq_s[:], in_=p4[0:1, :], func=AF.Copy))
                W(scalar, "s_kacc", NTB * (h + 1))
                W(scalar, "s_mfin", NTB * h)
                S_["s_lnk"].inc(scalar.activation(out=km[:, 1:2], in_=km[:, 0:1], func=AF.Ln))
                def emit_sq(c):
                    idx = 2 * NTB * h + NTB + c
                    W(scalar, "s_qsm", idx)
                    S_["s_qs"].inc(scalar.activation(out=sq_s[:], in_=Q_h[:, c * 512:(c + 1) * 512],
                                                     func=AF.Square))
                def emit_chain_sc(c):
                    nc_ = NTB * h + c
                    idx = 2 * NTB * h + NTB + c
                    W(scalar, "s_qsm", idx + 1)
                    W(scalar, "s_mexp", nc_)
                    S_["s_qse"].inc(scalar.activation(out=mtmp[:], in_=p4[0:1, :], func=AF.Ln))
                    W(scalar, "s_mfin", nc_ + 1)
                    S_["s_mexp"].inc(scalar.activation(out=mtmp[:], in_=mtmp[:], func=AF.Exp, scale=fc[0:1, 3:4]))
                    W(scalar, "s_bcm", nc_ + 1)
                    W(scalar, "s_pt", NG * h + max(0, c - 1) * NKT)
                    S_["s_bbf"].inc(scalar.activation(out=bcol[:, c % 2:c % 2 + 1],
                                                      in_=p6[:, c % 2:c % 2 + 1], func=AF.Copy))
                for qb in range(NTB):
                    nqb = NTB * h + qb
                    if qb == 0:
                        emit_sq(0)
                        emit_chain_sc(0)
                    if qb + 1 < NTB:
                        emit_sq(qb + 1)
                    for k in range(NKT):
                        g = NG * h + qb * NKT + k
                        W(scalar, "s_sc", g + 1)
                        W(scalar, "s_av", g - 2)
                        if k == 0:
                            W(scalar, "s_bbf", nqb + 1)
                        pb = [p0, p1, p5][g % 3]
                        S_["s_pt"].inc(scalar.activation(out=PT[g % 3][:], in_=pb[:], func=AF.Exp,
                                                         bias=bcol[:, qb % 2:qb % 2 + 1]))
                    W(scalar, "s_av", NG * h + (qb + 1) * NKT)
                    W(scalar, "s_sbf", nqb)
                    W(scalar, "s_rcl", nqb)
                    if nqb == 0:
                        W(scalar, "s_bvp", 1)
                    S_["s_smv"].inc(scalar.activation(out=sums_s[:], in_=p3[0:1, :], func=AF.Copy))
                    W(scalar, "s_rcl", nqb + 1)
                    S_["s_a"].inc(scalar.activation(out=recip_s[:], in_=recip_s[:], func=AF.Ln))
                    W(scalar, "s_a", S_["s_a"].n)
                    S_["s_rcp"].inc(scalar.activation(out=recip_s[:], in_=recip_s[:], func=AF.Exp,
                                                      scale=fc[0:1, 2:3]))
                    W(scalar, "s_bv", nqb + 1)
                    W(scalar, "s_nrm", nqb)
                    S_["s_evA"].inc(scalar.activation(out=atmp[:], in_=p2[:], func=AF.Copy))
                    if qb + 1 < NTB:
                        emit_chain_sc(qb + 1)

        @block.vector
        def _(vector):
            vector.memset(fc[:, 0:1], 1e-5)
            vector.memset(fc[:, 1:2], -0.5)
            vector.memset(fc[:, 2:3], -1.0)
            S_["s_fc"].inc(vector.memset(fc[:, 3:4], 0.5))
            W(vector, "s_cw", CW0)
            S_["s_bvp"].inc(vector.tensor_copy(out=bv_bf[:], in_=xn_tm[0:1, :].bitcast(F32)))
            for t in range(NTT):
                W(vector, f"s_in{t % 2}", 16 * (t // 2 + 1))
                W(vector, "s_stat", 4 * t)
                S_["s_rs"].inc(vector.tensor_reduce(out=stat[:, 0:1], in_=x_tile[:], axis=AX.X, op=ALU.add))
                W(vector, "s_rs", S_["s_rs"].n)
                S_["s_v"].inc(vector.tensor_scalar_mul(out=stat[:, 2:3], in0=stat[:, 0:1], scalar1=1.0 / D))
                W(vector, "s_v", S_["s_v"].n)
                S_["s_v"].inc(vector.tensor_tensor(out=stat[:, 4:5], in0=stat[:, 2:3], in1=stat[:, 2:3], op=ALU.mult))
                W(vector, "s_v", S_["s_v"].n)
                W(vector, "s_sq", t + 1)
                S_["s_stat"].inc(vector.scalar_tensor_tensor(out=stat[:, 5:6], in0=stat[:, 1:2], scalar=1.0 / D,
                                                             in1=stat[:, 4:5], op0=ALU.mult, op1=ALU.subtract), 4)
                W(vector, "s_rstd", 2 * (t + 1))
                W(vector, "s_tr", 2 * t)
                S_["s_ap"].inc(vector.tensor_scalar(out=xn_tm[:], in0=x_tile[:],
                                                    scalar1=stat[:, 2:3], scalar2=stat[:, 7:8],
                                                    op0=ALU.subtract, op1=ALU.mult))
                for half in range(2):
                    W(vector, "s_tr", 2 * t + half + 1)
                    for j in range(8):
                        fi = half * 8 + j
                        ins = vector.tensor_copy(out=xn0[fi][:, t * 128:(t + 1) * 128],
                                                 in_=pT[:, j * 128:(j + 1) * 128])
                    S_["s_ev"].inc(ins)
            for h in range(NH):
                for gi in range(12):
                    m = "qkv"[gi // 4]
                    g = gi % 4
                    W(vector, f"s_ld{gi % 2}", ld_at(h, gi))
                    W(vector, "s_sb", 12 * h + gi)
                    S_["s_scv"].inc(vector.tensor_copy(out=scales_bf[:], in_=scales_sb[:]))
                    W(vector, "s_sb", 12 * h + gi + 1)
                    if gi == 0:
                        W(vector, "s_qkpe", NTB * h)
                        W(vector, "s_vpe", NV * h)
                    S_["s_deq"].inc(vector.tensor_tensor(out=wdeq[m][:, g * 512:(g + 1) * 512],
                                                         in0=codes_sb[gi % 2][:], in1=p0[:], op=ALU.mult))
                for tb in range(NTB):
                    W(vector, "s_qkpe", NTB * h + tb + 1)
                    W(vector, "s_sc", NG * h)
                    vector.tensor_scalar(out=Q_h[:, tb * 512:(tb + 1) * 512], in0=p0[:],
                                         scalar1=bias_sb["q"][:, h:h + 1], scalar2=RS128,
                                         op0=ALU.add, op1=ALU.mult)
                    S_["s_qke"].inc(vector.tensor_scalar(out=K_h[:, tb * 512:(tb + 1) * 512], in0=p1[:],
                                                         scalar1=bias_sb["k"][:, h:h + 1], scalar2=None,
                                                         op0=ALU.add))
                for tb in range(NTB):
                    W(vector, "s_qse", 2 * NTB * h + tb + 1)
                    W(vector, "s_kacc", NTB * h + tb)
                    S_["s_kmx"].inc(vector.tensor_reduce(out=km[:, 2:3], in_=ksq_s[:], axis=AX.X, op=ALU.max))
                    W(vector, "s_kmx", NTB * h + tb + 1)
                    if tb == 0:
                        W(vector, "s_lnk", h)
                        S_["s_kacc"].inc(vector.tensor_copy(out=km[:, 0:1], in_=km[:, 2:3]))
                    else:
                        S_["s_kacc"].inc(vector.tensor_tensor(out=km[:, 0:1], in0=km[:, 0:1],
                                                              in1=km[:, 2:3], op=ALU.max))
                def emit_chain_v(c):
                    nc_ = NTB * h + c
                    W(vector, "s_qse", 2 * NTB * h + NTB + c + 1)
                    W(vector, "s_lnk", h + 1)
                    S_["s_mfin"].inc(vector.tensor_scalar_add(out=mtmp[:], in0=mtmp[:], scalar1=km[:, 1:2]))
                    W(vector, "s_mexp", nc_ + 1)
                    W(vector, "s_mng", nc_)
                    S_["s_v"].inc(vector.tensor_reduce(out=km[:, 3:4], in_=mtmp[:], axis=AX.X, op=ALU.max))
                    W(vector, "s_v", S_["s_v"].n)
                    W(vector, "s_bcm", nc_ - 1)
                    S_["s_mng"].inc(vector.tensor_scalar(out=mng_bf[0:1, c % 2:c % 2 + 1], in0=km[:, 3:4],
                                                         scalar1=-1.0, scalar2=SOFTMAX_C,
                                                         op0=ALU.mult, op1=ALU.add))
                for qb in range(NTB):
                    nqb = NTB * h + qb
                    if qb == 0:
                        emit_chain_v(0)
                    W(vector, "s_smv", nqb + 1)
                    W(vector, "s_bv", nqb)
                    S_["s_sbf"].inc(vector.tensor_copy(out=sums_bf[:], in_=sums_s[:]))
                    W(vector, "s_rb", nqb)
                    S_["s_rcl"].inc(vector.tensor_scalar_max(out=recip_s[:], in0=sums_s[:], scalar1=1e-30))
                    W(vector, "s_evA", nqb + 1)
                    W(vector, "s_rb", nqb + 1)
                    W(vector, f"s_aout{nqb % 2}", 16 * (nqb // 2))
                    S_["s_nrm"].inc(vector.tensor_tensor(out=att_bf[nqb % 2][:], in0=atmp[:], in1=p3[:], op=ALU.mult))
                    if qb + 1 < NTB:
                        emit_chain_v(qb + 1)

        @block.tensor
        def _(tensor):
            W(tensor, "s_cw", CW0)
            for t in range(NTT):
                W(tensor, "s_ap", t + 1)
                for half in range(2):
                    W(tensor, "s_ev", 2 * t + half)
                    for j in range(8):
                        fi = half * 8 + j
                        ins = tensor.transpose(pT[:, j * 128:(j + 1) * 128],
                                               xn_tm[:, fi * 128:(fi + 1) * 128], ident_sb[:])
                    S_["s_tr"].inc(ins)
            for h in range(NH):
                for gi in range(12):
                    W(tensor, "s_scv", 12 * h + gi + 1)
                    W(tensor, "s_deq", 12 * h + gi)
                    if gi == 0:
                        W(tensor, "s_pt", NG * h)
                    for j in range(4):
                        ins = tensor.matmul(p0[:, j * 128:(j + 1) * 128], e4_sb[:],
                                            scales_bf[:, j * 128:(j + 1) * 128],
                                            start=True, stop=True)
                    S_["s_sb"].inc(ins)
                W(tensor, "s_deq", 12 * (h + 1))
                if h == 0:
                    W(tensor, "s_ev", 2 * NTT)
                for tb in range(NTB):
                    W(tensor, "s_qke", NTB * h + tb)
                    for fi in range(NFC):
                        tensor.matmul(p0[:], wdeq["q"][:, fi * 128:(fi + 1) * 128],
                                      xn0[fi][:, tb * 512:(tb + 1) * 512],
                                      start=(fi == 0), stop=(fi == NFC - 1))
                    for fi in range(NFC):
                        ins = tensor.matmul(p1[:], wdeq["k"][:, fi * 128:(fi + 1) * 128],
                                            xn0[fi][:, tb * 512:(tb + 1) * 512],
                                            start=(fi == 0), stop=(fi == NFC - 1))
                    S_["s_qkpe"].inc(ins)
                for g in range(NV):
                    W(tensor, "s_ve", NV * h + g)
                    W(tensor, "s_evA", NTB * h)
                    for jj in range(4):
                        tt = 4 * g + jj
                        for fi in range(NFC):
                            ins = tensor.matmul(p2[:, jj * 128:(jj + 1) * 128],
                                                xn0[fi][:, tt * 128:(tt + 1) * 128],
                                                wdeq["v"][:, fi * 128:(fi + 1) * 128],
                                                start=(fi == 0), stop=(fi == NFC - 1))
                    S_["s_vpe"].inc(ins)
                for tb in range(NTB):
                    idx = 2 * NTB * h + tb
                    W(tensor, "s_qs", idx + 1)
                    W(tensor, "s_qse", idx)
                    S_["s_qsm"].inc(tensor.matmul(p4[0:1, :], ones128_bf[:], sq_s[:],
                                                  start=True, stop=True))
                W(tensor, "s_qke", NTB * (h + 1))
                W(tensor, "s_ve", NV * (h + 1))
                W(tensor, "s_bvp", 1)
                def emit_scores(qb, k):
                    g = NG * h + qb * NKT + k
                    W(tensor, "s_pt", g - 2)
                    pb = [p0, p1, p5][g % 3]
                    S_["s_sc"].inc(tensor.matmul(pb[:], K_h[:, k * 128:(k + 1) * 128],
                                                 Q_h[:, qb * 512:(qb + 1) * 512], start=True, stop=True))
                for qb in range(NTB):
                    nqb = NTB * h + qb
                    if qb == 0:
                        idx0 = 2 * NTB * h + NTB
                        W(tensor, "s_qs", idx0 + 1)
                        W(tensor, "s_qse", idx0)
                        S_["s_qsm"].inc(tensor.matmul(p4[0:1, :], ones128_bf[:], sq_s[:],
                                                      start=True, stop=True))
                        emit_scores(0, 0)
                        emit_scores(0, 1)
                        emit_scores(0, 2)
                        W(tensor, "s_mng", NTB * h + 1)
                        W(tensor, "s_bbf", max(0, NTB * h - 1))
                        S_["s_bcm"].inc(tensor.matmul(p6[:, 0:1], ones1_bf[:],
                                                      mng_bf[0:1, 0:1], start=True, stop=True))
                    for k in range(NKT):
                        g = NG * h + qb * NKT + k
                        if k + 3 < NKT:
                            emit_scores(qb, k + 3)
                        if k == 0:
                            W(tensor, "s_evA", nqb)
                            W(tensor, "s_nrm", nqb)
                        W(tensor, "s_pt", g + 1)
                        tensor.matmul(p2[:], V_tm[:, k * 128:(k + 1) * 128], PT[g % 3][:],
                                      start=(k == 0), stop=False)
                        S_["s_av"].inc(tensor.matmul(p3[0:1, :], ones128_bf[:], PT[g % 3][:],
                                                     start=(k == 0), stop=(k == NKT - 1)))
                    if qb + 1 < NTB:
                        idx = 2 * NTB * h + NTB + qb + 1
                        W(tensor, "s_qs", idx + 1)
                        W(tensor, "s_qse", idx)
                        S_["s_qsm"].inc(tensor.matmul(p4[0:1, :], ones128_bf[:], sq_s[:],
                                                      start=True, stop=True))
                    W(tensor, "s_sbf", nqb + 1)
                    S_["s_bv"].inc(tensor.matmul(p2[:], bv_bf[0:1, h * 128:(h + 1) * 128],
                                                 sums_bf[:], start=False, stop=True))
                    W(tensor, "s_rcp", nqb + 1)
                    W(tensor, "s_smv", nqb + 1)
                    S_["s_rb"].inc(tensor.matmul(p3[:], ones1_f[:], recip_s[:],
                                                 start=True, stop=True))
                    if qb + 1 < NTB:
                        emit_scores(qb + 1, 0)
                        emit_scores(qb + 1, 1)
                        emit_scores(qb + 1, 2)
                        c = qb + 1
                        W(tensor, "s_mng", NTB * h + c + 1)
                        W(tensor, "s_bbf", max(0, NTB * h + c - 1))
                        S_["s_bcm"].inc(tensor.matmul(p6[:, c % 2:c % 2 + 1], ones1_bf[:],
                                                      mng_bf[0:1, c % 2:c % 2 + 1],
                                                      start=True, stop=True))

        @block.gpsimd
        def _(gpsimd):
            for fi in range(NFC):
                gpsimd.wait_ge(S_["s_ev"].h, S_["s_ev"].base + 2 * NTT)
                S_["s_xout"].inc_dma(gpsimd.dma_start(out=xn_out[fi * 128:(fi + 1) * 128, :], in_=xn0[fi][:]))
            for h in range(NH):
                for qb in range(NTB):
                    nqb = NTB * h + qb
                    W(gpsimd, "s_nrm", nqb + 1)
                    S_[f"s_aout{nqb % 2}"].inc_dma(gpsimd.dma_start(
                        out=attn_out[h * 128:(h + 1) * 128, qb * 512:(qb + 1) * 512],
                        in_=att_bf[nqb % 2][:]))

        @block.sync
        def _(sync):
            sync.wait_ge(S_["s_aout0"].h, S_["s_aout0"].base + S_["s_aout0"].n)
            sync.wait_ge(S_["s_aout1"].h, S_["s_aout1"].base + S_["s_aout1"].n)
            sync.wait_ge(S_["s_xout"].h, S_["s_xout"].base + S_["s_xout"].n)

    assert S_["s_sc"].n == NH * NG and S_["s_pt"].n == NH * NG and S_["s_av"].n == NH * NG
    assert S_["s_qse"].n == 2 * NTB * NH and S_["s_deq"].n == 12 * NH
    return nc



D = 2048
NFC = 16
NI = 8192
NIC = 64


def emit_mid(nc, TW, stack, io=None, win=None):
    NTB5 = [(i * 512, min(512, TW - i * 512)) for i in range((TW + 511) // 512)]
    NTB9 = [(i * 256, min(256, TW - i * 256)) for i in range((TW + 255) // 256)]
    NT5, NT9 = len(NTB5), len(NTB9)
    NTT = TW // 128

    xT_in = dram_io(nc, io, "xT", [D, TW], F32, "ExternalInput")
    attn_in = None if win is not None else dram_io(nc, io, "attn", [D, TW], BF16, "ExternalInput")
    xn0_in = None if win is not None else dram_io(nc, io, "xn0", [D, TW], BF16, "ExternalInput")
    wo_c = dram_io(nc, io, "wo_c", [D, D], F32, "ExternalInput")
    wo_s = dram_io(nc, io, "wo_s", [64, D], F32, "ExternalInput")
    bo_r = dram_io(nc, io, "bo_r", [128, 16], F32, "ExternalInput")
    wg8 = dram_io(nc, io, "wg8", [D, NI], I8, "ExternalInput")
    wu8 = dram_io(nc, io, "wu8", [D, NI], I8, "ExternalInput")
    wd8 = dram_io(nc, io, "wd8", [NI, D], I8, "ExternalInput")
    sg_r = dram_io(nc, io, "sg_r", [128, 64], F32, "ExternalInput")
    su_r = dram_io(nc, io, "su_r", [128, 64], F32, "ExternalInput")
    sd_r = dram_io(nc, io, "sd_r", [128, 16], F32, "ExternalInput")
    x1_out = dram_io(nc, io, "x1", [D, TW], F32, "ExternalOutput")
    xn1_out = dram_io(nc, io, "xn1", [D, TW], BF16, "ExternalOutput")
    x1p = dram_io(nc, io, "x1p", [D, TW], F32, "Internal")
    h0 = dram_io(nc, io, "h0", [NI, TW], BF16, "Internal")

    e4 = np.zeros((4, 128), dtype=bf16)
    for r in range(4):
        e4[r, 32 * r:32 * (r + 1)] = 1
    e4_bf, c5 = inline_const(nc, "e4sel", e4)
    ident_bf, c4 = inline_const(nc, "ident", np.eye(128, dtype=bf16))
    ident_f32, c6 = inline_const(nc, "identf", np.eye(128, dtype=np.float32))

    def W(S_, eng, sem, v):
        if v > 0:
            eng.wait_ge(S_[sem].h, S_[sem].base + v)

    st1 = ExitStack()
    sb1 = lambda *a: st1.enter_context(nc.sbuf_tensor(_u(a[0]), *a[1:]))
    ps1 = lambda *a: st1.enter_context(nc.psum_tensor(_u(a[0]), *a[1:]))
    attn_big = sb1("attn_big", [128, NFC * TW], BF16)
    attn_sb = lambda fc, t0, tl: attn_big[:, fc * TW + t0: fc * TW + t0 + tl]
    codes_sb = [sb1(f"codes{i}", [128, 512], F32) for i in range(2)]
    scales_sb = sb1("scales", [4, 4 * 128], F32)
    scales_bf = sb1("scales_bf", [4, 4 * 128], BF16)
    wo_bf = [sb1(f"wo_bf{i}", [128, 16 * 128], BF16) for i in range(2)]
    bo_sb = sb1("bo_sb", [128, 16], F32)
    e4_sb = sb1("e4_sb", [4, 128], BF16)
    xT_t = [sb1(f"xT_t{i}", [128, 512], F32) for i in range(2)]
    y0_s = sb1("y0_s", [128, 512], F32)
    x1_t = [sb1(f"x1_t{i}", [128, 512], F32) for i in range(2)]
    pd = ps1("pd", [128, 512], F32)
    pm = [ps1(f"pm{i}", [128, 512], F32) for i in range(2)]
    S1 = mk_sems(nc, stack, ["m_cw", "m_at", "m_ld0", "m_ld1", "m_scv", "m_sb", "m_deq",
                             "m_xt0", "m_xt1", "m_mm", "m_add", "m_out0", "m_out1", "m_v"])

    with nc.Block() as block:
        @block.sync
        def _(sync):
            S1["m_cw"].inc_dma(load_const(sync, e4_sb[:], e4_bf, c5))
            S1["m_cw"].inc_dma(sync.dma_start(out=bo_sb[:], in_=bo_r))
            if win is None:
                for fc in range(NFC):
                    S1["m_at"].inc_dma(sync.dma_start(out=attn_sb(fc, 0, TW),
                                                      in_=attn_in[fc * 128:(fc + 1) * 128, :]))
            else:
                t0v = sync.value_load(in_tensor=win["offs"][0:1, 0:1])
                ag = win["attn_full"]
                for fc in range(NFC):
                    S1["m_at"].inc_dma(sync.dma_start(out=attn_sb(fc, 0, 2048),
                                                      in_=ag[fc * 128:(fc + 1) * 128, bass.ds(t0v, 2048)]))
                    S1["m_at"].inc_dma(sync.dma_start(out=attn_sb(fc, 2048, TW - 2048),
                                                      in_=ag[fc * 128:(fc + 1) * 128, 1920:2048]))
            for oc in range(16):
                for g in range(4):
                    gi = 4 * oc + g
                    W(S1, sync, "m_deq", gi - 1)
                    src = wo_c[512 * g:512 * (g + 1), 128 * oc:128 * (oc + 1)].rearrange("(a p) c -> p a c", p=128)
                    S1[f"m_ld{gi % 2}"].inc_dma(sync.dma_start(
                        out=codes_sb[gi % 2][:].rearrange("p (a c) -> p a c", a=4), in_=src))
                    W(S1, sync, "m_scv", gi)
                    ssrc = wo_s[16 * g:16 * (g + 1), 128 * oc:128 * (oc + 1)].rearrange("(a r) c -> r a c", r=4)
                    S1[f"m_ld{gi % 2}"].inc_dma(sync.dma_start(
                        out=scales_sb[:].rearrange("r (a c) -> r a c", a=4), in_=ssrc))

        @block.gpsimd
        def _(gpsimd):
            for oc in range(16):
                for it in range(NT5):
                    t0, tl = NTB5[it]
                    i = oc * NT5 + it
                    W(S1, gpsimd, "m_add", i - 1)
                    S1[f"m_xt{i % 2}"].inc_dma(gpsimd.dma_start(
                        out=xT_t[i % 2][:, :tl], in_=xT_in[oc * 128:(oc + 1) * 128, t0:t0 + tl]))

        @block.vector
        def _(vector):
            W(S1, vector, "m_cw", 32)
            for oc in range(16):
                for g in range(4):
                    gi = 4 * oc + g
                    W(S1, vector, f"m_ld{gi % 2}", 32 * (gi // 2 + 1))
                    W(S1, vector, "m_sb", gi)
                    S1["m_scv"].inc(vector.tensor_copy(out=scales_bf[:], in_=scales_sb[:]))
                    W(S1, vector, "m_sb", gi + 1)
                    if g == 0 and oc >= 2:
                        W(S1, vector, "m_mm", (oc - 1) * NT5)
                    S1["m_deq"].inc(vector.tensor_tensor(out=wo_bf[oc % 2][:, g * 512:(g + 1) * 512],
                                                         in0=codes_sb[gi % 2][:], in1=pd[:], op=ALU.mult))
                for it in range(NT5):
                    t0, tl = NTB5[it]
                    i = oc * NT5 + it
                    W(S1, vector, "m_mm", i + 1)
                    W(S1, vector, f"m_xt{i % 2}", 16 * (i // 2 + 1))
                    W(S1, vector, "m_add", i)
                    S1["m_v"].inc(vector.scalar_tensor_tensor(out=y0_s[:, :tl], in0=pm[i % 2][:, :tl],
                                                              scalar=1.0, in1=xT_t[i % 2][:, :tl],
                                                              op0=ALU.mult, op1=ALU.add))
                    W(S1, vector, "m_v", S1["m_v"].n)
                    W(S1, vector, f"m_out{i % 2}", 16 * (i // 2))
                    S1["m_add"].inc(vector.tensor_scalar_add(out=x1_t[i % 2][:, :tl], in0=y0_s[:, :tl],
                                                             scalar1=bo_sb[:, oc:oc + 1]))

        @block.tensor
        def _(tensor):
            W(S1, tensor, "m_cw", 32)
            for oc in range(16):
                for g in range(4):
                    gi = 4 * oc + g
                    W(S1, tensor, "m_scv", gi + 1)
                    W(S1, tensor, "m_deq", gi)
                    for j in range(4):
                        ins = tensor.matmul(pd[:, j * 128:(j + 1) * 128], e4_sb[:],
                                            scales_bf[:, j * 128:(j + 1) * 128], start=True, stop=True)
                    S1["m_sb"].inc(ins)
                W(S1, tensor, "m_deq", 4 * (oc + 1))
                if oc == 0:
                    W(S1, tensor, "m_at", (32 if win is not None else 16) * NFC)
                for it in range(NT5):
                    t0, tl = NTB5[it]
                    i = oc * NT5 + it
                    W(S1, tensor, "m_v", i - 1)
                    for fcx in range(NFC):
                        ins = tensor.matmul(pm[i % 2][:, :tl], wo_bf[oc % 2][:, fcx * 128:(fcx + 1) * 128],
                                            attn_sb(fcx, t0, tl),
                                            start=(fcx == 0), stop=(fcx == NFC - 1))
                    S1["m_mm"].inc(ins)

        @block.scalar
        def _(scalar):
            for oc in range(16):
                for it in range(NT5):
                    t0, tl = NTB5[it]
                    i = oc * NT5 + it
                    W(S1, scalar, "m_add", i + 1)
                    S1[f"m_out{i % 2}"].inc_dma(scalar.dma_start(
                        out=x1p[oc * 128:(oc + 1) * 128, t0:t0 + tl], in_=x1_t[i % 2][:, :tl]))

        @block.sync
        def _(sync):
            sync.wait_ge(S1["m_out0"].h, S1["m_out0"].base + S1["m_out0"].n)
            sync.wait_ge(S1["m_out1"].h, S1["m_out1"].base + S1["m_out1"].n)
    st1.close()

    st2 = ExitStack()
    sb2 = lambda *a: st2.enter_context(nc.sbuf_tensor(_u(a[0]), *a[1:]))
    ps2 = lambda *a: st2.enter_context(nc.psum_tensor(_u(a[0]), *a[1:]))
    xn0_big = sb2("xn0_big", [128, NFC * TW], BF16)
    xn0_sb = lambda fc, t0, tl: xn0_big[:, fc * TW + t0: fc * TW + t0 + tl]
    wg_i8 = [sb2(f"wg_i8{i}", [128, 16 * 512], I8) for i in range(2)]
    wu_i8 = [sb2(f"wu_i8{i}", [128, 16 * 512], I8) for i in range(2)]
    wg_bf = [sb2(f"wg_bf{i}", [128, 16 * 512], BF16) for i in range(2)]
    wu_bf = [sb2(f"wu_bf{i}", [128, 16 * 512], BF16) for i in range(2)]
    sg_sb = sb2("sg_sb", [128, 64], F32)
    su_sb = sb2("su_sb", [128, 64], F32)
    g_s = sb2("g_s", [128, 512], F32)
    u_s = sb2("u_s", [128, 512], F32)
    h_t = [sb2(f"h_t{i}", [128, 512], BF16) for i in range(2)]
    pg = [ps2(f"pg{i}", [128, 512], F32) for i in range(2)]
    pu = [ps2(f"pu{i}", [128, 512], F32) for i in range(2)]
    S2 = mk_sems(nc, stack, ["f_cw", "f_xn", "f_w0", "f_w1", "f_cv", "f_gmm", "f_umm",
                             "f_ge", "f_ue", "f_h", "f_o0", "f_o1"])

    with nc.Block() as block:
        @block.sync
        def _(sync):
            S2["f_cw"].inc_dma(sync.dma_start(out=sg_sb[:], in_=sg_r))
            S2["f_cw"].inc_dma(sync.dma_start(out=su_sb[:], in_=su_r))
            if win is None:
                for fc in range(NFC):
                    S2["f_xn"].inc_dma(sync.dma_start(out=xn0_sb(fc, 0, TW),
                                                      in_=xn0_in[fc * 128:(fc + 1) * 128, :]))
            for ig in range(16):
                W(S2, sync, "f_cv", 2 * (ig - 1))
                src = wg8[:, 512 * ig:512 * (ig + 1)].rearrange("(a p) c -> p a c", p=128)
                S2[f"f_w{ig % 2}"].inc_dma(sync.dma_start(
                    out=wg_i8[ig % 2][:].rearrange("p (a c) -> p a c", a=16), in_=src))
                src = wu8[:, 512 * ig:512 * (ig + 1)].rearrange("(a p) c -> p a c", p=128)
                S2[f"f_w{ig % 2}"].inc_dma(sync.dma_start(
                    out=wu_i8[ig % 2][:].rearrange("p (a c) -> p a c", a=16), in_=src))

        @block.vector
        def _(vector):
            W(S2, vector, "f_cw", 32)
            for ig in range(16):
                W(S2, vector, f"f_w{ig % 2}", 32 * (ig // 2 + 1))
                if ig >= 2:
                    W(S2, vector, "f_gmm", (4 * (ig - 1)) * NT5)
                    W(S2, vector, "f_umm", (4 * (ig - 1)) * NT5)
                S2["f_cv"].inc(vector.tensor_copy(out=wg_bf[ig % 2][:], in_=wg_i8[ig % 2][:]))
                S2["f_cv"].inc(vector.tensor_copy(out=wu_bf[ig % 2][:], in_=wu_i8[ig % 2][:]))
                for ii in range(4):
                    ic = 4 * ig + ii
                    for it in range(NT5):
                        t0, tl = NTB5[it]
                        i = ic * NT5 + it
                        W(S2, vector, "f_umm", i + 1)
                        W(S2, vector, "f_h", i)
                        S2["f_ue"].inc(vector.tensor_scalar(out=u_s[:, :tl], in0=pu[i % 2][:, :tl],
                                                            scalar1=su_sb[:, ic:ic + 1], scalar2=None,
                                                            op0=ALU.mult))
                        W(S2, vector, "f_ge", i + 1)
                        W(S2, vector, "f_ue", S2["f_ue"].n)
                        W(S2, vector, f"f_o{i % 2}", 16 * (i // 2))
                        S2["f_h"].inc(vector.tensor_tensor(out=h_t[i % 2][:, :tl], in0=g_s[:, :tl],
                                                           in1=u_s[:, :tl], op=ALU.mult))

        @block.scalar
        def _(scalar):
            if win is not None:
                t0v2 = scalar.value_load(in_tensor=win["offs"][0:1, 0:1])
                xg = win["xn_full"]
                for fc in range(NFC):
                    S2["f_xn"].inc_dma(scalar.dma_start(out=xn0_sb(fc, 0, 2048),
                                                        in_=xg[fc * 128:(fc + 1) * 128, bass.ds(t0v2, 2048)]))
                    S2["f_xn"].inc_dma(scalar.dma_start(out=xn0_sb(fc, 2048, TW - 2048),
                                                        in_=xg[fc * 128:(fc + 1) * 128, 1920:2048]))
            W(S2, scalar, "f_cw", 32)
            for ic in range(NIC):
                for it in range(NT5):
                    t0, tl = NTB5[it]
                    i = ic * NT5 + it
                    W(S2, scalar, "f_gmm", i + 1)
                    W(S2, scalar, "f_h", i)
                    S2["f_ge"].inc(scalar.activation(out=g_s[:, :tl], in_=pg[i % 2][:, :tl],
                                                     func=AF.Silu, scale=sg_sb[:, ic:ic + 1]))

        @block.tensor
        def _(tensor):
            W(S2, tensor, "f_cw", 32)
            W(S2, tensor, "f_xn", (32 if win is not None else 16) * NFC)
            for ic in range(NIC):
                ig, ii = ic // 4, ic % 4
                W(S2, tensor, "f_cv", 2 * (ig + 1))
                for it in range(NT5):
                    t0, tl = NTB5[it]
                    i = ic * NT5 + it
                    W(S2, tensor, "f_ge", i - 1)
                    for fcx in range(NFC):
                        ins = tensor.matmul(pg[i % 2][:, :tl],
                                            wg_bf[ig % 2][:, fcx * 512 + ii * 128: fcx * 512 + (ii + 1) * 128],
                                            xn0_sb(fcx, t0, tl),
                                            start=(fcx == 0), stop=(fcx == NFC - 1))
                    S2["f_gmm"].inc(ins)
                    W(S2, tensor, "f_ue", i - 1)
                    for fcx in range(NFC):
                        ins = tensor.matmul(pu[i % 2][:, :tl],
                                            wu_bf[ig % 2][:, fcx * 512 + ii * 128: fcx * 512 + (ii + 1) * 128],
                                            xn0_sb(fcx, t0, tl),
                                            start=(fcx == 0), stop=(fcx == NFC - 1))
                    S2["f_umm"].inc(ins)

        @block.gpsimd
        def _(gpsimd):
            for ic in range(NIC):
                for it in range(NT5):
                    t0, tl = NTB5[it]
                    i = ic * NT5 + it
                    W(S2, gpsimd, "f_h", i + 1)
                    S2[f"f_o{i % 2}"].inc_dma(gpsimd.dma_start(
                        out=h0[ic * 128:(ic + 1) * 128, t0:t0 + tl], in_=h_t[i % 2][:, :tl]))

        @block.sync
        def _(sync):
            sync.wait_ge(S2["f_o0"].h, S2["f_o0"].base + S2["f_o0"].n)
            sync.wait_ge(S2["f_o1"].h, S2["f_o1"].base + S2["f_o1"].n)
    st2.close()

    st3 = ExitStack()
    sb3 = lambda *a: st3.enter_context(nc.sbuf_tensor(_u(a[0]), *a[1:]))
    ps3 = lambda *a: st3.enter_context(nc.psum_tensor(_u(a[0]), *a[1:]))
    wd_i8 = sb3("wd_i8", [128, 64 * 512], I8)
    wd_bf = sb3("wd_bf", [128, 64 * 512], BF16)
    h_sb = [sb3(f"h_sb{i}", [128, 64 * 256], BF16) for i in range(2)]
    sd_sb = sb3("sd_sb", [128, 16], F32)
    x1p_t = [sb3(f"x1p_t{i}", [128, 256], F32) for i in range(2)]
    y_s = sb3("y_s", [128, 256], F32)
    x1f = [sb3(f"x1f{i}", [128, 256], F32) for i in range(2)]
    pb8 = [ps3(f"pb8_{i}", [128, 256], F32) for i in range(8)]
    S3 = mk_sems(nc, stack, ["d_cw", "d_w", "d_cv", "d_h0", "d_h1", "d_mm", "d_xp0", "d_xp1",
                             "d_add", "d_o0", "d_o1", "d_v"])
    hv = [h_sb[i][:].rearrange("p (a c) -> p a c", a=64) for i in range(2)]

    with nc.Block() as block:
        @block.sync
        def _(sync):
            S3["d_cw"].inc_dma(sync.dma_start(out=sd_sb[:], in_=sd_r))
            for og in range(4):
                W(S3, sync, "d_cv", og)
                src = wd8[:, 512 * og:512 * (og + 1)].rearrange("(a p) c -> p a c", p=128)
                S3["d_w"].inc_dma(sync.dma_start(
                    out=wd_i8[:].rearrange("p (a c) -> p a c", a=64), in_=src))
                for it in range(NT9):
                    t0, tl = NTB9[it]
                    i5 = og * NT9 + it
                    W(S3, sync, "d_mm", 4 * (i5 - 1))
                    src = h0[:, t0:t0 + tl].rearrange("(a p) c -> p a c", p=128)
                    S3[f"d_h{i5 % 2}"].inc_dma(sync.dma_start(out=hv[i5 % 2][:, :, :tl], in_=src))

        @block.vector
        def _(vector):
            W(S3, vector, "d_cw", 16)
            for og in range(4):
                W(S3, vector, "d_w", 16 * (og + 1))
                if og >= 1:
                    W(S3, vector, "d_mm", 4 * og * NT9)
                S3["d_cv"].inc(vector.tensor_copy(out=wd_bf[:], in_=wd_i8[:]))
                for it in range(NT9):
                    t0, tl = NTB9[it]
                    for j in range(4):
                        oc = 4 * og + j
                        i = (og * NT9 + it) * 4 + j
                        W(S3, vector, "d_mm", i + 1)
                        W(S3, vector, "d_add", i)
                        S3["d_v"].inc(vector.tensor_scalar(out=y_s[:, :tl], in0=pb8[(it % 2) * 4 + j][:, :tl],
                                                           scalar1=sd_sb[:, oc:oc + 1], scalar2=None,
                                                           op0=ALU.mult))
                        W(S3, vector, "d_v", S3["d_v"].n)
                        W(S3, vector, f"d_xp{i % 2}", 16 * (i // 2 + 1))
                        W(S3, vector, f"d_o{i % 2}", 16 * (i // 2))
                        S3["d_add"].inc(vector.tensor_tensor(out=x1f[i % 2][:, :tl], in0=y_s[:, :tl],
                                                             in1=x1p_t[i % 2][:, :tl], op=ALU.add))

        @block.gpsimd
        def _(gpsimd):
            for og in range(4):
                for it in range(NT9):
                    t0, tl = NTB9[it]
                    for j in range(4):
                        oc = 4 * og + j
                        i = (og * NT9 + it) * 4 + j
                        W(S3, gpsimd, "d_add", i - 1)
                        S3[f"d_xp{i % 2}"].inc_dma(gpsimd.dma_start(
                            out=x1p_t[i % 2][:, :tl], in_=x1p[oc * 128:(oc + 1) * 128, t0:t0 + tl]))

        @block.scalar
        def _(scalar):
            for og in range(4):
                for it in range(NT9):
                    t0, tl = NTB9[it]
                    for j in range(4):
                        oc = 4 * og + j
                        i = (og * NT9 + it) * 4 + j
                        W(S3, scalar, "d_add", i + 1)
                        S3[f"d_o{i % 2}"].inc_dma(scalar.dma_start(
                            out=x1_out[oc * 128:(oc + 1) * 128, t0:t0 + tl], in_=x1f[i % 2][:, :tl]))

        @block.tensor
        def _(tensor):
            for og in range(4):
                W(S3, tensor, "d_cv", og + 1)
                for it in range(NT9):
                    t0, tl = NTB9[it]
                    i5 = og * NT9 + it
                    W(S3, tensor, f"d_h{i5 % 2}", 16 * (i5 // 2 + 1))
                    if i5 >= 2:
                        W(S3, tensor, "d_v", 4 * (i5 - 1))
                    for ic in range(64):
                        for j in range(4):
                            bank = (it % 2) * 4 + j
                            ins = tensor.matmul(pb8[bank][:, :tl],
                                                wd_bf[:, ic * 512 + j * 128:ic * 512 + (j + 1) * 128],
                                                hv[i5 % 2][:, ic, :tl],
                                                start=(ic == 0), stop=(ic == 63))
                            if ic == 63:
                                S3["d_mm"].inc(ins)

        @block.sync
        def _(sync):
            sync.wait_ge(S3["d_o0"].h, S3["d_o0"].base + S3["d_o0"].n)
            sync.wait_ge(S3["d_o1"].h, S3["d_o1"].base + S3["d_o1"].n)
    st3.close()

    st4 = ExitStack()
    sb4 = lambda *a: st4.enter_context(nc.sbuf_tensor(_u(a[0]), *a[1:]))
    ps4 = lambda *a: st4.enter_context(nc.psum_tensor(_u(a[0]), *a[1:]))
    x1s = [sb4(f"x1s{i}", [128, 16 * 128], F32) for i in range(2)]
    xtm = sb4("xtm", [128, D], F32)
    xnt = sb4("xnt", [128, D], BF16)
    xnf = [sb4(f"xnf{i}", [128, D], BF16) for i in range(2)]
    stat = sb4("stat", [128, 8], F32)
    sqs = sb4("sqs", [128, D], BF16)
    ident_sb = sb4("ident_s", [128, 128], BF16)
    identf_sb = sb4("identf_s", [128, 128], F32)
    fc4 = sb4("fc4", [128, 2], F32)
    pF = ps4("pF", [128, 512], F32)
    pB = ps4("pB", [128, 1024], BF16)
    S4 = mk_sems(nc, stack, ["l_cw", "l_x0", "l_x1", "l_tr", "l_te", "l_sq", "l_rs",
                             "l_st", "l_rd", "l_ap", "l_tb", "l_tbe", "l_oo0", "l_oo1",
                             "l_v", "l_a", "l_fc"])

    with nc.Block() as block:
        @block.sync
        def _(sync):
            S4["l_cw"].inc_dma(load_const(sync, ident_sb[:], ident_bf, c4))
            S4["l_cw"].inc_dma(load_const(sync, identf_sb[:], ident_f32, c6))
            for tt in range(NTT):
                W(S4, sync, "l_tr", 4 * (tt - 1))
                src = x1_out[:, tt * 128:(tt + 1) * 128].rearrange("(a p) c -> p a c", p=128)
                S4[f"l_x{tt % 2}"].inc_dma(sync.dma_start(
                    out=x1s[tt % 2][:].rearrange("p (a c) -> p a c", a=16), in_=src))

        @block.tensor
        def _(tensor):
            W(S4, tensor, "l_cw", 32)
            for tt in range(NTT):
                W(S4, tensor, f"l_x{tt % 2}", 16 * (tt // 2 + 1))
                for b in range(4):
                    W(S4, tensor, "l_te", 4 * tt + b)
                    for j in range(4):
                        fcx = 4 * b + j
                        ins = tensor.transpose(pF[:, j * 128:(j + 1) * 128],
                                               x1s[tt % 2][:, fcx * 128:(fcx + 1) * 128], identf_sb[:])
                    S4["l_tr"].inc(ins)
                W(S4, tensor, "l_ap", tt + 1)
                for half in range(2):
                    W(S4, tensor, "l_tbe", 2 * tt + half)
                    for j in range(8):
                        fcx = half * 8 + j
                        ins = tensor.transpose(pB[:, j * 128:(j + 1) * 128],
                                               xnt[:, fcx * 128:(fcx + 1) * 128], ident_sb[:])
                    S4["l_tb"].inc(ins)

        @block.vector
        def _(vector):
            vector.memset(fc4[:, 0:1], 1e-5)
            S4["l_fc"].inc(vector.memset(fc4[:, 1:2], -0.5))
            for tt in range(NTT):
                for b in range(4):
                    W(S4, vector, "l_tr", 4 * tt + b + 1)
                    S4["l_te"].inc(vector.tensor_copy(out=xtm[:, b * 512:(b + 1) * 512], in_=pF[:]))
                W(S4, vector, "l_te", 4 * (tt + 1))
                W(S4, vector, "l_st", 4 * tt)
                S4["l_rs"].inc(vector.tensor_reduce(out=stat[:, 0:1], in_=xtm[:], axis=AX.X, op=ALU.add))
                W(S4, vector, "l_rs", S4["l_rs"].n)
                S4["l_v"].inc(vector.tensor_scalar_mul(out=stat[:, 2:3], in0=stat[:, 0:1], scalar1=1.0 / D))
                W(S4, vector, "l_v", S4["l_v"].n)
                S4["l_v"].inc(vector.tensor_tensor(out=stat[:, 4:5], in0=stat[:, 2:3], in1=stat[:, 2:3], op=ALU.mult))
                W(S4, vector, "l_v", S4["l_v"].n)
                W(S4, vector, "l_sq", tt + 1)
                S4["l_st"].inc(vector.scalar_tensor_tensor(out=stat[:, 5:6], in0=stat[:, 1:2], scalar=1.0 / D,
                                                           in1=stat[:, 4:5], op0=ALU.mult, op1=ALU.subtract), 4)
                W(S4, vector, "l_rd", 2 * (tt + 1))
                W(S4, vector, "l_tb", 2 * tt)
                S4["l_ap"].inc(vector.tensor_scalar(out=xnt[:], in0=xtm[:],
                                                    scalar1=stat[:, 2:3], scalar2=stat[:, 7:8],
                                                    op0=ALU.subtract, op1=ALU.mult))
                for half in range(2):
                    W(S4, vector, "l_tb", 2 * tt + half + 1)
                    W(S4, vector, f"l_oo{tt % 2}", 16 * 16 * (tt // 2))
                    S4["l_tbe"].inc(vector.tensor_copy(out=xnf[tt % 2][:, half * 1024:(half + 1) * 1024], in_=pB[:]))

        @block.scalar
        def _(scalar):
            for tt in range(NTT):
                W(S4, scalar, "l_te", 4 * (tt + 1))
                W(S4, scalar, "l_st", 4 * tt)
                W(S4, scalar, "l_ap", tt)
                if tt == 0:
                    W(S4, scalar, "l_fc", 1)
                S4["l_sq"].inc(scalar.activation(out=sqs[:], in_=xtm[:], func=AF.Square,
                                                 accum_out=stat[:, 1:2]))
                W(S4, scalar, "l_st", 4 * (tt + 1))
                S4["l_a"].inc(scalar.activation(out=stat[:, 6:7], in_=stat[:, 5:6], func=AF.Ln, bias=fc4[:, 0:1]))
                W(S4, scalar, "l_a", S4["l_a"].n)
                S4["l_rd"].inc(scalar.activation(out=stat[:, 7:8], in_=stat[:, 6:7], func=AF.Exp, scale=fc4[:, 1:2]), 2)

        @block.gpsimd
        def _(gpsimd):
            for tt in range(NTT):
                for fcx in range(NFC):
                    W(S4, gpsimd, "l_tbe", 2 * tt + fcx // 8 + 1)
                    S4[f"l_oo{tt % 2}"].inc_dma(gpsimd.dma_start(
                        out=xn1_out[fcx * 128:(fcx + 1) * 128, tt * 128:(tt + 1) * 128],
                        in_=xnf[tt % 2][:, fcx * 128:(fcx + 1) * 128]))

        @block.sync
        def _(sync):
            sync.wait_ge(S4["l_oo0"].h, S4["l_oo0"].base + S4["l_oo0"].n)
            sync.wait_ge(S4["l_oo1"].h, S4["l_oo1"].base + S4["l_oo1"].n)
    st4.close()
    return nc


D = 2048
NFC = 16
CONVD = 4224
NCC = 33
DIN = 4096


def emit_b1(nc, TW, stack, io=None):
    TOWN = TW - 128
    NTB5 = [(i * 512, min(512, TW - i * 512)) for i in range((TW + 511) // 512)]
    NT5 = len(NTB5)
    NOB = TOWN // 512
    NOT = TOWN // 128

    xn1_in = dram_io(nc, io, "xn1", [D, TW], BF16, "ExternalInput")
    winT = dram_io(nc, io, "winT", [D, 8352], F32, "ExternalInput")
    convw = dram_io(nc, io, "convw", [128, NCC * 4], F32, "ExternalInput")
    convb = dram_io(nc, io, "convb", [128, NCC], F32, "ExternalInput")
    padmask = dram_io(nc, io, "padmask", [128, 1], F32, "ExternalInput")
    dtb = dram_io(nc, io, "dtb", [32, 1], F32, "ExternalInput")
    xcT_out = dram_io(nc, io, "xcT", [TOWN, CONVD], BF16, "ExternalOutput")
    z_out = dram_io(nc, io, "z_tm", [TOWN, DIN], BF16, "ExternalOutput")
    dt_out = dram_io(nc, io, "dt_f", [32, TW], F32, "ExternalOutput")

    ident_bf, c4 = inline_const(nc, "ident", np.eye(128, dtype=bf16))

    def W(S_, eng, sem, v):
        if v > 0:
            eng.wait_ge(S_[sem].h, S_[sem].base + v)

    xn1_sb = stack.enter_context(nc.sbuf_tensor(_u("xn1_sb"), [128, NFC * TW], BF16))
    xn1v = xn1_sb[:].rearrange("p (a c) -> p a c", a=NFC)

    S0 = mk_sems(nc, stack, ["z_xn"])
    with nc.Block() as block:
        @block.sync
        def _(sync):
            for fcx in range(NFC):
                S0["z_xn"].inc_dma(sync.dma_start(out=xn1v[:, fcx, :], in_=xn1_in[fcx * 128:(fcx + 1) * 128, :]))
            sync.wait_ge(S0["z_xn"].h, S0["z_xn"].base + 16 * NFC)

    st1 = ExitStack()
    sb1 = lambda *a: st1.enter_context(nc.sbuf_tensor(_u(a[0]), *a[1:]))
    ps1 = lambda *a: st1.enter_context(nc.psum_tensor(_u(a[0]), *a[1:]))
    w_f32 = [sb1(f"w_f32{i}", [128, NFC * 128], F32) for i in range(2)]
    w_bf = [sb1(f"w_bf{i}", [128, NFC * 128], BF16) for i in range(2)]
    cw_sb = sb1("cw_sb", [128, NCC * 4], F32)
    cb_sb = sb1("cb_sb", [128, NCC], F32)
    pmask = sb1("pmask", [128, 1], F32)
    dtb_sb = sb1("dtb_sb", [32, 1], F32)
    ident_sb = sb1("ident_s", [128, 128], BF16)
    xbc_w = sb1("xbc_w", [128, TW], BF16)
    convin = sb1("convin", [128, 3 + TOWN], BF16)
    dg = sb1("dg", [128, 4 * 128], BF16)
    xc_f = sb1("xc_f", [128, 512], F32)
    xc_s = sb1("xc_s", [128, 512], BF16)
    xct_s = [sb1(f"xct_s{i}", [128, 512], BF16) for i in range(2)]
    dtraw = sb1("dtraw", [32, TW], F32)
    dt_a = sb1("dt_a", [32, TW], F32)
    dt_r = sb1("dt_r", [32, TW], F32)
    dt_o = sb1("dt_o", [32, TW], F32)
    fc1 = sb1("fc1", [128, 2], F32)
    pp = [ps1(f"pp{i}", [128, 512], F32) for i in range(2)]
    pc = [ps1(f"pc{i}", [128, 512], F32) for i in range(2)]
    pt = ps1("pt", [128, 512], BF16)
    cwv = cw_sb[:].rearrange("p (a c) -> p a c", a=NCC)
    S1 = mk_sems(nc, stack, ["b_cw", "b_w0", "b_w1", "b_cv", "b_mm", "b_pe", "b_ci",
                             "b_dgv", "b_cmm", "b_cb", "b_ce", "b_tr", "b_te",
                             "b_o0", "b_o1", "b_fc", "b_v", "b_a", "b_dt"])

    with nc.Block() as block:
        @block.sync
        def _(sync):
            S1["b_cw"].inc_dma(load_const(sync, ident_sb[:], ident_bf, c4))
            S1["b_cw"].inc_dma(sync.dma_start(out=cw_sb[:], in_=convw))
            S1["b_cw"].inc_dma(sync.dma_start(out=cb_sb[:], in_=convb))
            S1["b_cw"].inc_dma(sync.dma_start(out=pmask[:], in_=padmask))
            S1["b_cw"].inc_dma(sync.dma_start(out=dtb_sb[:], in_=dtb))
            for jc in range(NCC + 1):
                W(S1, sync, "b_cv", jc - 1)
                if jc < NCC:
                    src = winT[:, DIN + 128 * jc: DIN + 128 * (jc + 1)].rearrange("(a p) c -> p a c", p=128)
                    S1[f"b_w{jc % 2}"].inc_dma(sync.dma_start(
                        out=w_f32[jc % 2][:].rearrange("p (a c) -> p a c", a=NFC), in_=src))
                else:
                    src = winT[:, 8320:8352].rearrange("(a p) c -> p a c", p=128)
                    S1[f"b_w{jc % 2}"].inc_dma(sync.dma_start(
                        out=w_f32[jc % 2][:].rearrange("p (a c) -> p a c", a=NFC)[:, :, 0:32], in_=src))

        @block.vector
        def _(vector):
            vector.memset(fc1[:, 0:1], -1.0)
            S1["b_fc"].inc(vector.memset(fc1[:, 1:2], 1.0))
            W(S1, vector, "b_cw", 16 * 5)
            def emit_conv_ops(jc):
                for tap in range(4):
                    if jc >= 1 and tap == 0:
                        W(S1, vector, "b_cmm", NOB * jc)
                    S1["b_dgv"].inc(vector.tensor_scalar(out=dg[:, tap * 128:(tap + 1) * 128],
                                                         in0=ident_sb[:], scalar1=cwv[:, jc, tap:tap + 1],
                                                         scalar2=None, op0=ALU.mult))
                W(S1, vector, "b_ce", NT5 * (jc + 1))
                if jc >= 1:
                    W(S1, vector, "b_cmm", NOB * jc)
                vector.tensor_copy(out=convin[:, 3:], in_=xbc_w[:, 0:TOWN])
                S1["b_ci"].inc(vector.tensor_scalar(out=convin[:, 0:3], in0=xbc_w[:, TOWN + 125:TOWN + 128],
                                                    scalar1=pmask[:, 0:1], scalar2=None, op0=ALU.mult))
                for ob in range(NOB):
                    i = jc * NOB + ob
                    W(S1, vector, "b_cmm", i + 1)
                    W(S1, vector, "b_a", i)
                    S1["b_cb"].inc(vector.tensor_scalar_add(out=xc_f[:], in0=pc[ob % 2][:],
                                                            scalar1=cb_sb[:, jc:jc + 1]))
                    W(S1, vector, "b_tr", i + 1)
                    W(S1, vector, f"b_o{i % 2}", 16 * 4 * (i // 2))
                    S1["b_te"].inc(vector.tensor_copy(out=xct_s[i % 2][:], in_=pt[:]))
            for jc in range(NCC + 2):
                if jc < NCC + 1:
                    W(S1, vector, f"b_w{jc % 2}", 16 * (jc // 2 + 1))
                    if jc >= 2:
                        W(S1, vector, "b_mm", NT5 * (jc - 1))
                    S1["b_cv"].inc(vector.tensor_copy(out=w_bf[jc % 2][:], in_=w_f32[jc % 2][:]))
                if 1 <= jc <= NCC:
                    emit_conv_ops(jc - 1)
            W(S1, vector, "b_ce", NT5 * (NCC + 1))
            S1["b_v"].inc(vector.tensor_scalar_add(out=dt_o[:], in0=dtraw[:], scalar1=dtb_sb[:, 0:1]))
            W(S1, vector, "b_v", S1["b_v"].n)
            W(S1, vector, "b_a", NOB * NCC + 3)
            S1["b_v"].inc(vector.tensor_scalar_max(out=dt_r[:], in0=dt_o[:], scalar1=0.0))
            W(S1, vector, "b_v", S1["b_v"].n)
            S1["b_dt"].inc(vector.tensor_tensor(out=dt_o[:], in0=dt_r[:], in1=dt_a[:], op=ALU.add), 1)

        @block.scalar
        def _(scalar):
            for jc in range(NCC):
                for it in range(NT5):
                    t0, tl = NTB5[it]
                    i = jc * NT5 + it
                    W(S1, scalar, "b_mm", i + 1)
                    if it == 0 and jc >= 1:
                        W(S1, scalar, "b_ci", jc)
                    S1["b_ce"].inc(scalar.activation(out=xbc_w[:, t0:t0 + tl], in_=pp[i % 2][:, :tl],
                                                     func=AF.Copy))
                for ob in range(NOB):
                    i = jc * NOB + ob
                    W(S1, scalar, "b_cb", i + 1)
                    W(S1, scalar, "b_tr", i)
                    S1["b_a"].inc(scalar.activation(out=xc_s[:], in_=xc_f[:], func=AF.Silu))
            for it in range(NT5):
                t0, tl = NTB5[it]
                i = NCC * NT5 + it
                W(S1, scalar, "b_mm", NCC * NT5 + it + 1)
                S1["b_ce"].inc(scalar.activation(out=dtraw[0:32, t0:t0 + tl],
                                                 in_=pp[i % 2][0:32, :tl], func=AF.Copy))
            W(S1, scalar, "b_v", 1)
            S1["b_a"].inc(scalar.activation(out=dt_a[:], in_=dt_o[:], func=AF.Abs))
            W(S1, scalar, "b_a", S1["b_a"].n)
            W(S1, scalar, "b_fc", 1)
            S1["b_a"].inc(scalar.activation(out=dt_a[:], in_=dt_a[:], func=AF.Exp, scale=fc1[0:32, 0:1]))
            W(S1, scalar, "b_a", S1["b_a"].n)
            S1["b_a"].inc(scalar.activation(out=dt_a[:], in_=dt_a[:], func=AF.Ln, bias=fc1[0:32, 1:2]))

        @block.tensor
        def _(tensor):
            W(S1, tensor, "b_cw", 16 * 5)
            for jc in range(NCC + 1):
                W(S1, tensor, "b_cv", jc + 1)
                nb = NT5 if jc < NCC else NT5
                for it in range(NT5):
                    t0, tl = NTB5[it]
                    i = jc * NT5 + it
                    W(S1, tensor, "b_ce", i - 1)
                    for fcx in range(NFC):
                        ins = tensor.matmul(pp[i % 2][:, :tl] if jc < NCC else pp[i % 2][0:32, :tl],
                                            w_bf[jc % 2][:, fcx * 128:(fcx + 1) * 128] if jc < NCC
                                            else w_bf[jc % 2][:].rearrange("p (a c) -> p a c", a=NFC)[:, fcx, 0:32],
                                            xn1v[:, fcx, t0:t0 + tl],
                                            start=(fcx == 0), stop=(fcx == NFC - 1))
                    S1["b_mm"].inc(ins)
                if jc >= NCC:
                    break
                W(S1, tensor, "b_dgv", 4 * (jc + 1))
                W(S1, tensor, "b_ci", jc + 1)
                for ob in range(NOB):
                    i = jc * NOB + ob
                    W(S1, tensor, "b_cb", i)
                    for tap in range(4):
                        ins = tensor.matmul(pc[ob % 2][:], dg[:, tap * 128:(tap + 1) * 128],
                                            convin[:, ob * 512 + tap: ob * 512 + tap + 512],
                                            start=(tap == 0), stop=(tap == 3))
                    S1["b_cmm"].inc(ins)
                    W(S1, tensor, "b_a", i + 1)
                    W(S1, tensor, "b_te", i)
                    for j in range(4):
                        ins = tensor.transpose(pt[:, j * 128:(j + 1) * 128],
                                               xc_s[:, j * 128:(j + 1) * 128], ident_sb[:])
                    S1["b_tr"].inc(ins)

        @block.gpsimd
        def _(gpsimd):
            for jc in range(NCC):
                for ob in range(NOB):
                    i = jc * NOB + ob
                    W(S1, gpsimd, "b_te", i + 1)
                    for j in range(4):
                        S1[f"b_o{i % 2}"].inc_dma(gpsimd.dma_start(
                            out=xcT_out[ob * 512 + j * 128: ob * 512 + (j + 1) * 128,
                                        jc * 128:(jc + 1) * 128],
                            in_=xct_s[i % 2][:, j * 128:(j + 1) * 128]))
            W(S1, gpsimd, "b_dt", 1)
            S1["b_o0"].inc_dma(gpsimd.dma_start(out=dt_out, in_=dt_o[:]))

        @block.sync
        def _(sync):
            sync.wait_ge(S1["b_o0"].h, S1["b_o0"].base + S1["b_o0"].n)
            sync.wait_ge(S1["b_o1"].h, S1["b_o1"].base + S1["b_o1"].n)
    st1.close()

    st2 = ExitStack()
    sb2 = lambda *a: st2.enter_context(nc.sbuf_tensor(_u(a[0]), *a[1:]))
    ps2 = lambda *a: st2.enter_context(nc.psum_tensor(_u(a[0]), *a[1:]))
    wz_f = [sb2(f"wz_f{i}", [128, NFC * 512], F32) for i in range(2)]
    wz_b = [sb2(f"wz_b{i}", [128, NFC * 512], BF16) for i in range(2)]
    z_s = [sb2(f"z_s{i}", [128, 512], BF16) for i in range(2)]
    pz = [ps2(f"pz{i}", [128, 512], F32) for i in range(2)]
    S2 = mk_sems(nc, stack, ["c_w0", "c_w1", "c_cv", "c_mm", "c_ev", "c_o0", "c_o1"])
    NOT_ = TOWN // 128

    with nc.Block() as block:
        @block.sync
        def _(sync):
            for zb in range(8):
                W(S2, sync, "c_cv", zb - 1)
                src = winT[:, zb * 512:(zb + 1) * 512].rearrange("(a p) c -> p a c", p=128)
                S2[f"c_w{zb % 2}"].inc_dma(sync.dma_start(
                    out=wz_f[zb % 2][:].rearrange("p (a c) -> p a c", a=NFC), in_=src))

        @block.vector
        def _(vector):
            for zb in range(8):
                W(S2, vector, f"c_w{zb % 2}", 16 * (zb // 2 + 1))
                if zb >= 2:
                    W(S2, vector, "c_mm", NOT_ * (zb - 1))
                S2["c_cv"].inc(vector.tensor_copy(out=wz_b[zb % 2][:], in_=wz_f[zb % 2][:]))

        @block.tensor
        def _(tensor):
            wzv = [wz_b[i][:].rearrange("p (a c) -> p a c", a=NFC) for i in range(2)]
            for zb in range(8):
                W(S2, tensor, "c_cv", zb + 1)
                for tt in range(NOT_):
                    i = zb * NOT_ + tt
                    W(S2, tensor, "c_ev", i - 1)
                    for fcx in range(NFC):
                        ins = tensor.matmul(pz[i % 2][:], xn1v[:, fcx, tt * 128:(tt + 1) * 128],
                                            wzv[zb % 2][:, fcx, :],
                                            start=(fcx == 0), stop=(fcx == NFC - 1))
                    S2["c_mm"].inc(ins)

        @block.scalar
        def _(scalar):
            for zb in range(8):
                for tt in range(NOT_):
                    i = zb * NOT_ + tt
                    W(S2, scalar, "c_mm", i + 1)
                    W(S2, scalar, f"c_o{i % 2}", 16 * (i // 2))
                    S2["c_ev"].inc(scalar.activation(out=z_s[i % 2][:], in_=pz[i % 2][:], func=AF.Copy))

        @block.gpsimd
        def _(gpsimd):
            for zb in range(8):
                for tt in range(NOT_):
                    i = zb * NOT_ + tt
                    W(S2, gpsimd, "c_ev", i + 1)
                    S2[f"c_o{i % 2}"].inc_dma(gpsimd.dma_start(
                        out=z_out[tt * 128:(tt + 1) * 128, zb * 512:(zb + 1) * 512],
                        in_=z_s[i % 2][:]))

        @block.sync
        def _(sync):
            sync.wait_ge(S2["c_o0"].h, S2["c_o0"].base + S2["c_o0"].n)
            sync.wait_ge(S2["c_o1"].h, S2["c_o1"].base + S2["c_o1"].n)
    st2.close()
    return nc



DIN = 4096
NST = 64
CONVD = 4224
NGX = 8
EPS_RMS = 1.1920929e-07


def sel_matrix(dtype):
    s = np.zeros((32, 4096), dtype=dtype)
    for h in range(32):
        s[h, h * 128:(h + 1) * 128] = 1
    return s


def emit_b2a(nc, TOWN, stack, io=None):
    NCH = TOWN // 128

    xcT = dram_io(nc, io, "xcT", [TOWN, CONVD], BF16, "ExternalInput")
    dt_in = dram_io(nc, io, "dt_f", [32, TOWN + 128], F32, "ExternalInput")
    alog = dram_io(nc, io, "A_log", [32, 1], F32, "ExternalInput")
    Dexp_in = dram_io(nc, io, "D_exp", [1, DIN], F32, "ExternalInput")
    Yp = dram_io(nc, io, "Yp", [TOWN, DIN], BF16, "ExternalOutput")
    carry_out = dram_io(nc, io, "carry", [NST, DIN], F32, "ExternalOutput")
    acs_out = dram_io(nc, io, "acs_all", [NCH * 32, 128], F32, "ExternalOutput")
    cums_out = dram_io(nc, io, "cums", [32, NCH], F32, "ExternalOutput")
    acs_scr = dram_io(nc, io, "acs_scr", [32, 128], F32, "Internal")

    selb, cs1 = inline_const(nc, "selb", sel_matrix(bf16))
    seln, cs2 = inline_const(nc, "seln", -sel_matrix(np.float32))
    ones1f, cs3 = inline_const(nc, "ones1f", np.ones((1, 128), dtype=np.float32))
    masku, cs6 = inline_const(nc, "masku", np.triu(np.ones((128, 128), dtype=np.float32)))
    ones3264, cs7 = inline_const(nc, "ones3264", np.ones((32, 64), dtype=bf16))
    identb, cs8 = inline_const(nc, "identb", np.eye(128, dtype=bf16))

    sb = lambda *a: stack.enter_context(nc.sbuf_tensor(_u(a[0]), *a[1:]))
    ps = lambda *a: stack.enter_context(nc.psum_tensor(_u(a[0]), *a[1:]))

    sel_sb = sb("sel_sb", [32, 4096], BF16)
    seln_sb = sb("seln_sb", [32, 4096], F32)
    ones1f_sb = sb("ones1f_sb", [1, 128], F32)
    masku_sb = sb("masku_sb", [128, 128], F32)
    ones3264_sb = sb("o3264_sb", [32, 64], BF16)
    ident_sb = sb("ident_sb", [128, 128], BF16)
    A_col = sb("A_col", [32, 1], F32)
    Dexp_sb = sb("Dexp_sb", [1, DIN], F32)
    D_full = sb("D_full", [128, 4096], BF16)
    xcs = [sb(f"xcs{i}", [128, CONVD], BF16) for i in range(2)]
    dtf = sb("dtf", [32, 128], F32)
    dtf_bf = sb("dtf_bf", [32, 128], BF16)
    dAf = sb("dAf", [32, 128], F32)
    acs_a = sb("acs_a", [32, 128], F32)
    acs_b = sb("acs_b", [32, 128], F32)
    dec_fm = sb("dec_fm", [32, 128], F32)
    dtdec_bf = sb("dtdec_bf", [32, 128], BF16)
    eA_fm = sb("eA_fm", [32, 128], BF16)
    dec_col = sb("dec_col", [32, 1], F32)
    dec_rep = sb("dec_rep", [32, 64], BF16)
    acc_col = sb("acc_col", [32, 1], F32)
    cums_sb = sb("cums_sb", [32, 16], F32)
    acs_rows = sb("acs_rows", [1, 4096], F32)
    B_fm = sb("B_fm", [64, 128], BF16)
    C_fm = sb("C_fm", [64, 128], BF16)
    sT = sb("sT", [128, 128], BF16)
    Lm = sb("Lm", [128, 4096], BF16)
    PTl = sb("PTl", [128, 4096], BF16)
    xd = sb("xd", [128, 4096], BF16)
    xdd = sb("xdd", [128, 4096], BF16)
    yo_s = sb("yo_s", [128, 512], F32)
    ygrp = sb("ygrp", [128, 512], F32)
    xdD = sb("xdD", [128, 512], BF16)
    y_all = [sb(f"y_all{i}", [128, 4096], BF16) for i in range(2)]
    carry = sb("carry_s", [64, 4096], F32)
    carry_bf = sb("carry_bf", [64, 4096], BF16)
    tdec = sb("tdec", [64, 512], F32)

    pS = ps("pS", [128, 512], F32)
    pL = ps("pL", [128, 512], F32)
    pX = ps("pX", [128, 512], F32)
    pYd = ps("pYd", [128, 512], F32)
    pYo = ps("pYo", [128, 512], F32)
    pSt = ps("pSt", [64, 512], F32)
    pDc = ps("pDc", [64, 512], F32)
    pB16 = ps("pB16", [128, 512], BF16)

    S_ = mk_sems(nc, stack, [
        "a_cw", "a_x0", "a_x1", "a_dt", "a_A", "a_Dfm", "a_Df", "a_dA", "a_acs",
        "a_decv", "a_deca", "a_dd", "a_eA", "a_drep", "a_cum", "a_acc", "a_bc0",
        "a_bc1", "a_tr", "a_bce", "a_scm", "a_stm", "a_lmm", "a_lme", "a_lmx",
        "a_pt", "a_xm", "a_xe", "a_ym", "a_eam", "a_yoe", "a_yos", "a_ygr",
        "a_xdd2", "a_ya", "a_sm", "a_dcm", "a_rc1", "a_rc2", "a_cbf",
        "a_o0", "a_o1", "a_oc", "a_dtbf"])

    def W(eng, sem, v):
        if v > 0:
            eng.wait_ge(S_[sem].h, S_[sem].base + v)

    with nc.Block() as block:
        @block.sync
        def _(sync):
            S_["a_cw"].inc_dma(load_const(sync, sel_sb[:], selb, cs1))
            S_["a_cw"].inc_dma(load_const(sync, seln_sb[:], seln, cs2))
            S_["a_cw"].inc_dma(load_const(sync, ones1f_sb[:], ones1f, cs3))
            S_["a_cw"].inc_dma(load_const(sync, masku_sb[:], masku, cs6))
            S_["a_cw"].inc_dma(load_const(sync, ones3264_sb[:], ones3264, cs7))
            S_["a_cw"].inc_dma(load_const(sync, ident_sb[:], identb, cs8))
            S_["a_cw"].inc_dma(sync.dma_start(out=A_col[:], in_=alog))
            S_["a_cw"].inc_dma(sync.dma_start(out=Dexp_sb[:], in_=Dexp_in))
            for c in range(NCH):
                W(sync, "a_ya", NGX * (c - 1))
                W(sync, "a_sm", NGX * (c - 1))
                W(sync, "a_tr", 2 * (c - 1))
                S_[f"a_x{c % 2}"].inc_dma(sync.dma_start(out=xcs[c % 2][:],
                                                         in_=xcT[c * 128:(c + 1) * 128, :]))
                W(sync, "a_dtbf", c)
                S_["a_dt"].inc_dma(sync.dma_start(out=dtf[:], in_=dt_in[:, c * 128:(c + 1) * 128]))

        @block.gpsimd
        def _(gpsimd):
            for c in range(NCH):
                W(gpsimd, "a_acs", 15 * c + 14)
                S_["a_bc0"].inc_dma(gpsimd.dma_start(out=acs_scr, in_=acs_a[:]))
                gpsimd.wait_ge(S_["a_bc0"].h, S_["a_bc0"].base + 16 * (c + 1))
                W(gpsimd, "a_lmm", 2 * NGX * c)
                S_["a_bc1"].inc_dma(gpsimd.dma_start(
                    out=acs_rows[0:1, :],
                    in_=acs_scr.rearrange("(o a) c -> o (a c)", o=1)))
                S_["a_oc"].inc_dma(gpsimd.dma_start(out=acs_out[32 * c:32 * (c + 1), :], in_=acs_a[:]))
                W(gpsimd, "a_ya", NGX * (c + 1))
                S_[f"a_o{c % 2}"].inc_dma(gpsimd.dma_start(out=Yp[c * 128:(c + 1) * 128, :],
                                                           in_=y_all[c % 2][:]))
            W(gpsimd, "a_rc2", NCH * NGX)
            S_["a_oc"].inc_dma(gpsimd.dma_start(out=carry_out, in_=carry[:]))
            W(gpsimd, "a_cum", NCH)
            S_["a_oc"].inc_dma(gpsimd.dma_start(out=cums_out, in_=cums_sb[:, 0:NCH]))

        @block.vector
        def _(vector):
            vector.memset(carry[:], 0.0)
            vector.memset(carry_bf[:], 0.0)
            S_["a_cbf"].inc(vector.memset(acc_col[:], 0.0))
            for sl in range(NGX):
                W(vector, "a_Dfm", sl + 1)
                S_["a_Df"].inc(vector.tensor_copy(out=D_full[:, sl * 512:(sl + 1) * 512], in_=pX[:]))
            for c in range(NCH):
                W(vector, "a_dt", 16 * (c + 1))
                W(vector, "a_A", 2)
                S_["a_dA"].inc(vector.tensor_scalar(out=dAf[:], in0=dtf[:], scalar1=A_col[:, 0:1],
                                                    scalar2=None, op0=ALU.mult))
                W(vector, "a_dA", S_["a_dA"].n)
                W(vector, "a_bc0", 16 * c)
                W(vector, "a_oc", 16 * c)
                W(vector, "a_eA", 2 * c)
                W(vector, "a_lmm", 2 * NGX * c)
                cur = dAf
                for si, s in enumerate([1, 2, 4, 8, 16, 32, 64]):
                    nxt = acs_a if si % 2 == 0 else acs_b
                    vector.tensor_tensor(out=nxt[:, s:], in0=cur[:, s:], in1=cur[:, :128 - s], op=ALU.add)
                    S_["a_acs"].inc(vector.tensor_copy(out=nxt[:, :s], in_=cur[:, :s]), 2)
                    W(vector, "a_acs", S_["a_acs"].n)
                    cur = nxt
                S_["a_acs"].inc(vector.tensor_copy(out=acs_b[:, 0:1], in_=acs_a[:, 0:1]))
                W(vector, "a_deca", c)
                S_["a_decv"].inc(vector.tensor_scalar(out=dec_fm[:], in0=acs_a[:],
                                                      scalar1=acs_a[:, 127:128], scalar2=-1.0,
                                                      op0=ALU.subtract, op1=ALU.mult))
                W(vector, "a_acc", c)
                S_["a_cum"].inc(vector.tensor_copy(out=cums_sb[:, c:c + 1], in_=acc_col[:]))
                W(vector, "a_cum", c + 1)
                S_["a_acc"].inc(vector.tensor_tensor(out=acc_col[:], in0=acc_col[:],
                                                     in1=acs_a[:, 127:128], op=ALU.add))
                W(vector, "a_deca", c + 1)
                S_["a_dd"].inc(vector.tensor_tensor(out=dtdec_bf[:], in0=dtf[:], in1=dec_fm[:], op=ALU.mult))
                W(vector, "a_dd", S_["a_dd"].n)
                S_["a_dtbf"].inc(vector.tensor_copy(out=dtf_bf[:], in_=dtf[:]))
                W(vector, "a_eA", 2 * (c + 1))
                W(vector, "a_dcm", NGX * c)
                S_["a_drep"].inc(vector.tensor_scalar(out=dec_rep[:], in0=ones3264_sb[:],
                                                      scalar1=dec_col[:, 0:1], scalar2=None, op0=ALU.mult))
                W(vector, "a_tr", 2 * (c + 1))
                W(vector, "a_scm", c)
                W(vector, "a_ym", 2 * NGX * c)
                vector.tensor_copy(out=B_fm[:], in_=pB16[0:64, 0:128])
                S_["a_bce"].inc(vector.tensor_copy(out=C_fm[:], in_=pB16[0:64, 128:256]))
                W(vector, "a_stm", c + 1)
                W(vector, "a_pt", NGX * c)
                S_["a_scm"].inc(vector.tensor_tensor(out=sT[:], in0=pS[:, 0:128], in1=masku_sb[:], op=ALU.mult))
                for sl in range(NGX):
                    i = c * NGX + sl
                    W(vector, "a_lmm", 2 * i + 2)
                    if c >= 1:
                        W(vector, "a_pt", (c - 1) * NGX + sl + 1)
                    S_["a_lme"].inc(vector.tensor_scalar_min(out=Lm[:, sl * 512:(sl + 1) * 512],
                                                             in0=pL[:], scalar1=0.0))
                    W(vector, "a_xm", 2 * i + 1)
                    if c >= 1 and sl == 0:
                        W(vector, "a_ym", 2 * NGX * c)
                        W(vector, "a_sm", NGX * c)
                    S_["a_xe"].inc(vector.tensor_tensor(out=xd[:, sl * 512:(sl + 1) * 512],
                                                        in0=xcs[c % 2][:, sl * 512:(sl + 1) * 512],
                                                        in1=pX[:], op=ALU.mult))
                    W(vector, "a_xm", 2 * i + 2)
                    S_["a_xe"].inc(vector.tensor_tensor(out=xdd[:, sl * 512:(sl + 1) * 512],
                                                        in0=xcs[c % 2][:, sl * 512:(sl + 1) * 512],
                                                        in1=pX[:], op=ALU.mult))
                    W(vector, "a_lmx", i + 1)
                    for hh in range(4):
                        ins = vector.tensor_tensor(
                            out=PTl[:, sl * 512 + hh * 128: sl * 512 + (hh + 1) * 128],
                            in0=Lm[:, sl * 512 + hh * 128: sl * 512 + (hh + 1) * 128],
                            in1=sT[:], op=ALU.mult)
                    S_["a_pt"].inc(ins)
                for g in range(NGX):
                    i = c * NGX + g
                    W(vector, "a_yoe", i + 1)
                    W(vector, "a_eam", i + 1)
                    S_["a_yos"].inc(vector.tensor_tensor(out=yo_s[:], in0=yo_s[:], in1=pX[:], op=ALU.mult))
                    W(vector, "a_yos", S_["a_yos"].n)
                    W(vector, "a_ym", 2 * i + 2)
                    S_["a_ygr"].inc(vector.scalar_tensor_tensor(out=ygrp[:], in0=pYd[:], scalar=1.0,
                                                                in1=yo_s[:], op0=ALU.mult, op1=ALU.add))
                    W(vector, "a_Df", NGX)
                    W(vector, "a_xdd2", i)
                    S_["a_xdd2"].inc(vector.tensor_tensor(out=xdD[:],
                                                          in0=xcs[c % 2][:, g * 512:(g + 1) * 512],
                                                          in1=D_full[:, g * 512:(g + 1) * 512], op=ALU.mult))
                    W(vector, "a_ygr", S_["a_ygr"].n)
                    W(vector, "a_xdd2", S_["a_xdd2"].n)
                    W(vector, f"a_o{c % 2}", 16 * (c // 2))
                    S_["a_ya"].inc(vector.tensor_tensor(out=y_all[c % 2][:, g * 512:(g + 1) * 512],
                                                        in0=ygrp[:], in1=xdD[:], op=ALU.add))
                    W(vector, "a_dcm", i + 1)
                    W(vector, "a_rc2", i)
                    S_["a_rc1"].inc(vector.tensor_tensor(out=tdec[:], in0=carry[:, g * 512:(g + 1) * 512],
                                                         in1=pDc[:], op=ALU.mult))
                    W(vector, "a_rc1", S_["a_rc1"].n)
                    W(vector, "a_sm", i + 1)
                    S_["a_rc2"].inc(vector.tensor_tensor(out=carry[:, g * 512:(g + 1) * 512],
                                                         in0=tdec[:], in1=pSt[:], op=ALU.add))
                W(vector, "a_rc2", NGX * (c + 1))
                W(vector, "a_ym", 2 * NGX * (c + 1))
                S_["a_cbf"].inc(vector.tensor_copy(out=carry_bf[:], in_=carry[:]))

        @block.scalar
        def _(scalar):
            W(scalar, "a_cw", 16 * 8)
            S_["a_A"].inc(scalar.activation(out=A_col[:], in_=A_col[:], func=AF.Exp))
            W(scalar, "a_A", 1)
            S_["a_A"].inc(scalar.mul(out=A_col[:], in_=A_col[:], mul=-1.0))
            for c in range(NCH):
                W(scalar, "a_decv", c + 1)
                S_["a_deca"].inc(scalar.activation(out=dec_fm[:], in_=dec_fm[:], func=AF.Exp))
                W(scalar, "a_acs", 15 * c + 14)
                W(scalar, "a_eam", NGX * c)
                W(scalar, "a_drep", c)
                S_["a_eA"].inc(scalar.activation(out=eA_fm[:], in_=acs_a[:], func=AF.Exp))
                S_["a_eA"].inc(scalar.activation(out=dec_col[:], in_=acs_a[:, 127:128], func=AF.Exp))
                for sl in range(NGX):
                    i = c * NGX + sl
                    W(scalar, "a_lme", i + 1)
                    S_["a_lmx"].inc(scalar.activation(out=Lm[:, sl * 512:(sl + 1) * 512],
                                                      in_=Lm[:, sl * 512:(sl + 1) * 512], func=AF.Exp))
                for g in range(NGX):
                    i = c * NGX + g
                    W(scalar, "a_ym", 2 * i + 1)
                    W(scalar, "a_ygr", i)
                    S_["a_yoe"].inc(scalar.activation(out=yo_s[:], in_=pYo[:], func=AF.Copy))

        @block.tensor
        def _(tensor):
            W(tensor, "a_cw", 16 * 8)
            for sl in range(NGX):
                W(tensor, "a_Df", sl)
                S_["a_Dfm"].inc(tensor.matmul(pX[:], ones1f_sb[:],
                                              Dexp_sb[0:1, sl * 512:(sl + 1) * 512],
                                              start=True, stop=True))
            for c in range(NCH):
                W(tensor, f"a_x{c % 2}", 16 * (c // 2 + 1))
                W(tensor, "a_bce", c)
                tensor.transpose(pB16[0:64, 0:128], xcs[c % 2][:, 4096:4160], ident_sb[:])
                S_["a_tr"].inc(tensor.transpose(pB16[0:64, 128:256], xcs[c % 2][:, 4160:4224], ident_sb[:]), 2)
                W(tensor, "a_bce", c + 1)
                W(tensor, "a_scm", c)
                S_["a_stm"].inc(tensor.matmul(pS[:, 0:128], B_fm[:], C_fm[:], start=True, stop=True))
                for sl in range(NGX):
                    i = c * NGX + sl
                    W(tensor, "a_bc1", 16 * (c + 1))
                    W(tensor, "a_lme", i)
                    tensor.matmul(pL[:], ones1f_sb[:], acs_rows[0:1, sl * 512:(sl + 1) * 512],
                                  start=True, stop=False)
                    S_["a_lmm"].inc(tensor.matmul(pL[:], acs_a[:], seln_sb[:, sl * 512:(sl + 1) * 512],
                                                  start=False, stop=True), 2)
                    W(tensor, "a_dtbf", c + 1)
                    W(tensor, "a_xe", 2 * i)
                    if i == 0:
                        W(tensor, "a_Df", NGX)
                    S_["a_xm"].inc(tensor.matmul(pX[:], dtf_bf[:], sel_sb[:, sl * 512:(sl + 1) * 512],
                                                 start=True, stop=True))
                    W(tensor, "a_dd", c + 1)
                    W(tensor, "a_xe", 2 * i + 1)
                    S_["a_xm"].inc(tensor.matmul(pX[:], dtdec_bf[:], sel_sb[:, sl * 512:(sl + 1) * 512],
                                                 start=True, stop=True))
                for g in range(NGX):
                    i = c * NGX + g
                    W(tensor, "a_yoe", i)
                    W(tensor, "a_cbf", c + 1)
                    for hh in range(4):
                        ins = tensor.matmul(pYo[:, hh * 128:(hh + 1) * 128], C_fm[:],
                                            carry_bf[:, (4 * g + hh) * 128:(4 * g + hh + 1) * 128],
                                            start=True, stop=True)
                    S_["a_ym"].inc(ins)
                    W(tensor, "a_ygr", i)
                    W(tensor, "a_pt", i + 1)
                    W(tensor, "a_xe", 2 * i + 1)
                    for hh in range(4):
                        off = g * 512 + hh * 128
                        ins = tensor.matmul(pYd[:, hh * 128:(hh + 1) * 128],
                                            PTl[:, off:off + 128], xd[:, off:off + 128],
                                            start=True, stop=True)
                    S_["a_ym"].inc(ins)
                    W(tensor, "a_eA", 2 * (c + 1))
                    W(tensor, "a_yos", i)
                    if g == 0:
                        W(tensor, "a_xe", 2 * NGX * (c + 1))
                    S_["a_eam"].inc(tensor.matmul(pX[:], eA_fm[:], sel_sb[:, g * 512:(g + 1) * 512],
                                                  start=True, stop=True))
                    W(tensor, "a_rc2", i)
                    W(tensor, "a_xe", 2 * i + 2)
                    for hh in range(4):
                        ins = tensor.matmul(pSt[:, hh * 128:(hh + 1) * 128],
                                            xcs[c % 2][:, 4096:4160],
                                            xdd[:, (4 * g + hh) * 128:(4 * g + hh + 1) * 128],
                                            start=True, stop=True)
                    S_["a_sm"].inc(ins)
                    W(tensor, "a_drep", c + 1)
                    W(tensor, "a_rc1", i)
                    S_["a_dcm"].inc(tensor.matmul(pDc[:], dec_rep[:], sel_sb[:, g * 512:(g + 1) * 512],
                                                  start=True, stop=True))

        @block.sync
        def _(sync):
            sync.wait_ge(S_["a_o0"].h, S_["a_o0"].base + S_["a_o0"].n)
            sync.wait_ge(S_["a_o1"].h, S_["a_o1"].base + S_["a_o1"].n)
            sync.wait_ge(S_["a_oc"].h, S_["a_oc"].base + S_["a_oc"].n)
    return nc


def emit_b2b(nc, TOWN, stack, io=None, mask=None):
    NCH = TOWN // 128
    NOB = TOWN // 512

    Yp = dram_io(nc, io, "Yp", [TOWN, DIN], BF16, "ExternalInput")
    xcT = dram_io(nc, io, "xcT", [TOWN, CONVD], BF16, "ExternalInput")
    z_in = dram_io(nc, io, "z_tm", [TOWN, DIN], BF16, "ExternalInput")
    acs_in = dram_io(nc, io, "acs_all", [NCH * 32, 128], F32, "ExternalInput")
    cums_in = dram_io(nc, io, "cums", [32, NCH], F32, "ExternalInput")
    carry0_in = dram_io(nc, io, "carry0", [NST, DIN], F32, "ExternalInput")
    woutT = dram_io(nc, io, "woutT", [DIN, 2048], F32, "ExternalInput")
    y2_out = dram_io(nc, io, "y2", [2048, TOWN], F32, "ExternalOutput")

    selb, cs1 = inline_const(nc, "selb", sel_matrix(bf16))
    ones3264, cs7 = inline_const(nc, "ones3264", np.ones((32, 64), dtype=bf16))
    identb, cs8 = inline_const(nc, "identb", np.eye(128, dtype=bf16))

    sb = lambda *a: stack.enter_context(nc.sbuf_tensor(_u(a[0]), *a[1:]))
    g_fm = [sb(f"g_fm{ic}", [128, TOWN], BF16) for ic in range(32)]

    st1 = ExitStack()
    sb1 = lambda *a: st1.enter_context(nc.sbuf_tensor(_u(a[0]), *a[1:]))
    ps1 = lambda *a: st1.enter_context(nc.psum_tensor(_u(a[0]), *a[1:]))
    sel_sb = sb1("sel_sb", [32, 4096], BF16)
    ones3264_sb = sb1("o3264_sb", [32, 64], BF16)
    ident_sb = sb1("ident_sb", [128, 128], BF16)
    cums_sb = sb1("cums_sb", [32, 16], F32)
    carry0 = sb1("carry0_s", [64, 4096], F32)
    ysb = [sb1("ysb0", [128, 4096], BF16)] * 2
    zsb = [sb1("zsb0", [128, 4096], BF16)] * 2
    Csl = [sb1(f"Csl{i}", [128, 64], BF16) for i in range(2)]
    acsc = [sb1(f"acsc{i}", [32, 128], F32) for i in range(2)]
    cdc = sb1("cdc", [32, 1], F32)
    crep = sb1("crep", [32, 64], BF16)
    c0b = sb1("c0b", [64, 512], BF16)
    C_fm = sb1("C_fm", [64, 128], BF16)
    eA_fm = sb1("eA_fm", [32, 128], BF16)
    dy_s = sb1("dy_s", [128, 512], F32)
    yf32 = sb1("yf32", [128, 4096], F32)
    ms = sb1("ms", [128, 4], F32)
    zs_s = sb1("zs_s", [128, 4096], BF16)
    gbuf = sb1("gbuf", [128, 512], F32)
    gtm = sb1("gtm", [128, 4096], BF16)
    fcb = sb1("fcb", [128, 2], F32)
    pC = ps1("pC", [64, 512], F32)
    pD = ps1("pD", [128, 512], F32)
    pE = ps1("pE", [128, 512], F32)
    pT = ps1("pT", [128, 512], BF16)
    S_ = mk_sems(nc, stack, [
        "b_cw", "b_y0", "b_y1", "b_z0", "b_z1", "b_C0", "b_C1", "b_a0", "b_a1",
        "b_cd", "b_crep", "b_cbm", "b_c0b", "b_tr", "b_ce", "b_eA", "b_em",
        "b_de", "b_ds", "b_yac", "b_sq", "b_ms", "b_rstd", "b_sil",
        "b_g1", "b_g2", "b_gt", "b_gte", "b_fc", "b_v", "b_a", "b_msk"])

    def W(eng, sem, v):
        if v > 0:
            eng.wait_ge(S_[sem].h, S_[sem].base + v)

    with nc.Block() as block:
        @block.sync
        def _(sync):
            S_["b_cw"].inc_dma(load_const(sync, sel_sb[:], selb, cs1))
            S_["b_cw"].inc_dma(load_const(sync, ones3264_sb[:], ones3264, cs7))
            S_["b_cw"].inc_dma(load_const(sync, ident_sb[:], identb, cs8))
            S_["b_cw"].inc_dma(sync.dma_start(out=cums_sb[:, 0:NCH], in_=cums_in))
            S_["b_cw"].inc_dma(sync.dma_start(out=carry0[:], in_=carry0_in))
            for c in range(NCH):
                W(sync, "b_yac", NGX * c)
                S_[f"b_y{c % 2}"].inc_dma(sync.dma_start(out=ysb[c % 2][:], in_=Yp[c * 128:(c + 1) * 128, :]))
                W(sync, "b_sil", c)
                S_[f"b_z{c % 2}"].inc_dma(sync.dma_start(out=zsb[c % 2][:], in_=z_in[c * 128:(c + 1) * 128, :]))
                W(sync, "b_tr", c - 1)
                S_[f"b_C{c % 2}"].inc_dma(sync.dma_start(out=Csl[c % 2][:],
                                                         in_=xcT[c * 128:(c + 1) * 128, 4160:4224]))
                W(sync, "b_eA", c - 1)
                S_[f"b_a{c % 2}"].inc_dma(sync.dma_start(out=acsc[c % 2][:],
                                                         in_=acs_in[32 * c:32 * (c + 1), :]))

        @block.vector
        def _(vector):
            vector.memset(fcb[:, 0:1], EPS_RMS)
            S_["b_fc"].inc(vector.memset(fcb[:, 1:2], -0.5))
            if mask is not None:
                W(vector, "b_cw", 16 * 5)
                S_["b_msk"].inc(vector.tensor_scalar_mul(out=carry0[:], in0=carry0[:], scalar1=mask))
            for c in range(NCH):
                W(vector, "b_cd", c + 1)
                W(vector, "b_cbm", NGX * c)
                S_["b_crep"].inc(vector.tensor_scalar(out=crep[:], in0=ones3264_sb[:],
                                                      scalar1=cdc[:, 0:1], scalar2=None, op0=ALU.mult))
                W(vector, "b_tr", c + 1)
                W(vector, "b_ds", NGX * c)
                S_["b_ce"].inc(vector.tensor_copy(out=C_fm[:], in_=pT[0:64, 0:128]))
                for g in range(NGX):
                    i = c * NGX + g
                    W(vector, "b_cbm", i + 1)
                    W(vector, "b_ds", i)
                    W(vector, "b_msk", 1 if mask is not None else 0)
                    S_["b_c0b"].inc(vector.tensor_tensor(out=c0b[:], in0=carry0[:, g * 512:(g + 1) * 512],
                                                         in1=pC[:], op=ALU.mult))
                    W(vector, "b_de", i + 1)
                    W(vector, "b_em", i + 1)
                    S_["b_v"].inc(vector.tensor_tensor(out=dy_s[:], in0=dy_s[:], in1=pE[:], op=ALU.mult))
                    W(vector, "b_v", S_["b_v"].n)
                    W(vector, f"b_y{c % 2}", 16 * (c // 2 + 1))
                    W(vector, "b_sq", c)
                    S_["b_yac"].inc(vector.tensor_tensor(out=yf32[:, g * 512:(g + 1) * 512],
                                                         in0=ysb[c % 2][:, g * 512:(g + 1) * 512],
                                                         in1=dy_s[:], op=ALU.add))
                W(vector, "b_sq", c + 1)
                S_["b_ms"].inc(vector.tensor_scalar_mul(out=ms[:, 1:2], in0=ms[:, 0:1], scalar1=1.0 / DIN))
                for g in range(NGX):
                    i = c * NGX + g
                    W(vector, "b_sil", c + 1)
                    W(vector, "b_rstd", 2 * (c + 1))
                    W(vector, "b_g2", i)
                    S_["b_g1"].inc(vector.tensor_tensor(out=gbuf[:], in0=yf32[:, g * 512:(g + 1) * 512],
                                                        in1=zs_s[:, g * 512:(g + 1) * 512], op=ALU.mult))
                    W(vector, "b_g1", S_["b_g1"].n)
                    W(vector, "b_gt", 8 * c)
                    W(vector, "b_sq", c + 1)
                    S_["b_g2"].inc(vector.tensor_scalar(out=gtm[:, g * 512:(g + 1) * 512], in0=gbuf[:],
                                                        scalar1=ms[:, 3:4], scalar2=None, op0=ALU.mult))
                for b in range(8):
                    W(vector, "b_gt", 8 * c + b + 1)
                    for j in range(4):
                        ic = 4 * b + j
                        ins = vector.tensor_copy(out=g_fm[ic][:, c * 128:(c + 1) * 128],
                                                 in_=pT[:, j * 128:(j + 1) * 128])
                    S_["b_gte"].inc(ins)

        @block.scalar
        def _(scalar):
            W(scalar, "b_cw", 16 * 5)
            for c in range(NCH):
                W(scalar, "b_crep", c)
                S_["b_cd"].inc(scalar.activation(out=cdc[:], in_=cums_sb[:, c:c + 1], func=AF.Exp))
                W(scalar, f"b_a{c % 2}", 16 * (c // 2 + 1))
                W(scalar, "b_em", NGX * c)
                S_["b_eA"].inc(scalar.activation(out=eA_fm[:], in_=acsc[c % 2][:], func=AF.Exp))
                for g in range(NGX):
                    i = c * NGX + g
                    W(scalar, "b_ds", i + 1)
                    W(scalar, "b_v", i)
                    S_["b_de"].inc(scalar.activation(out=dy_s[:], in_=pD[:], func=AF.Copy))
                W(scalar, "b_yac", NGX * (c + 1))
                W(scalar, "b_ms", c)
                W(scalar, "b_gt", 8 * c)
                S_["b_sq"].inc(scalar.activation(out=gtm[:], in_=yf32[:], func=AF.Square,
                                                 accum_out=ms[:, 0:1]))
                W(scalar, "b_ms", c + 1)
                if c == 0:
                    W(scalar, "b_fc", 1)
                S_["b_a"].inc(scalar.activation(out=ms[:, 2:3], in_=ms[:, 1:2], func=AF.Ln, bias=fcb[:, 0:1]))
                W(scalar, "b_a", S_["b_a"].n)
                S_["b_rstd"].inc(scalar.activation(out=ms[:, 3:4], in_=ms[:, 2:3], func=AF.Exp,
                                                   scale=fcb[:, 1:2]), 2)
                W(scalar, f"b_z{c % 2}", 16 * (c // 2 + 1))
                W(scalar, "b_g1", NGX * c)
                S_["b_sil"].inc(scalar.activation(out=zs_s[:], in_=zsb[c % 2][:], func=AF.Silu))

        @block.tensor
        def _(tensor):
            W(tensor, "b_cw", 16 * 5)
            for c in range(NCH):
                W(tensor, f"b_C{c % 2}", 16 * (c // 2 + 1))
                W(tensor, "b_ce", c)
                W(tensor, "b_gte", 8 * c)
                S_["b_tr"].inc(tensor.transpose(pT[0:64, 0:128], Csl[c % 2][:], ident_sb[:]))
                for g in range(NGX):
                    i = c * NGX + g
                    W(tensor, "b_crep", c + 1)
                    W(tensor, "b_c0b", i)
                    S_["b_cbm"].inc(tensor.matmul(pC[:], crep[:], sel_sb[:, g * 512:(g + 1) * 512],
                                                  start=True, stop=True))
                    W(tensor, "b_c0b", i + 1)
                    W(tensor, "b_ce", c + 1)
                    W(tensor, "b_de", i)
                    for hh in range(4):
                        ins = tensor.matmul(pD[:, hh * 128:(hh + 1) * 128], C_fm[:],
                                            c0b[:, hh * 128:(hh + 1) * 128], start=True, stop=True)
                    S_["b_ds"].inc(ins)
                    W(tensor, "b_eA", c + 1)
                    W(tensor, "b_v", i)
                    S_["b_em"].inc(tensor.matmul(pE[:], eA_fm[:], sel_sb[:, g * 512:(g + 1) * 512],
                                                 start=True, stop=True))
                for b in range(8):
                    W(tensor, "b_g2", c * NGX + b + 1)
                    W(tensor, "b_gte", 8 * c + b)
                    W(tensor, "b_ce", c + 1)
                    for j in range(4):
                        ic = 4 * b + j
                        ins = tensor.transpose(pT[:, j * 128:(j + 1) * 128],
                                               gtm[:, ic * 128:(ic + 1) * 128], ident_sb[:])
                    S_["b_gt"].inc(ins)

        @block.sync
        def _(sync):
            sync.wait_ge(S_["b_gte"].h, S_["b_gte"].base + 8 * NCH)
    st1.close()

    st2 = ExitStack()
    sb2 = lambda *a: st2.enter_context(nc.sbuf_tensor(_u(a[0]), *a[1:]))
    ps2 = lambda *a: st2.enter_context(nc.psum_tensor(_u(a[0]), *a[1:]))
    wst = [sb2(f"wst{i}", [128, 32 * 128], F32) for i in range(2)]
    wbf = [sb2(f"wbf{i}", [128, 32 * 128], BF16) for i in range(2)]
    y2s = [sb2(f"y2s{i}", [128, 512], F32) for i in range(2)]
    po = [ps2(f"po{i}", [128, 512], F32) for i in range(2)]
    S2 = mk_sems(nc, stack, ["o_w0", "o_w1", "o_cv", "o_mm", "o_ev", "o_o0", "o_o1"])

    def W2(eng, sem, v):
        if v > 0:
            eng.wait_ge(S2[sem].h, S2[sem].base + v)

    with nc.Block() as block:
        @block.sync
        def _(sync):
            for dc in range(16):
                W2(sync, "o_cv", dc - 1)
                src = woutT[:, dc * 128:(dc + 1) * 128].rearrange("(a p) c -> p a c", p=128)
                S2[f"o_w{dc % 2}"].inc_dma(sync.dma_start(
                    out=wst[dc % 2][:].rearrange("p (a c) -> p a c", a=32), in_=src))

        @block.vector
        def _(vector):
            for dc in range(16):
                W2(vector, f"o_w{dc % 2}", 16 * (dc // 2 + 1))
                if dc >= 2:
                    W2(vector, "o_mm", NOB * (dc - 1))
                S2["o_cv"].inc(vector.tensor_copy(out=wbf[dc % 2][:], in_=wst[dc % 2][:]))

        @block.tensor
        def _(tensor):
            for dc in range(16):
                W2(tensor, "o_cv", dc + 1)
                for tb in range(NOB):
                    i = dc * NOB + tb
                    W2(tensor, "o_ev", i - 1)
                    for ic in range(32):
                        ins = tensor.matmul(po[i % 2][:], wbf[dc % 2][:, ic * 128:(ic + 1) * 128],
                                            g_fm[ic][:, tb * 512:(tb + 1) * 512],
                                            start=(ic == 0), stop=(ic == 31))
                    S2["o_mm"].inc(ins)

        @block.scalar
        def _(scalar):
            for dc in range(16):
                for tb in range(NOB):
                    i = dc * NOB + tb
                    W2(scalar, "o_mm", i + 1)
                    W2(scalar, f"o_o{i % 2}", 16 * (i // 2))
                    S2["o_ev"].inc(scalar.activation(out=y2s[i % 2][:], in_=po[i % 2][:], func=AF.Copy))

        @block.gpsimd
        def _(gpsimd):
            for dc in range(16):
                for tb in range(NOB):
                    i = dc * NOB + tb
                    W2(gpsimd, "o_ev", i + 1)
                    S2[f"o_o{i % 2}"].inc_dma(gpsimd.dma_start(
                        out=y2_out[dc * 128:(dc + 1) * 128, tb * 512:(tb + 1) * 512],
                        in_=y2s[i % 2][:]))

        @block.sync
        def _(sync):
            sync.wait_ge(S2["o_o0"].h, S2["o_o0"].base + S2["o_o0"].n)
            sync.wait_ge(S2["o_o1"].h, S2["o_o1"].base + S2["o_o1"].n)
    st2.close()
    return nc



D = 2048
NFC = 16
NI = 8192


def emit_c(nc, TW, stack, io=None):
    TOWN = TW - 128
    NTT = TOWN // 128
    NGB = 4
    NT9 = TOWN // 256

    xn1_in = dram_io(nc, io, "xn1", [D, TW], BF16, "ExternalInput")
    x1_in = dram_io(nc, io, "x1", [D, TW], F32, "ExternalInput")
    y2_in = dram_io(nc, io, "y2", [D, TOWN], F32, "ExternalInput")
    wg_in = dram_io(nc, io, "wg1p", [D, NI], F32, "ExternalInput")
    wu_in = dram_io(nc, io, "wu1p", [D, NI], F32, "ExternalInput")
    wd_in = dram_io(nc, io, "wd1p", [NI, D], F32, "ExternalInput")
    x2_out = dram_io(nc, io, "x2", [D, TOWN], F32, "ExternalOutput")
    hsp = dram_io(nc, io, "hsp", [NI, TOWN], BF16, "Internal")

    identb, cs8 = inline_const(nc, "identb", np.eye(128, dtype=bf16))

    st1 = ExitStack()
    sb1 = lambda *a: st1.enter_context(nc.sbuf_tensor(_u(a[0]), *a[1:]))
    ps1 = lambda *a: st1.enter_context(nc.psum_tensor(_u(a[0]), *a[1:]))
    xn1_sb = sb1("xn1_sb", [128, NFC * TOWN], BF16)
    xnv = xn1_sb[:].rearrange("p (a c) -> p a c", a=NFC)
    wstage = sb1("wstage", [128, NFC * 256], F32)
    wg_bf = sb1("wg_bf", [128, NFC * 512], BF16)
    wu_bf = sb1("wu_bf", [128, NFC * 512], BF16)
    ident_sb = sb1("ident_s", [128, 128], BF16)
    hqf = [sb1(f"hqf{j}", [128, NTT * 512], BF16) for j in range(4)]
    aq = [sb1(f"aq{j}", [128, 512], F32) for j in range(4)]
    t1 = sb1("t1", [128, 512], F32)
    b1 = sb1("b1", [128, 512], F32)
    m2 = sb1("m2", [128, 512], F32)
    kq = sb1("kq", [128, 512], F32)
    hs = sb1("hs", [128, 512], BF16)
    hto = [sb1(f"hto{i}", [128, 512], BF16) for i in range(2)]
    pg = [ps1(f"pg{i}", [128, 512], F32) for i in range(2)]
    pu = [ps1(f"pu{i}", [128, 512], F32) for i in range(2)]
    pT = ps1("pT", [128, 512], BF16)
    S1 = mk_sems(nc, stack, ["c_xn", "c_wl0", "c_wl1", "c_cv", "c_gmm", "c_umm",
                             "c_ge", "c_ue", "c_hq", "c_sp", "c_tr", "c_te",
                             "c_o0", "c_o1", "c_v", "c_a"])

    def W(S_, eng, sem, v):
        if v > 0:
            eng.wait_ge(S_[sem].h, S_[sem].base + v)

    with nc.Block() as block:
        @block.sync
        def _(sync):
            S1["c_xn"].inc_dma(load_const(sync, ident_sb[:], identb, cs8))
            for fcx in range(NFC):
                S1["c_xn"].inc_dma(sync.dma_start(out=xnv[:, fcx, :],
                                                  in_=xn1_in[fcx * 128:(fcx + 1) * 128, 0:TOWN]))
            for gb in range(NGB):
                for q in range(4):
                    k = gb * 4 + q
                    col = q * 2048 + gb * 512
                    for hf in range(4):
                        m = 8 * k + 2 * hf
                        W(S1, sync, "c_cv", m - 1)
                        srcw = wg_in[:, col + hf * 128: col + (hf + 1) * 128].rearrange("(a p) c -> p a c", p=128)
                        S1[f"c_wl{m % 2}"].inc_dma(sync.dma_start(
                            out=wstage[:].rearrange("p (a c) -> p a c", a=NFC)[:, :, 0:128], in_=srcw))
                        m = 8 * k + 2 * hf + 1
                        W(S1, sync, "c_cv", m - 1)
                        srcw = wu_in[:, col + hf * 128: col + (hf + 1) * 128].rearrange("(a p) c -> p a c", p=128)
                        S1[f"c_wl{m % 2}"].inc_dma(sync.dma_start(
                            out=wstage[:].rearrange("p (a c) -> p a c", a=NFC)[:, :, 128:256], in_=srcw))

        @block.vector
        def _(vector):
            for gb in range(NGB):
                for q in range(4):
                    k = gb * 4 + q
                    for hf in range(4):
                        m = 8 * k + 2 * hf
                        W(S1, vector, f"c_wl{m % 2}", 16 * (m // 2 + 1))
                        if k >= 1 and hf == 0:
                            W(S1, vector, "c_gmm", NTT * k)
                        S1["c_cv"].inc(vector.tensor_copy(
                            out=wg_bf[:].rearrange("p (a c) -> p a c", a=NFC)[:, :, hf * 128:(hf + 1) * 128],
                            in_=wstage[:].rearrange("p (a c) -> p a c", a=NFC)[:, :, 0:128]))
                        m = 8 * k + 2 * hf + 1
                        W(S1, vector, f"c_wl{m % 2}", 16 * (m // 2 + 1))
                        if k >= 1 and hf == 0:
                            W(S1, vector, "c_umm", NTT * k)
                        S1["c_cv"].inc(vector.tensor_copy(
                            out=wu_bf[:].rearrange("p (a c) -> p a c", a=NFC)[:, :, hf * 128:(hf + 1) * 128],
                            in_=wstage[:].rearrange("p (a c) -> p a c", a=NFC)[:, :, 128:256]))
                    for tt in range(NTT):
                        j = q * NTT + tt
                        gj = gb * 4 * NTT + j
                        W(S1, vector, "c_umm", gj + 1)
                        W(S1, vector, "c_ge", gj + 1)
                        if gb >= 1 and j == 0:
                            W(S1, vector, "c_tr", 4 * NTT * gb)
                        ins = vector.tensor_tensor(out=hqf[q][:, tt * 512:(tt + 1) * 512],
                                                   in0=hs[:], in1=pu[j % 2][:], op=ALU.mult)
                        S1["c_ue"].inc(ins)
                for tt in range(NTT):
                    it = gb * NTT + tt
                    def vv(ins):
                        S1["c_v"].inc(ins)
                        W(S1, vector, "c_v", S1["c_v"].n)
                    W(S1, vector, "c_ue", gb * 4 * NTT + 3 * NTT + tt + 1)
                    sl = slice(tt * 512, (tt + 1) * 512)
                    for j in range(4):
                        vv(vector.tensor_tensor(out=aq[j][:], in0=hqf[j][:, sl], in1=hqf[j][:, sl], op=ALU.mult))
                        vv(vector.tensor_scalar_mul(out=aq[j][:], in0=aq[j][:], scalar1=1.0 - j * 2.0 ** -11))
                    vv(vector.tensor_tensor(out=t1[:], in0=aq[0][:], in1=aq[1][:], op=ALU.max))
                    vv(vector.tensor_tensor(out=b1[:], in0=aq[0][:], in1=aq[1][:], op=ALU.min))
                    vv(vector.tensor_tensor(out=m2[:], in0=aq[2][:], in1=aq[3][:], op=ALU.max))
                    vv(vector.tensor_tensor(out=kq[:], in0=aq[2][:], in1=aq[3][:], op=ALU.min))
                    vv(vector.tensor_tensor(out=t1[:], in0=t1[:], in1=m2[:], op=ALU.min))
                    vv(vector.tensor_tensor(out=b1[:], in0=b1[:], in1=kq[:], op=ALU.max))
                    vv(vector.tensor_tensor(out=m2[:], in0=t1[:], in1=b1[:], op=ALU.max))
                    for j in range(4):
                        W(S1, vector, "c_sp", S1["c_sp"].n)
                        vv(vector.tensor_tensor(out=kq[:], in0=aq[j][:], in1=m2[:], op=ALU.is_ge))
                        S1["c_sp"].inc(vector.tensor_tensor(out=hqf[j][:, sl], in0=hqf[j][:, sl],
                                                            in1=kq[:], op=ALU.mult), 2)
                    for j in range(4):
                        i = it * 4 + j
                        W(S1, vector, "c_tr", i + 1)
                        W(S1, vector, f"c_o{i % 2}", 16 * (i // 2))
                        S1["c_te"].inc(vector.tensor_copy(out=hto[i % 2][:], in_=pT[:]))

        @block.scalar
        def _(scalar):
            for gb in range(NGB):
                for q in range(4):
                    for tt in range(NTT):
                        j = q * NTT + tt
                        gj = gb * 4 * NTT + j
                        W(S1, scalar, "c_gmm", gj + 1)
                        W(S1, scalar, "c_ue", gj)
                        S1["c_ge"].inc(scalar.activation(out=hs[:], in_=pg[j % 2][:], func=AF.Silu))

        @block.tensor
        def _(tensor):
            W(S1, tensor, "c_xn", 16 * (NFC + 1))
            for gb in range(NGB):
                for q in range(4):
                    k = gb * 4 + q
                    W(S1, tensor, "c_cv", 8 * (k + 1))
                    for tt in range(NTT):
                        j = q * NTT + tt
                        gj = gb * 4 * NTT + j
                        W(S1, tensor, "c_ge", gj - 1)
                        for fcx in range(NFC):
                            ins = tensor.matmul(pg[j % 2][:], xnv[:, fcx, tt * 128:(tt + 1) * 128],
                                                wg_bf[:].rearrange("p (a c) -> p a c", a=NFC)[:, fcx, :],
                                                start=(fcx == 0), stop=(fcx == NFC - 1))
                        S1["c_gmm"].inc(ins)
                        W(S1, tensor, "c_ue", gj - 1)
                        for fcx in range(NFC):
                            ins = tensor.matmul(pu[j % 2][:], xnv[:, fcx, tt * 128:(tt + 1) * 128],
                                                wu_bf[:].rearrange("p (a c) -> p a c", a=NFC)[:, fcx, :],
                                                start=(fcx == 0), stop=(fcx == NFC - 1))
                        S1["c_umm"].inc(ins)
                for tt in range(NTT):
                    it = gb * NTT + tt
                    W(S1, tensor, "c_sp", 8 * (it + 1))
                    for j in range(4):
                        i = it * 4 + j
                        W(S1, tensor, "c_te", i)
                        for sub in range(4):
                            ins = tensor.transpose(pT[:, sub * 128:(sub + 1) * 128],
                                                   hqf[j][:, tt * 512 + sub * 128: tt * 512 + (sub + 1) * 128],
                                                   ident_sb[:])
                        S1["c_tr"].inc(ins)

        @block.gpsimd
        def _(gpsimd):
            for gb in range(NGB):
                for tt in range(NTT):
                    it = gb * NTT + tt
                    for j in range(4):
                        i = it * 4 + j
                        W(S1, gpsimd, "c_te", i + 1)
                        row0 = j * 2048 + gb * 512
                        dst = hsp[row0:row0 + 512, tt * 128:(tt + 1) * 128]
                        dst = dst.rearrange("(a p) c -> p a c", p=128)
                        S1[f"c_o{i % 2}"].inc_dma(gpsimd.dma_start(
                            out=dst, in_=hto[i % 2][:].rearrange("p (a c) -> p a c", a=4)))

        @block.sync
        def _(sync):
            sync.wait_ge(S1["c_o0"].h, S1["c_o0"].base + S1["c_o0"].n)
            sync.wait_ge(S1["c_o1"].h, S1["c_o1"].base + S1["c_o1"].n)
    st1.close()

    st2 = ExitStack()
    sb2 = lambda *a: st2.enter_context(nc.sbuf_tensor(_u(a[0]), *a[1:]))
    ps2 = lambda *a: st2.enter_context(nc.psum_tensor(_u(a[0]), *a[1:]))
    wst = sb2("wst", [128, 32 * 512], F32)
    wd_bf = sb2("wd_bf", [128, 64 * 512], BF16)
    h_sb = [sb2(f"h_sb{i}", [128, 64 * 256], BF16) for i in range(2)]
    x1t = [sb2(f"x1t{i}", [128, 256], F32) for i in range(2)]
    y2t = [sb2(f"y2t{i}", [128, 256], F32) for i in range(2)]
    acc = sb2("acc_c2", [128, 256], F32)
    x2t = [sb2(f"x2t{i}", [128, 256], F32) for i in range(2)]
    pb8 = [ps2(f"pb8_{i}", [128, 256], F32) for i in range(8)]
    S2 = mk_sems(nc, stack, ["d_w", "d_cv", "d_h0", "d_h1", "d_mm", "d_x0", "d_x1",
                             "d_y0", "d_y1", "d_add", "d_fin", "d_o0", "d_o1", "d_v"])
    hv = [h_sb[i][:].rearrange("p (a c) -> p a c", a=64) for i in range(2)]

    with nc.Block() as block:
        @block.sync
        def _(sync):
            for og in range(4):
                for hf in range(2):
                    W(S2, sync, "d_cv", 2 * og + hf)
                    src = wd_in[4096 * hf:4096 * (hf + 1), 512 * og:512 * (og + 1)].rearrange("(a p) c -> p a c", p=128)
                    S2["d_w"].inc_dma(sync.dma_start(out=wst[:].rearrange("p (a c) -> p a c", a=32), in_=src))
                for it in range(NT9):
                    t0 = it * 256
                    i5 = og * NT9 + it
                    W(S2, sync, "d_mm", 4 * (i5 - 1))
                    src = hsp[:, t0:t0 + 256].rearrange("(a p) c -> p a c", p=128)
                    S2[f"d_h{i5 % 2}"].inc_dma(sync.dma_start(out=hv[i5 % 2][:, :, :], in_=src))

        @block.gpsimd
        def _(gpsimd):
            for og in range(4):
                for it in range(NT9):
                    t0 = it * 256
                    for j in range(4):
                        oc = 4 * og + j
                        i = (og * NT9 + it) * 4 + j
                        W(S2, gpsimd, "d_add", i - 1)
                        S2[f"d_x{i % 2}"].inc_dma(gpsimd.dma_start(
                            out=x1t[i % 2][:], in_=x1_in[oc * 128:(oc + 1) * 128, t0:t0 + 256]))
                        W(S2, gpsimd, "d_fin", i - 1)
                        S2[f"d_y{i % 2}"].inc_dma(gpsimd.dma_start(
                            out=y2t[i % 2][:], in_=y2_in[oc * 128:(oc + 1) * 128, t0:t0 + 256]))

        @block.vector
        def _(vector):
            for og in range(4):
                for hf in range(2):
                    W(S2, vector, "d_w", 16 * (2 * og + hf + 1))
                    if og >= 1 and hf == 0:
                        W(S2, vector, "d_mm", 4 * og * NT9)
                    S2["d_cv"].inc(vector.tensor_copy(out=wd_bf[:, hf * 16384:(hf + 1) * 16384], in_=wst[:]))
                for it in range(NT9):
                    for j in range(4):
                        oc = 4 * og + j
                        i = (og * NT9 + it) * 4 + j
                        W(S2, vector, "d_mm", i + 1)
                        W(S2, vector, f"d_x{i % 2}", 16 * (i // 2 + 1))
                        W(S2, vector, "d_fin", i)
                        S2["d_add"].inc(vector.scalar_tensor_tensor(out=acc[:], in0=pb8[(it % 2) * 4 + j][:],
                                                                    scalar=1.0, in1=x1t[i % 2][:],
                                                                    op0=ALU.mult, op1=ALU.add))
                        W(S2, vector, "d_add", S2["d_add"].n)
                        W(S2, vector, f"d_y{i % 2}", 16 * (i // 2 + 1))
                        W(S2, vector, f"d_o{i % 2}", 16 * (i // 2))
                        S2["d_fin"].inc(vector.tensor_tensor(out=x2t[i % 2][:], in0=acc[:],
                                                             in1=y2t[i % 2][:], op=ALU.add))

        @block.tensor
        def _(tensor):
            for og in range(4):
                W(S2, tensor, "d_cv", 2 * (og + 1))
                for it in range(NT9):
                    i5 = og * NT9 + it
                    W(S2, tensor, f"d_h{i5 % 2}", 16 * (i5 // 2 + 1))
                    if i5 >= 2:
                        W(S2, tensor, "d_add", 4 * (i5 - 1))
                    for ic in range(64):
                        for j in range(4):
                            bank = (it % 2) * 4 + j
                            ins = tensor.matmul(pb8[bank][:],
                                                wd_bf[:, ic * 512 + j * 128:ic * 512 + (j + 1) * 128],
                                                hv[i5 % 2][:, ic, :],
                                                start=(ic == 0), stop=(ic == 63))
                            if ic == 63:
                                S2["d_mm"].inc(ins)

        @block.scalar
        def _(scalar):
            for og in range(4):
                for it in range(NT9):
                    t0 = it * 256
                    for j in range(4):
                        oc = 4 * og + j
                        i = (og * NT9 + it) * 4 + j
                        W(S2, scalar, "d_fin", i + 1)
                        S2[f"d_o{i % 2}"].inc_dma(scalar.dma_start(
                            out=x2_out[oc * 128:(oc + 1) * 128, t0:t0 + 256], in_=x2t[i % 2][:]))

        @block.sync
        def _(sync):
            sync.wait_ge(S2["d_o0"].h, S2["d_o0"].base + S2["d_o0"].n)
            sync.wait_ge(S2["d_o1"].h, S2["d_o1"].base + S2["d_o1"].n)
    st2.close()
    return nc



def perm_index(NIw):
    ar = np.arange(NIw)
    return 4 * (ar % (NIw // 4)) + ar // (NIw // 4)


def perm_cols(w):
    return np.ascontiguousarray(w[:, perm_index(w.shape[1])])


def perm_rows(w):
    return np.ascontiguousarray(w[perm_index(w.shape[0]), :])





S_FULL = 4096
TOWN = 2048
TW = 2176
_PAIRS = [[0, 1], [2, 3], [4, 5], [6, 7]]
_CACHE = {}


def build_fused():
    nc = bass.Bass("TRN2", target_bir_lowering=False, debug=False)
    gstack = ExitStack()
    nc._ag_sem_pool = []
    nc._ag_sem_stack = gstack

    attn_loc = nc.dram_tensor("attn_loc", [1024, 4096], BF16, kind="Internal")
    xn_full = nc.dram_tensor("xn_full", [2048, 4096], BF16, kind="Internal")
    attn_gat = nc.dram_tensor("attn_gat", [2048, 4096], BF16, kind="Internal")
    x1_t = nc.dram_tensor("x1_t", [2048, TW], F32, kind="Internal")
    xn1_t = nc.dram_tensor("xn1_t", [2048, TW], BF16, kind="Internal")
    xcT_t = nc.dram_tensor("xcT_t", [TOWN, 4224], BF16, kind="Internal")
    z_t = nc.dram_tensor("z_t", [TOWN, 4096], BF16, kind="Internal")
    dtf_t = nc.dram_tensor("dtf_t", [32, TW], F32, kind="Internal")
    Yp_t = nc.dram_tensor("Yp_t", [TOWN, 4096], BF16, kind="Internal")
    carry_t = nc.dram_tensor("carry_t", [64, 4096], F32, kind="Internal")
    carry_gat = nc.dram_tensor("carry_gat", [128, 4096], F32, kind="Internal")
    acs_t = nc.dram_tensor("acs_t", [512, 128], F32, kind="Internal")
    cums_t = nc.dram_tensor("cums_t", [32, 16], F32, kind="Internal")
    y2_t = nc.dram_tensor("y2_t", [2048, TOWN], F32, kind="Internal")
    x2_o = nc.dram_tensor("x2", [2048, TOWN], F32, kind="ExternalOutput")
    offs_in = nc.dram_tensor("offs", [1, 2], I32, kind="ExternalInput")
    mask_in = nc.dram_tensor("mask64", [64, 1], F32, kind="ExternalInput")

    io = {"attn": attn_loc.ap(), "xn": xn_full.ap(), "x1": x1_t.ap(), "xn1": xn1_t.ap(),
          "xcT": xcT_t.ap(), "z_tm": z_t.ap(), "dt_f": dtf_t.ap(), "Yp": Yp_t.ap(),
          "carry": carry_t.ap(), "acs_all": acs_t.ap(), "cums": cums_t.ap(),
          "carry0": carry_gat.ap()[0:64, :], "y2": y2_t.ap(), "x2": x2_o.ap()}

    offs_sb = gstack.enter_context(nc.sbuf_tensor("offs_sb", [1, 2], I32))
    mask_sb = gstack.enter_context(nc.sbuf_tensor("mask_sb", [64, 1], F32))
    psem = gstack.enter_context(nc.semaphore("pre_sem"))
    ccsem = gstack.enter_context(nc.semaphore("cc_sem"))

    with nc.Block() as block:
        @block.sync
        def _(sync):
            sync.dma_start(out=offs_sb[:], in_=offs_in.ap()).then_inc(psem, 16)
            sync.dma_start(out=mask_sb[:], in_=mask_in.ap()).then_inc(psem, 16)
            sync.wait_ge(psem, 32)

    st = ExitStack(); emit_a1(nc, S_FULL, st, io=io); st.close()

    with nc.Block() as block:
        @block.gpsimd
        def _(g):
            g.collective_compute("AllGather", ALU.bypass, replica_groups=_PAIRS,
                                 ins=[attn_loc.ap().opt()],
                                 outs=[attn_gat.ap().opt()]).then_inc(ccsem, 1)
            g.wait_ge(ccsem, 1)

    st = ExitStack()
    emit_mid(nc, TW, st, io=io, win={"offs": offs_sb[:], "attn_full": attn_gat.ap(),
                                     "xn_full": xn_full.ap()})
    st.close()
    st = ExitStack(); emit_b1(nc, TW, st, io=io); st.close()
    st = ExitStack(); emit_b2a(nc, TOWN, st, io=io); st.close()

    with nc.Block() as block:
        @block.gpsimd
        def _(g):
            g.collective_compute("AllGather", ALU.bypass, replica_groups=_PAIRS,
                                 ins=[carry_t.ap().opt()],
                                 outs=[carry_gat.ap().opt()]).then_inc(ccsem, 1)
            g.wait_ge(ccsem, 2)

    st = ExitStack(); emit_b2b(nc, TOWN, st, io=io, mask=mask_sb[:, 0:1]); st.close()
    st = ExitStack(); emit_c(nc, TW, st, io=io); st.close()
    nc._ag_keep = gstack
    return nc


def _fused_nc():
    if "f" not in _CACHE:
        _CACHE["f"] = build_fused()
    return _CACHE["f"]


def kernel(**ins):
    f32 = np.float32
    ac = np.ascontiguousarray
    nc = _fused_nc()
    x = np.asarray(ins["L0_attn_x"], f32)

    bo = np.asarray(ins["L0_attn_bo"], f32)
    sg = np.asarray(ins["L0_ffn_s_gate"], f32)
    su = np.asarray(ins["L0_ffn_s_up"], f32)
    sd = np.asarray(ins["L0_ffn_s_down"], f32)
    winT = ac(np.asarray(ins["L1_attn_W_in"], f32).T)
    cw = np.asarray(ins["L1_attn_conv_w"], f32)
    cb = np.asarray(ins["L1_attn_conv_b"], f32)
    dtb = np.asarray(ins["L1_attn_dt_bias"], f32)
    alog = ac(np.asarray(ins["L1_attn_A_log"], f32).reshape(32, 1))
    dexp = ac(np.repeat(np.asarray(ins["L1_attn_D"], f32), 128).reshape(1, 4096))
    woutT = ac(np.asarray(ins["L1_attn_W_out"], f32).T)
    wg1p = perm_cols(np.asarray(ins["L1_ffn_w_gate"], f32))
    wu1p = perm_cols(np.asarray(ins["L1_ffn_w_up"], f32))
    wd1p = perm_rows(np.asarray(ins["L1_ffn_w_down"], f32))
    convw = ac(cw.reshape(33, 128, 4).transpose(1, 0, 2).reshape(128, 132))
    convb = ac(cb.reshape(33, 128).T)
    xTf = [ac(x[b].T) for b in range(4)]

    maps = []
    for c in range(8):
        b, e = c // 2, c % 2
        T0, E0 = 2048 * e, 1920 * e
        wcols = np.r_[T0:T0 + 2048, E0:E0 + 128]
        m = {"x": ac(x[b]), "xT": ac(xTf[b][:, wcols]),
             "offs": np.array([[T0, E0]], np.int32),
             "mask64": np.full((64, 1), float(e), f32),
             "wo_c": np.asarray(ins["L0_attn_wo_codes"], f32),
             "wo_s": np.asarray(ins["L0_attn_wo_scales"], f32),
             "bo_r": ac(bo.reshape(16, 128).T),
             "wg8": np.asarray(ins["L0_ffn_w_gate"], np.int8),
             "wu8": np.asarray(ins["L0_ffn_w_up"], np.int8),
             "wd8": np.asarray(ins["L0_ffn_w_down"], np.int8),
             "sg_r": ac(sg.reshape(64, 128).T),
             "su_r": ac(su.reshape(64, 128).T),
             "sd_r": ac(sd.reshape(16, 128).T),
             "winT": winT, "convw": convw, "convb": convb,
             "padmask": np.full((128, 1), float(e), f32),
             "dtb": ac(dtb.reshape(32, 1)),
             "A_log": alog, "D_exp": dexp, "woutT": woutT,
             "wg1p": wg1p, "wu1p": wu1p, "wd1p": wd1p}
        sl = slice(1024 * e, 1024 * (e + 1))
        for nm in "qkv":
            m["w" + nm] = ac(np.asarray(ins[f"L0_attn_w{nm}_codes"], f32)[:, sl])
            m["s" + nm] = ac(np.asarray(ins[f"L0_attn_w{nm}_scales"], f32)[:, sl])
            bb = np.asarray(ins[f"L0_attn_b{nm}"], f32)[sl]
            m["b" + nm] = ac(bb.reshape(8, 128).T)
        m["bvr"] = ac(np.asarray(ins["L0_attn_bv"], f32)[sl].reshape(1, 1024))
        maps.append(m)

    res = run_bass_kernel_spmd(nc, maps, core_ids=list(range(8))).results
    out = np.empty((4, 4096, 2048), f32)
    for c in range(8):
        b, e = c // 2, c % 2
        T0 = 2048 * e
        out[b, T0:T0 + 2048, :] = np.asarray(res[c]["x2"], f32).T
    return out


# version 03
# speedup vs baseline: 1.0827x; 1.0827x over previous
import numpy as np
from contextlib import ExitStack
import concourse.bass as bass
import concourse.mybir as mybir
from concourse.bass_utils import run_bass_kernel_spmd
import ml_dtypes

bf16 = ml_dtypes.bfloat16
F32 = mybir.dt.float32
BF16 = mybir.dt.bfloat16
I8 = mybir.dt.int8
I32 = mybir.dt.int32
AF = mybir.ActivationFunctionType
ALU = mybir.AluOpType
AX = mybir.AxisListType
_INLINE_N = [0]

RS128 = float(1.0 / np.sqrt(128.0))
SOFTMAX_C = 40.0

_INLINE_N = [0]
_NAME_N = [0]


def _u(nm):
    _NAME_N[0] += 1
    return f"{nm}_u{_NAME_N[0]}"


class Sem:
    def __init__(self, handle):
        self.h = handle
        self.n = 0

    def inc(self, ins, k=1):
        ins.then_inc(self.h, k)
        self.n += k
        return ins

    def inc_dma(self, ins):
        return self.inc(ins, 16)


class SemView:
    def __init__(self, sem):
        self.s = sem
        self.base = sem.n

    @property
    def h(self):
        return self.s.h

    @property
    def n(self):
        return self.s.n - self.base

    def inc(self, ins, k=1):
        self.s.inc(ins, k)
        return ins

    def inc_dma(self, ins):
        return self.inc(ins, 16)


def mk_sems(nc, stack, names):
    pool = getattr(nc, "_ag_sem_pool", None)
    if pool is None:
        out = {}
        for nm in names:
            h = stack.enter_context(nc.semaphore(nm))
            s = Sem(h)
            s.base = 0
            out[nm] = s
        return out
    gstack = nc._ag_sem_stack
    while len(pool) < len(names):
        i = len(pool)
        pool.append(Sem(gstack.enter_context(nc.semaphore(f"sp{i}"))))
    return {nm: SemView(pool[i]) for i, nm in enumerate(names)}


def wait_all(eng, *sems):
    for s in sems:
        if s.n > 0:
            eng.wait_ge(s.h, getattr(s, "base", 0) + s.n)


def inline_const(nc, name, arr):
    _INLINE_N[0] += 1
    name = f"{name}_{_INLINE_N[0]}"
    if arr.dtype == bf16:
        t = nc.inline_tensor(np.ascontiguousarray(arr.view(np.uint16)), name=name)
        return t, BF16
    t = nc.inline_tensor(np.ascontiguousarray(arr), name=name)
    return t, None


def load_const(eng, sb_ap, dram_t, cast):
    src = dram_t.ap()
    if cast is not None:
        src = src.bitcast(cast)
    return eng.dma_start(out=sb_ap, in_=src)


def dram_io(nc, io, name, shape, dtype, kind):
    if io is not None and name in io:
        return io[name]
    return nc.dram_tensor(name, shape, dtype, kind=kind).ap()


D = 2048
FEAT = 1024
NH = 8
NFC = 16
NI = 8192
NIC = 64
CONVD = 4224
NCC = 33
DIN = 4096
NST = 64

NGX = 8


D = 2048
FEAT = 1024
NH = 8
NFC = D // 128


def emit_a1(nc, S, stack, io=None):
    NTT = S // 128
    NTB = S // 512
    NKT = S // 128
    NV = NKT // 4
    NG = NTB * NKT
    sb = lambda *a: stack.enter_context(nc.sbuf_tensor(_u(a[0]), *a[1:]))
    ps = lambda *a: stack.enter_context(nc.psum_tensor(_u(a[0]), *a[1:]))

    x_in = dram_io(nc, io, "x", [S, D], F32, "ExternalInput")
    w_in = {m: nc.dram_tensor("w" + m, [D, FEAT], F32, kind="ExternalInput").ap() for m in "qkv"}
    s_in = {m: nc.dram_tensor("s" + m, [64, FEAT], F32, kind="ExternalInput").ap() for m in "qkv"}
    b_in = {m: nc.dram_tensor("b" + m, [128, NH], F32, kind="ExternalInput").ap() for m in "qkv"}
    bvr_in = dram_io(nc, io, "bvr", [1, FEAT], F32, "ExternalInput")
    attn_out = dram_io(nc, io, "attn", [FEAT, S], BF16, "ExternalOutput")
    xn_out = dram_io(nc, io, "xn", [D, S], BF16, "ExternalOutput")

    ones_1x128_bf, c1 = inline_const(nc, "ones_1x128", np.ones((1, 128), dtype=bf16))
    ones_128x1_bf, c2 = inline_const(nc, "ones_128x1", np.ones((128, 1), dtype=bf16))
    ones_1x128_f, c3 = inline_const(nc, "ones_1x128f", np.ones((1, 128), dtype=np.float32))
    ident_bf, c4 = inline_const(nc, "ident", np.eye(128, dtype=bf16))
    e4 = np.zeros((4, 128), dtype=bf16)
    for r in range(4):
        e4[r, 32 * r:32 * (r + 1)] = 1
    e4_bf, c5 = inline_const(nc, "e4sel", e4)

    x_tile = sb("x_tile", [128, D], F32)
    xn_tm = sb("xn_tm", [128, D], BF16)
    xn0 = [sb(f"xn0_{fi}", [128, S], BF16) for fi in range(NFC)]
    wdeq = {m: sb(f"wdeq_{m}", [128, 16 * 128], BF16) for m in "qkv"}
    codes_sb = [sb(f"codes{i}", [128, 512], F32) for i in range(2)]
    scales_sb = sb("scales", [4, 4 * 128], F32)
    scales_bf = sb("scales_bf", [4, 4 * 128], BF16)
    bias_sb = {m: sb(f"bias_{m}", [128, NH], F32) for m in "qkv"}
    bv_bf = sb("bv_bf", [1, FEAT], BF16)
    K_h = sb("K_h", [128, S], BF16)
    Q_h = sb("Q_h", [128, S], BF16)
    V_tm = sb("V_tm", [128, S], BF16)
    PT = [sb(f"PT{i}", [128, 512], BF16) for i in range(3)]
    sq_s = sb("sq_s", [128, 512], BF16)
    atmp = sb("atmp", [128, 512], F32)
    att_bf = [sb(f"att_bf{i}", [128, 512], BF16) for i in range(2)]
    stat = sb("stat", [128, 8], F32)
    mtmp = sb("mtmp", [1, 512], F32)
    sums_s = sb("sums_s", [1, 512], F32)
    sums_bf = sb("sums_bf", [1, 512], BF16)
    recip_s = sb("recip_s", [1, 512], F32)
    ksq_s = sb("ksq_s", [1, 512], F32)
    km = sb("km", [1, 6], F32)
    mng_bf = sb("mng_bf", [1, 2], BF16)
    bcol = sb("bcol", [128, 2], F32)
    ones1_bf = sb("ones1_bf", [1, 128], BF16)
    ones128_bf = sb("ones128_bf", [128, 1], BF16)
    ones1_f = sb("ones1_f", [1, 128], F32)
    ident_sb = sb("ident_sb", [128, 128], BF16)
    e4_sb = sb("e4_sb", [4, 128], BF16)
    fc = sb("fc", [128, 4], F32)

    p0 = ps("p0", [128, 512], F32)
    p1 = ps("p1", [128, 512], F32)
    p2 = ps("p2", [128, 512], F32)
    p3 = ps("p3", [128, 512], F32)
    pT = ps("pT", [128, 1024], BF16)
    p4 = ps("p4", [128, 512], F32)
    p5 = ps("p5", [128, 512], F32)
    p6 = ps("p6", [128, 512], F32)

    names = ["s_in0", "s_in1", "s_ld0", "s_ld1", "s_cw", "s_bvp", "s_fc",
             "s_sq", "s_rs", "s_stat", "s_rstd", "s_ap", "s_tr", "s_ev", "s_xout",
             "s_sb", "s_scv", "s_deq", "s_qkpe", "s_vpe", "s_qke", "s_ve",
             "s_qs", "s_qsm", "s_qse", "s_kmx", "s_kacc", "s_lnk", "s_mfin",
             "s_mexp", "s_bbf", "s_mng", "s_bcm",
             "s_sc", "s_pt", "s_av", "s_smv", "s_sbf", "s_bv", "s_evA",
             "s_rcl", "s_rcp", "s_rb", "s_nrm", "s_aout0", "s_aout1", "s_v", "s_a"]
    S_ = mk_sems(nc, stack, names)

    def W(eng, sem, v):
        if v > 0:
            eng.wait_ge(S_[sem].h, S_[sem].base + v)

    with nc.Block() as block:
        @block.sync
        def _(sync):
            S_["s_cw"].inc_dma(load_const(sync, ones1_bf[:], ones_1x128_bf, c1))
            S_["s_cw"].inc_dma(load_const(sync, ones128_bf[:], ones_128x1_bf, c2))
            S_["s_cw"].inc_dma(load_const(sync, ones1_f[:], ones_1x128_f, c3))
            S_["s_cw"].inc_dma(load_const(sync, ident_sb[:], ident_bf, c4))
            S_["s_cw"].inc_dma(load_const(sync, e4_sb[:], e4_bf, c5))
            for m in "qkv":
                S_["s_cw"].inc_dma(sync.dma_start(out=bias_sb[m][:], in_=b_in[m]))
            S_["s_cw"].inc_dma(sync.dma_start(out=xn_tm[0:1, :].bitcast(F32), in_=bvr_in))
            for t in range(NTT):
                W(sync, "s_ap", t)
                S_[f"s_in{t % 2}"].inc_dma(sync.dma_start(out=x_tile[:], in_=x_in[t * 128:(t + 1) * 128, :]))
            for h in range(NH):
                for gi in range(12):
                    m = "qkv"[gi // 4]
                    g = gi % 4
                    W(sync, "s_deq", 12 * h + gi - 1)
                    src = w_in[m][512 * g:512 * (g + 1), 128 * h:128 * (h + 1)].rearrange("(a p) c -> p a c", p=128)
                    S_[f"s_ld{gi % 2}"].inc_dma(sync.dma_start(
                        out=codes_sb[gi % 2][:].rearrange("p (a c) -> p a c", a=4), in_=src))
                    W(sync, "s_scv", 12 * h + gi)
                    ssrc = s_in[m][16 * g:16 * (g + 1), 128 * h:128 * (h + 1)].rearrange("(a r) c -> r a c", r=4)
                    S_[f"s_ld{gi % 2}"].inc_dma(sync.dma_start(
                        out=scales_sb[:].rearrange("r (a c) -> r a c", a=4), in_=ssrc))

        CW0 = 16 * 9
        ld_at = lambda h, gi: 32 * ((12 * h + gi) // 2 + 1)

        @block.scalar
        def _(scalar):
            for t in range(NTT):
                W(scalar, f"s_in{t % 2}", 16 * (t // 2 + 1))
                W(scalar, "s_stat", 4 * t)
                W(scalar, "s_ap", t)
                W(scalar, "s_tr", 2 * t)
                if t == 0:
                    W(scalar, "s_bvp", 1)
                    W(scalar, "s_fc", 1)
                S_["s_sq"].inc(scalar.activation(out=xn_tm[:], in_=x_tile[:],
                                                 func=AF.Square, accum_out=stat[:, 1:2]))
                W(scalar, "s_stat", 4 * (t + 1))
                S_["s_a"].inc(scalar.activation(out=stat[:, 6:7], in_=stat[:, 5:6], func=AF.Ln, bias=fc[:, 0:1]))
                W(scalar, "s_a", S_["s_a"].n)
                S_["s_rstd"].inc(scalar.activation(out=stat[:, 7:8], in_=stat[:, 6:7], func=AF.Exp, scale=fc[:, 1:2]), 2)
            for h in range(NH):
                for g in range(NV):
                    W(scalar, "s_vpe", NV * h + g + 1)
                    W(scalar, "s_av", NG * h)
                    S_["s_ve"].inc(scalar.activation(out=V_tm[:, g * 512:(g + 1) * 512], in_=p2[:], func=AF.Copy))
                for tb in range(NTB):
                    idx = 2 * NTB * h + tb
                    W(scalar, "s_qke", NTB * (h + 1))
                    W(scalar, "s_qsm", idx)
                    S_["s_qs"].inc(scalar.activation(out=sq_s[:], in_=K_h[:, tb * 512:(tb + 1) * 512],
                                                     func=AF.Square))
                    W(scalar, "s_qsm", idx + 1)
                    W(scalar, "s_kmx", NTB * h + tb)
                    S_["s_qse"].inc(scalar.activation(out=ksq_s[:], in_=p4[0:1, :], func=AF.Copy))
                W(scalar, "s_kacc", NTB * (h + 1))
                W(scalar, "s_mfin", NTB * h)
                S_["s_lnk"].inc(scalar.activation(out=km[:, 1:2], in_=km[:, 0:1], func=AF.Ln))
                def emit_sq(c):
                    idx = 2 * NTB * h + NTB + c
                    W(scalar, "s_qsm", idx)
                    S_["s_qs"].inc(scalar.activation(out=sq_s[:], in_=Q_h[:, c * 512:(c + 1) * 512],
                                                     func=AF.Square))
                def emit_chain_sc(c):
                    nc_ = NTB * h + c
                    idx = 2 * NTB * h + NTB + c
                    W(scalar, "s_qsm", idx + 1)
                    W(scalar, "s_mexp", nc_)
                    S_["s_qse"].inc(scalar.activation(out=mtmp[:], in_=p4[0:1, :], func=AF.Ln))
                    W(scalar, "s_mfin", nc_ + 1)
                    S_["s_mexp"].inc(scalar.activation(out=mtmp[:], in_=mtmp[:], func=AF.Exp, scale=fc[0:1, 3:4]))
                    W(scalar, "s_bcm", nc_ + 1)
                    W(scalar, "s_pt", NG * h + max(0, c - 1) * NKT)
                    S_["s_bbf"].inc(scalar.activation(out=bcol[:, c % 2:c % 2 + 1],
                                                      in_=p6[:, c % 2:c % 2 + 1], func=AF.Copy))
                for qb in range(NTB):
                    nqb = NTB * h + qb
                    if qb == 0:
                        emit_sq(0)
                        emit_chain_sc(0)
                    if qb + 1 < NTB:
                        emit_sq(qb + 1)
                    for k in range(NKT):
                        g = NG * h + qb * NKT + k
                        W(scalar, "s_sc", g + 1)
                        W(scalar, "s_av", g - 2)
                        if k == 0:
                            W(scalar, "s_bbf", nqb + 1)
                        pb = [p0, p1, p5][g % 3]
                        S_["s_pt"].inc(scalar.activation(out=PT[g % 3][:], in_=pb[:], func=AF.Exp,
                                                         bias=bcol[:, qb % 2:qb % 2 + 1]))
                    W(scalar, "s_av", NG * h + (qb + 1) * NKT)
                    W(scalar, "s_sbf", nqb)
                    W(scalar, "s_rcl", nqb)
                    if nqb == 0:
                        W(scalar, "s_bvp", 1)
                    S_["s_smv"].inc(scalar.activation(out=sums_s[:], in_=p3[0:1, :], func=AF.Copy))
                    W(scalar, "s_rcl", nqb + 1)
                    S_["s_a"].inc(scalar.activation(out=recip_s[:], in_=recip_s[:], func=AF.Ln))
                    W(scalar, "s_a", S_["s_a"].n)
                    S_["s_rcp"].inc(scalar.activation(out=recip_s[:], in_=recip_s[:], func=AF.Exp,
                                                      scale=fc[0:1, 2:3]))
                    W(scalar, "s_bv", nqb + 1)
                    W(scalar, "s_nrm", nqb)
                    S_["s_evA"].inc(scalar.activation(out=atmp[:], in_=p2[:], func=AF.Copy))
                    if qb + 1 < NTB:
                        emit_chain_sc(qb + 1)

        @block.vector
        def _(vector):
            vector.memset(fc[:, 0:1], 1e-5)
            vector.memset(fc[:, 1:2], -0.5)
            vector.memset(fc[:, 2:3], -1.0)
            S_["s_fc"].inc(vector.memset(fc[:, 3:4], 0.5))
            W(vector, "s_cw", CW0)
            S_["s_bvp"].inc(vector.tensor_copy(out=bv_bf[:], in_=xn_tm[0:1, :].bitcast(F32)))
            for t in range(NTT):
                W(vector, f"s_in{t % 2}", 16 * (t // 2 + 1))
                W(vector, "s_stat", 4 * t)
                S_["s_rs"].inc(vector.tensor_reduce(out=stat[:, 0:1], in_=x_tile[:], axis=AX.X, op=ALU.add))
                W(vector, "s_rs", S_["s_rs"].n)
                S_["s_v"].inc(vector.tensor_scalar_mul(out=stat[:, 2:3], in0=stat[:, 0:1], scalar1=1.0 / D))
                W(vector, "s_v", S_["s_v"].n)
                S_["s_v"].inc(vector.tensor_tensor(out=stat[:, 4:5], in0=stat[:, 2:3], in1=stat[:, 2:3], op=ALU.mult))
                W(vector, "s_v", S_["s_v"].n)
                W(vector, "s_sq", t + 1)
                S_["s_stat"].inc(vector.scalar_tensor_tensor(out=stat[:, 5:6], in0=stat[:, 1:2], scalar=1.0 / D,
                                                             in1=stat[:, 4:5], op0=ALU.mult, op1=ALU.subtract), 4)
                W(vector, "s_rstd", 2 * (t + 1))
                W(vector, "s_tr", 2 * t)
                S_["s_ap"].inc(vector.tensor_scalar(out=xn_tm[:], in0=x_tile[:],
                                                    scalar1=stat[:, 2:3], scalar2=stat[:, 7:8],
                                                    op0=ALU.subtract, op1=ALU.mult))
                for half in range(2):
                    W(vector, "s_tr", 2 * t + half + 1)
                    for j in range(8):
                        fi = half * 8 + j
                        ins = vector.tensor_copy(out=xn0[fi][:, t * 128:(t + 1) * 128],
                                                 in_=pT[:, j * 128:(j + 1) * 128])
                    S_["s_ev"].inc(ins)
            for h in range(NH):
                for gi in range(12):
                    m = "qkv"[gi // 4]
                    g = gi % 4
                    W(vector, f"s_ld{gi % 2}", ld_at(h, gi))
                    W(vector, "s_sb", 12 * h + gi)
                    S_["s_scv"].inc(vector.tensor_copy(out=scales_bf[:], in_=scales_sb[:]))
                    W(vector, "s_sb", 12 * h + gi + 1)
                    if gi == 0:
                        W(vector, "s_qkpe", NTB * h)
                        W(vector, "s_vpe", NV * h)
                    S_["s_deq"].inc(vector.tensor_tensor(out=wdeq[m][:, g * 512:(g + 1) * 512],
                                                         in0=codes_sb[gi % 2][:], in1=p0[:], op=ALU.mult))
                for tb in range(NTB):
                    W(vector, "s_qkpe", NTB * h + tb + 1)
                    W(vector, "s_sc", NG * h)
                    vector.tensor_scalar(out=Q_h[:, tb * 512:(tb + 1) * 512], in0=p0[:],
                                         scalar1=bias_sb["q"][:, h:h + 1], scalar2=RS128,
                                         op0=ALU.add, op1=ALU.mult)
                    S_["s_qke"].inc(vector.tensor_scalar(out=K_h[:, tb * 512:(tb + 1) * 512], in0=p1[:],
                                                         scalar1=bias_sb["k"][:, h:h + 1], scalar2=None,
                                                         op0=ALU.add))
                for tb in range(NTB):
                    W(vector, "s_qse", 2 * NTB * h + tb + 1)
                    W(vector, "s_kacc", NTB * h + tb)
                    S_["s_kmx"].inc(vector.tensor_reduce(out=km[:, 2:3], in_=ksq_s[:], axis=AX.X, op=ALU.max))
                    W(vector, "s_kmx", NTB * h + tb + 1)
                    if tb == 0:
                        W(vector, "s_lnk", h)
                        S_["s_kacc"].inc(vector.tensor_copy(out=km[:, 0:1], in_=km[:, 2:3]))
                    else:
                        S_["s_kacc"].inc(vector.tensor_tensor(out=km[:, 0:1], in0=km[:, 0:1],
                                                              in1=km[:, 2:3], op=ALU.max))
                def emit_chain_v(c):
                    nc_ = NTB * h + c
                    W(vector, "s_qse", 2 * NTB * h + NTB + c + 1)
                    W(vector, "s_lnk", h + 1)
                    S_["s_mfin"].inc(vector.tensor_scalar_add(out=mtmp[:], in0=mtmp[:], scalar1=km[:, 1:2]))
                    W(vector, "s_mexp", nc_ + 1)
                    W(vector, "s_mng", nc_)
                    S_["s_v"].inc(vector.tensor_reduce(out=km[:, 3:4], in_=mtmp[:], axis=AX.X, op=ALU.max))
                    W(vector, "s_v", S_["s_v"].n)
                    W(vector, "s_bcm", nc_ - 1)
                    S_["s_mng"].inc(vector.tensor_scalar(out=mng_bf[0:1, c % 2:c % 2 + 1], in0=km[:, 3:4],
                                                         scalar1=-1.0, scalar2=SOFTMAX_C,
                                                         op0=ALU.mult, op1=ALU.add))
                for qb in range(NTB):
                    nqb = NTB * h + qb
                    if qb == 0:
                        emit_chain_v(0)
                    W(vector, "s_smv", nqb + 1)
                    W(vector, "s_bv", nqb)
                    S_["s_sbf"].inc(vector.tensor_copy(out=sums_bf[:], in_=sums_s[:]))
                    W(vector, "s_rb", nqb)
                    S_["s_rcl"].inc(vector.tensor_scalar_max(out=recip_s[:], in0=sums_s[:], scalar1=1e-30))
                    W(vector, "s_evA", nqb + 1)
                    W(vector, "s_rb", nqb + 1)
                    W(vector, f"s_aout{nqb % 2}", 16 * (nqb // 2))
                    S_["s_nrm"].inc(vector.tensor_tensor(out=att_bf[nqb % 2][:], in0=atmp[:], in1=p3[:], op=ALU.mult))
                    if qb + 1 < NTB:
                        emit_chain_v(qb + 1)

        @block.tensor
        def _(tensor):
            W(tensor, "s_cw", CW0)
            for t in range(NTT):
                W(tensor, "s_ap", t + 1)
                for half in range(2):
                    W(tensor, "s_ev", 2 * t + half)
                    for j in range(8):
                        fi = half * 8 + j
                        ins = tensor.transpose(pT[:, j * 128:(j + 1) * 128],
                                               xn_tm[:, fi * 128:(fi + 1) * 128], ident_sb[:])
                    S_["s_tr"].inc(ins)
            for h in range(NH):
                for gi in range(12):
                    W(tensor, "s_scv", 12 * h + gi + 1)
                    W(tensor, "s_deq", 12 * h + gi)
                    if gi == 0:
                        W(tensor, "s_pt", NG * h)
                    for j in range(4):
                        ins = tensor.matmul(p0[:, j * 128:(j + 1) * 128], e4_sb[:],
                                            scales_bf[:, j * 128:(j + 1) * 128],
                                            start=True, stop=True)
                    S_["s_sb"].inc(ins)
                W(tensor, "s_deq", 12 * (h + 1))
                if h == 0:
                    W(tensor, "s_ev", 2 * NTT)
                for tb in range(NTB):
                    W(tensor, "s_qke", NTB * h + tb)
                    for fi in range(NFC):
                        tensor.matmul(p0[:], wdeq["q"][:, fi * 128:(fi + 1) * 128],
                                      xn0[fi][:, tb * 512:(tb + 1) * 512],
                                      start=(fi == 0), stop=(fi == NFC - 1))
                    for fi in range(NFC):
                        ins = tensor.matmul(p1[:], wdeq["k"][:, fi * 128:(fi + 1) * 128],
                                            xn0[fi][:, tb * 512:(tb + 1) * 512],
                                            start=(fi == 0), stop=(fi == NFC - 1))
                    S_["s_qkpe"].inc(ins)
                for g in range(NV):
                    W(tensor, "s_ve", NV * h + g)
                    W(tensor, "s_evA", NTB * h)
                    for jj in range(4):
                        tt = 4 * g + jj
                        for fi in range(NFC):
                            ins = tensor.matmul(p2[:, jj * 128:(jj + 1) * 128],
                                                xn0[fi][:, tt * 128:(tt + 1) * 128],
                                                wdeq["v"][:, fi * 128:(fi + 1) * 128],
                                                start=(fi == 0), stop=(fi == NFC - 1))
                    S_["s_vpe"].inc(ins)
                for tb in range(NTB):
                    idx = 2 * NTB * h + tb
                    W(tensor, "s_qs", idx + 1)
                    W(tensor, "s_qse", idx)
                    S_["s_qsm"].inc(tensor.matmul(p4[0:1, :], ones128_bf[:], sq_s[:],
                                                  start=True, stop=True))
                W(tensor, "s_qke", NTB * (h + 1))
                W(tensor, "s_ve", NV * (h + 1))
                W(tensor, "s_bvp", 1)
                def emit_scores(qb, k):
                    g = NG * h + qb * NKT + k
                    W(tensor, "s_pt", g - 2)
                    pb = [p0, p1, p5][g % 3]
                    S_["s_sc"].inc(tensor.matmul(pb[:], K_h[:, k * 128:(k + 1) * 128],
                                                 Q_h[:, qb * 512:(qb + 1) * 512], start=True, stop=True))
                for qb in range(NTB):
                    nqb = NTB * h + qb
                    if qb == 0:
                        idx0 = 2 * NTB * h + NTB
                        W(tensor, "s_qs", idx0 + 1)
                        W(tensor, "s_qse", idx0)
                        S_["s_qsm"].inc(tensor.matmul(p4[0:1, :], ones128_bf[:], sq_s[:],
                                                      start=True, stop=True))
                        emit_scores(0, 0)
                        emit_scores(0, 1)
                        emit_scores(0, 2)
                        W(tensor, "s_mng", NTB * h + 1)
                        W(tensor, "s_bbf", max(0, NTB * h - 1))
                        S_["s_bcm"].inc(tensor.matmul(p6[:, 0:1], ones1_bf[:],
                                                      mng_bf[0:1, 0:1], start=True, stop=True))
                    for k in range(NKT):
                        g = NG * h + qb * NKT + k
                        if k + 3 < NKT:
                            emit_scores(qb, k + 3)
                        if k == 0:
                            W(tensor, "s_evA", nqb)
                            W(tensor, "s_nrm", nqb)
                        W(tensor, "s_pt", g + 1)
                        tensor.matmul(p2[:], V_tm[:, k * 128:(k + 1) * 128], PT[g % 3][:],
                                      start=(k == 0), stop=False)
                        S_["s_av"].inc(tensor.matmul(p3[0:1, :], ones128_bf[:], PT[g % 3][:],
                                                     start=(k == 0), stop=(k == NKT - 1)))
                    if qb + 1 < NTB:
                        idx = 2 * NTB * h + NTB + qb + 1
                        W(tensor, "s_qs", idx + 1)
                        W(tensor, "s_qse", idx)
                        S_["s_qsm"].inc(tensor.matmul(p4[0:1, :], ones128_bf[:], sq_s[:],
                                                      start=True, stop=True))
                    W(tensor, "s_sbf", nqb + 1)
                    S_["s_bv"].inc(tensor.matmul(p2[:], bv_bf[0:1, h * 128:(h + 1) * 128],
                                                 sums_bf[:], start=False, stop=True))
                    W(tensor, "s_rcp", nqb + 1)
                    W(tensor, "s_smv", nqb + 1)
                    S_["s_rb"].inc(tensor.matmul(p3[:], ones1_f[:], recip_s[:],
                                                 start=True, stop=True))
                    if qb + 1 < NTB:
                        emit_scores(qb + 1, 0)
                        emit_scores(qb + 1, 1)
                        emit_scores(qb + 1, 2)
                        c = qb + 1
                        W(tensor, "s_mng", NTB * h + c + 1)
                        W(tensor, "s_bbf", max(0, NTB * h + c - 1))
                        S_["s_bcm"].inc(tensor.matmul(p6[:, c % 2:c % 2 + 1], ones1_bf[:],
                                                      mng_bf[0:1, c % 2:c % 2 + 1],
                                                      start=True, stop=True))

        @block.gpsimd
        def _(gpsimd):
            for fi in range(NFC):
                gpsimd.wait_ge(S_["s_ev"].h, S_["s_ev"].base + 2 * NTT)
                S_["s_xout"].inc_dma(gpsimd.dma_start(out=xn_out[fi * 128:(fi + 1) * 128, :], in_=xn0[fi][:]))
            for h in range(NH):
                for qb in range(NTB):
                    nqb = NTB * h + qb
                    W(gpsimd, "s_nrm", nqb + 1)
                    S_[f"s_aout{nqb % 2}"].inc_dma(gpsimd.dma_start(
                        out=attn_out[h * 128:(h + 1) * 128, qb * 512:(qb + 1) * 512],
                        in_=att_bf[nqb % 2][:]))

        @block.sync
        def _(sync):
            sync.wait_ge(S_["s_aout0"].h, S_["s_aout0"].base + S_["s_aout0"].n)
            sync.wait_ge(S_["s_aout1"].h, S_["s_aout1"].base + S_["s_aout1"].n)
            sync.wait_ge(S_["s_xout"].h, S_["s_xout"].base + S_["s_xout"].n)

    assert S_["s_sc"].n == NH * NG and S_["s_pt"].n == NH * NG and S_["s_av"].n == NH * NG
    assert S_["s_qse"].n == 2 * NTB * NH and S_["s_deq"].n == 12 * NH
    return nc



D = 2048
NFC = 16
NI = 8192
NIC = 64


def emit_mid(nc, TW, stack, io=None, win=None):
    NTB5 = [(i * 512, min(512, TW - i * 512)) for i in range((TW + 511) // 512)]
    NTB9 = [(i * 256, min(256, TW - i * 256)) for i in range((TW + 255) // 256)]
    NT5, NT9 = len(NTB5), len(NTB9)
    NTT = TW // 128

    xT_in = dram_io(nc, io, "xT", [D, TW], F32, "ExternalInput")
    attn_in = None if win is not None else dram_io(nc, io, "attn", [D, TW], BF16, "ExternalInput")
    xn0_in = None if win is not None else dram_io(nc, io, "xn0", [D, TW], BF16, "ExternalInput")
    wo_c = dram_io(nc, io, "wo_c", [D, D], F32, "ExternalInput")
    wo_s = dram_io(nc, io, "wo_s", [64, D], F32, "ExternalInput")
    bo_r = dram_io(nc, io, "bo_r", [128, 16], F32, "ExternalInput")
    wg8 = dram_io(nc, io, "wg8", [D, NI], I8, "ExternalInput")
    wu8 = dram_io(nc, io, "wu8", [D, NI], I8, "ExternalInput")
    wd8 = dram_io(nc, io, "wd8", [NI, D], I8, "ExternalInput")
    sg_r = dram_io(nc, io, "sg_r", [128, 64], F32, "ExternalInput")
    su_r = dram_io(nc, io, "su_r", [128, 64], F32, "ExternalInput")
    sd_r = dram_io(nc, io, "sd_r", [128, 16], F32, "ExternalInput")
    x1_out = dram_io(nc, io, "x1", [D, TW], F32, "ExternalOutput")
    xn1_out = dram_io(nc, io, "xn1", [D, TW], BF16, "ExternalOutput")
    x1p = dram_io(nc, io, "x1p", [D, TW], F32, "Internal")
    h0 = dram_io(nc, io, "h0", [NI, TW], BF16, "Internal")

    e4 = np.zeros((4, 128), dtype=bf16)
    for r in range(4):
        e4[r, 32 * r:32 * (r + 1)] = 1
    e4_bf, c5 = inline_const(nc, "e4sel", e4)
    ident_bf, c4 = inline_const(nc, "ident", np.eye(128, dtype=bf16))
    ident_f32, c6 = inline_const(nc, "identf", np.eye(128, dtype=np.float32))

    def W(S_, eng, sem, v):
        if v > 0:
            eng.wait_ge(S_[sem].h, S_[sem].base + v)

    st1 = ExitStack()
    sb1 = lambda *a: st1.enter_context(nc.sbuf_tensor(_u(a[0]), *a[1:]))
    ps1 = lambda *a: st1.enter_context(nc.psum_tensor(_u(a[0]), *a[1:]))
    attn_big = sb1("attn_big", [128, NFC * TW], BF16)
    attn_sb = lambda fc, t0, tl: attn_big[:, fc * TW + t0: fc * TW + t0 + tl]
    codes_sb = [sb1(f"codes{i}", [128, 512], F32) for i in range(2)]
    scales_sb = sb1("scales", [4, 4 * 128], F32)
    scales_bf = sb1("scales_bf", [4, 4 * 128], BF16)
    wo_bf = [sb1(f"wo_bf{i}", [128, 16 * 128], BF16) for i in range(2)]
    bo_sb = sb1("bo_sb", [128, 16], F32)
    e4_sb = sb1("e4_sb", [4, 128], BF16)
    xT_t = [sb1(f"xT_t{i}", [128, 512], F32) for i in range(2)]
    y0_s = sb1("y0_s", [128, 512], F32)
    x1_t = [sb1(f"x1_t{i}", [128, 512], F32) for i in range(2)]
    pd = ps1("pd", [128, 512], F32)
    pm = [ps1(f"pm{i}", [128, 512], F32) for i in range(2)]
    S1 = mk_sems(nc, stack, ["m_cw", "m_at", "m_ld0", "m_ld1", "m_scv", "m_sb", "m_deq",
                             "m_xt0", "m_xt1", "m_mm", "m_add", "m_out0", "m_out1", "m_v"])

    with nc.Block() as block:
        @block.sync
        def _(sync):
            S1["m_cw"].inc_dma(load_const(sync, e4_sb[:], e4_bf, c5))
            S1["m_cw"].inc_dma(sync.dma_start(out=bo_sb[:], in_=bo_r))
            if win is None:
                for fc in range(NFC):
                    S1["m_at"].inc_dma(sync.dma_start(out=attn_sb(fc, 0, TW),
                                                      in_=attn_in[fc * 128:(fc + 1) * 128, :]))
            else:
                t0v = sync.value_load(in_tensor=win["offs"][0:1, 0:1])
                ag = win["attn_full"]
                for fc in range(NFC):
                    S1["m_at"].inc_dma(sync.dma_start(out=attn_sb(fc, 0, 2048),
                                                      in_=ag[fc * 128:(fc + 1) * 128, bass.ds(t0v, 2048)]))
                    S1["m_at"].inc_dma(sync.dma_start(out=attn_sb(fc, 2048, TW - 2048),
                                                      in_=ag[fc * 128:(fc + 1) * 128, 1920:2048]))
            for oc in range(16):
                for g in range(4):
                    gi = 4 * oc + g
                    W(S1, sync, "m_deq", gi - 1)
                    src = wo_c[512 * g:512 * (g + 1), 128 * oc:128 * (oc + 1)].rearrange("(a p) c -> p a c", p=128)
                    S1[f"m_ld{gi % 2}"].inc_dma(sync.dma_start(
                        out=codes_sb[gi % 2][:].rearrange("p (a c) -> p a c", a=4), in_=src))
                    W(S1, sync, "m_scv", gi)
                    ssrc = wo_s[16 * g:16 * (g + 1), 128 * oc:128 * (oc + 1)].rearrange("(a r) c -> r a c", r=4)
                    S1[f"m_ld{gi % 2}"].inc_dma(sync.dma_start(
                        out=scales_sb[:].rearrange("r (a c) -> r a c", a=4), in_=ssrc))

        @block.gpsimd
        def _(gpsimd):
            for oc in range(16):
                for it in range(NT5):
                    t0, tl = NTB5[it]
                    i = oc * NT5 + it
                    W(S1, gpsimd, "m_add", i - 1)
                    S1[f"m_xt{i % 2}"].inc_dma(gpsimd.dma_start(
                        out=xT_t[i % 2][:, :tl], in_=xT_in[oc * 128:(oc + 1) * 128, t0:t0 + tl]))

        @block.vector
        def _(vector):
            W(S1, vector, "m_cw", 32)
            for oc in range(16):
                for g in range(4):
                    gi = 4 * oc + g
                    W(S1, vector, f"m_ld{gi % 2}", 32 * (gi // 2 + 1))
                    W(S1, vector, "m_sb", gi)
                    S1["m_scv"].inc(vector.tensor_copy(out=scales_bf[:], in_=scales_sb[:]))
                    W(S1, vector, "m_sb", gi + 1)
                    if g == 0 and oc >= 2:
                        W(S1, vector, "m_mm", (oc - 1) * NT5)
                    S1["m_deq"].inc(vector.tensor_tensor(out=wo_bf[oc % 2][:, g * 512:(g + 1) * 512],
                                                         in0=codes_sb[gi % 2][:], in1=pd[:], op=ALU.mult))
                for it in range(NT5):
                    t0, tl = NTB5[it]
                    i = oc * NT5 + it
                    W(S1, vector, "m_mm", i + 1)
                    W(S1, vector, f"m_xt{i % 2}", 16 * (i // 2 + 1))
                    W(S1, vector, "m_add", i)
                    S1["m_v"].inc(vector.scalar_tensor_tensor(out=y0_s[:, :tl], in0=pm[i % 2][:, :tl],
                                                              scalar=1.0, in1=xT_t[i % 2][:, :tl],
                                                              op0=ALU.mult, op1=ALU.add))
                    W(S1, vector, "m_v", S1["m_v"].n)
                    W(S1, vector, f"m_out{i % 2}", 16 * (i // 2))
                    S1["m_add"].inc(vector.tensor_scalar_add(out=x1_t[i % 2][:, :tl], in0=y0_s[:, :tl],
                                                             scalar1=bo_sb[:, oc:oc + 1]))

        @block.tensor
        def _(tensor):
            W(S1, tensor, "m_cw", 32)
            for oc in range(16):
                for g in range(4):
                    gi = 4 * oc + g
                    W(S1, tensor, "m_scv", gi + 1)
                    W(S1, tensor, "m_deq", gi)
                    for j in range(4):
                        ins = tensor.matmul(pd[:, j * 128:(j + 1) * 128], e4_sb[:],
                                            scales_bf[:, j * 128:(j + 1) * 128], start=True, stop=True)
                    S1["m_sb"].inc(ins)
                W(S1, tensor, "m_deq", 4 * (oc + 1))
                if oc == 0:
                    W(S1, tensor, "m_at", (32 if win is not None else 16) * NFC)
                for it in range(NT5):
                    t0, tl = NTB5[it]
                    i = oc * NT5 + it
                    W(S1, tensor, "m_v", i - 1)
                    for fcx in range(NFC):
                        ins = tensor.matmul(pm[i % 2][:, :tl], wo_bf[oc % 2][:, fcx * 128:(fcx + 1) * 128],
                                            attn_sb(fcx, t0, tl),
                                            start=(fcx == 0), stop=(fcx == NFC - 1))
                    S1["m_mm"].inc(ins)

        @block.scalar
        def _(scalar):
            for oc in range(16):
                for it in range(NT5):
                    t0, tl = NTB5[it]
                    i = oc * NT5 + it
                    W(S1, scalar, "m_add", i + 1)
                    S1[f"m_out{i % 2}"].inc_dma(scalar.dma_start(
                        out=x1p[oc * 128:(oc + 1) * 128, t0:t0 + tl], in_=x1_t[i % 2][:, :tl]))

        @block.sync
        def _(sync):
            sync.wait_ge(S1["m_out0"].h, S1["m_out0"].base + S1["m_out0"].n)
            sync.wait_ge(S1["m_out1"].h, S1["m_out1"].base + S1["m_out1"].n)
    st1.close()

    st2 = ExitStack()
    sb2 = lambda *a: st2.enter_context(nc.sbuf_tensor(_u(a[0]), *a[1:]))
    ps2 = lambda *a: st2.enter_context(nc.psum_tensor(_u(a[0]), *a[1:]))
    xn0_big = sb2("xn0_big", [128, NFC * TW], BF16)
    xn0_sb = lambda fc, t0, tl: xn0_big[:, fc * TW + t0: fc * TW + t0 + tl]
    wg_i8 = [sb2(f"wg_i8{i}", [128, 16 * 512], I8) for i in range(2)]
    wu_i8 = [sb2(f"wu_i8{i}", [128, 16 * 512], I8) for i in range(2)]
    wg_bf = [sb2(f"wg_bf{i}", [128, 16 * 512], BF16) for i in range(2)]
    wu_bf = [sb2(f"wu_bf{i}", [128, 16 * 512], BF16) for i in range(2)]
    sg_sb = sb2("sg_sb", [128, 64], F32)
    su_sb = sb2("su_sb", [128, 64], F32)
    g_s = sb2("g_s", [128, 512], F32)
    u_s = sb2("u_s", [128, 512], F32)
    h_t = [sb2(f"h_t{i}", [128, 512], BF16) for i in range(2)]
    pg = [ps2(f"pg{i}", [128, 512], F32) for i in range(2)]
    pu = [ps2(f"pu{i}", [128, 512], F32) for i in range(2)]
    S2 = mk_sems(nc, stack, ["f_cw", "f_xn", "f_w0", "f_w1", "f_cv", "f_gmm", "f_umm",
                             "f_ge", "f_ue", "f_h", "f_o0", "f_o1"])

    with nc.Block() as block:
        @block.sync
        def _(sync):
            S2["f_cw"].inc_dma(sync.dma_start(out=sg_sb[:], in_=sg_r))
            S2["f_cw"].inc_dma(sync.dma_start(out=su_sb[:], in_=su_r))
            if win is None:
                for fc in range(NFC):
                    S2["f_xn"].inc_dma(sync.dma_start(out=xn0_sb(fc, 0, TW),
                                                      in_=xn0_in[fc * 128:(fc + 1) * 128, :]))
            for ig in range(16):
                W(S2, sync, "f_cv", 2 * (ig - 1))
                src = wg8[:, 512 * ig:512 * (ig + 1)].rearrange("(a p) c -> p a c", p=128)
                S2[f"f_w{ig % 2}"].inc_dma(sync.dma_start(
                    out=wg_i8[ig % 2][:].rearrange("p (a c) -> p a c", a=16), in_=src))
                src = wu8[:, 512 * ig:512 * (ig + 1)].rearrange("(a p) c -> p a c", p=128)
                S2[f"f_w{ig % 2}"].inc_dma(sync.dma_start(
                    out=wu_i8[ig % 2][:].rearrange("p (a c) -> p a c", a=16), in_=src))

        @block.vector
        def _(vector):
            W(S2, vector, "f_cw", 32)
            for ig in range(16):
                W(S2, vector, f"f_w{ig % 2}", 32 * (ig // 2 + 1))
                if ig >= 2:
                    W(S2, vector, "f_gmm", (4 * (ig - 1)) * NT5)
                    W(S2, vector, "f_umm", (4 * (ig - 1)) * NT5)
                S2["f_cv"].inc(vector.tensor_copy(out=wg_bf[ig % 2][:], in_=wg_i8[ig % 2][:]))
                S2["f_cv"].inc(vector.tensor_copy(out=wu_bf[ig % 2][:], in_=wu_i8[ig % 2][:]))
                for ii in range(4):
                    ic = 4 * ig + ii
                    for it in range(NT5):
                        t0, tl = NTB5[it]
                        i = ic * NT5 + it
                        W(S2, vector, "f_umm", i + 1)
                        W(S2, vector, "f_h", i)
                        S2["f_ue"].inc(vector.tensor_scalar(out=u_s[:, :tl], in0=pu[i % 2][:, :tl],
                                                            scalar1=su_sb[:, ic:ic + 1], scalar2=None,
                                                            op0=ALU.mult))
                        W(S2, vector, "f_ge", i + 1)
                        W(S2, vector, "f_ue", S2["f_ue"].n)
                        W(S2, vector, f"f_o{i % 2}", 16 * (i // 2))
                        S2["f_h"].inc(vector.tensor_tensor(out=h_t[i % 2][:, :tl], in0=g_s[:, :tl],
                                                           in1=u_s[:, :tl], op=ALU.mult))

        @block.scalar
        def _(scalar):
            if win is not None:
                t0v2 = scalar.value_load(in_tensor=win["offs"][0:1, 0:1])
                xg = win["xn_full"]
                for fc in range(NFC):
                    S2["f_xn"].inc_dma(scalar.dma_start(out=xn0_sb(fc, 0, 2048),
                                                        in_=xg[fc * 128:(fc + 1) * 128, bass.ds(t0v2, 2048)]))
                    S2["f_xn"].inc_dma(scalar.dma_start(out=xn0_sb(fc, 2048, TW - 2048),
                                                        in_=xg[fc * 128:(fc + 1) * 128, 1920:2048]))
            W(S2, scalar, "f_cw", 32)
            for ic in range(NIC):
                for it in range(NT5):
                    t0, tl = NTB5[it]
                    i = ic * NT5 + it
                    W(S2, scalar, "f_gmm", i + 1)
                    W(S2, scalar, "f_h", i)
                    S2["f_ge"].inc(scalar.activation(out=g_s[:, :tl], in_=pg[i % 2][:, :tl],
                                                     func=AF.Silu, scale=sg_sb[:, ic:ic + 1]))

        @block.tensor
        def _(tensor):
            W(S2, tensor, "f_cw", 32)
            W(S2, tensor, "f_xn", (32 if win is not None else 16) * NFC)
            for ic in range(NIC):
                ig, ii = ic // 4, ic % 4
                W(S2, tensor, "f_cv", 2 * (ig + 1))
                for it in range(NT5):
                    t0, tl = NTB5[it]
                    i = ic * NT5 + it
                    W(S2, tensor, "f_ge", i - 1)
                    for fcx in range(NFC):
                        ins = tensor.matmul(pg[i % 2][:, :tl],
                                            wg_bf[ig % 2][:, fcx * 512 + ii * 128: fcx * 512 + (ii + 1) * 128],
                                            xn0_sb(fcx, t0, tl),
                                            start=(fcx == 0), stop=(fcx == NFC - 1))
                    S2["f_gmm"].inc(ins)
                    W(S2, tensor, "f_ue", i - 1)
                    for fcx in range(NFC):
                        ins = tensor.matmul(pu[i % 2][:, :tl],
                                            wu_bf[ig % 2][:, fcx * 512 + ii * 128: fcx * 512 + (ii + 1) * 128],
                                            xn0_sb(fcx, t0, tl),
                                            start=(fcx == 0), stop=(fcx == NFC - 1))
                    S2["f_umm"].inc(ins)

        @block.gpsimd
        def _(gpsimd):
            for ic in range(NIC):
                for it in range(NT5):
                    t0, tl = NTB5[it]
                    i = ic * NT5 + it
                    W(S2, gpsimd, "f_h", i + 1)
                    S2[f"f_o{i % 2}"].inc_dma(gpsimd.dma_start(
                        out=h0[ic * 128:(ic + 1) * 128, t0:t0 + tl], in_=h_t[i % 2][:, :tl]))

        @block.sync
        def _(sync):
            sync.wait_ge(S2["f_o0"].h, S2["f_o0"].base + S2["f_o0"].n)
            sync.wait_ge(S2["f_o1"].h, S2["f_o1"].base + S2["f_o1"].n)
    st2.close()

    st3 = ExitStack()
    sb3 = lambda *a: st3.enter_context(nc.sbuf_tensor(_u(a[0]), *a[1:]))
    ps3 = lambda *a: st3.enter_context(nc.psum_tensor(_u(a[0]), *a[1:]))
    wd_i8 = sb3("wd_i8", [128, 64 * 512], I8)
    wd_bf = sb3("wd_bf", [128, 64 * 512], BF16)
    h_sb = [sb3(f"h_sb{i}", [128, 64 * 256], BF16) for i in range(2)]
    sd_sb = sb3("sd_sb", [128, 16], F32)
    x1p_t = [sb3(f"x1p_t{i}", [128, 256], F32) for i in range(2)]
    y_s = sb3("y_s", [128, 256], F32)
    x1f = [sb3(f"x1f{i}", [128, 256], F32) for i in range(2)]
    pb8 = [ps3(f"pb8_{i}", [128, 256], F32) for i in range(8)]
    S3 = mk_sems(nc, stack, ["d_cw", "d_w", "d_cv", "d_h0", "d_h1", "d_mm", "d_xp0", "d_xp1",
                             "d_add", "d_o0", "d_o1", "d_v"])
    hv = [h_sb[i][:].rearrange("p (a c) -> p a c", a=64) for i in range(2)]

    with nc.Block() as block:
        @block.sync
        def _(sync):
            S3["d_cw"].inc_dma(sync.dma_start(out=sd_sb[:], in_=sd_r))
            for og in range(4):
                W(S3, sync, "d_cv", og)
                src = wd8[:, 512 * og:512 * (og + 1)].rearrange("(a p) c -> p a c", p=128)
                S3["d_w"].inc_dma(sync.dma_start(
                    out=wd_i8[:].rearrange("p (a c) -> p a c", a=64), in_=src))
                for it in range(NT9):
                    t0, tl = NTB9[it]
                    i5 = og * NT9 + it
                    W(S3, sync, "d_mm", 4 * (i5 - 1))
                    src = h0[:, t0:t0 + tl].rearrange("(a p) c -> p a c", p=128)
                    S3[f"d_h{i5 % 2}"].inc_dma(sync.dma_start(out=hv[i5 % 2][:, :, :tl], in_=src))

        @block.vector
        def _(vector):
            W(S3, vector, "d_cw", 16)
            for og in range(4):
                W(S3, vector, "d_w", 16 * (og + 1))
                if og >= 1:
                    W(S3, vector, "d_mm", 4 * og * NT9)
                S3["d_cv"].inc(vector.tensor_copy(out=wd_bf[:], in_=wd_i8[:]))
                for it in range(NT9):
                    t0, tl = NTB9[it]
                    for j in range(4):
                        oc = 4 * og + j
                        i = (og * NT9 + it) * 4 + j
                        W(S3, vector, "d_mm", i + 1)
                        W(S3, vector, "d_add", i)
                        S3["d_v"].inc(vector.tensor_scalar(out=y_s[:, :tl], in0=pb8[(it % 2) * 4 + j][:, :tl],
                                                           scalar1=sd_sb[:, oc:oc + 1], scalar2=None,
                                                           op0=ALU.mult))
                        W(S3, vector, "d_v", S3["d_v"].n)
                        W(S3, vector, f"d_xp{i % 2}", 16 * (i // 2 + 1))
                        W(S3, vector, f"d_o{i % 2}", 16 * (i // 2))
                        S3["d_add"].inc(vector.tensor_tensor(out=x1f[i % 2][:, :tl], in0=y_s[:, :tl],
                                                             in1=x1p_t[i % 2][:, :tl], op=ALU.add))

        @block.gpsimd
        def _(gpsimd):
            for og in range(4):
                for it in range(NT9):
                    t0, tl = NTB9[it]
                    for j in range(4):
                        oc = 4 * og + j
                        i = (og * NT9 + it) * 4 + j
                        W(S3, gpsimd, "d_add", i - 1)
                        S3[f"d_xp{i % 2}"].inc_dma(gpsimd.dma_start(
                            out=x1p_t[i % 2][:, :tl], in_=x1p[oc * 128:(oc + 1) * 128, t0:t0 + tl]))

        @block.scalar
        def _(scalar):
            for og in range(4):
                for it in range(NT9):
                    t0, tl = NTB9[it]
                    for j in range(4):
                        oc = 4 * og + j
                        i = (og * NT9 + it) * 4 + j
                        W(S3, scalar, "d_add", i + 1)
                        S3[f"d_o{i % 2}"].inc_dma(scalar.dma_start(
                            out=x1_out[oc * 128:(oc + 1) * 128, t0:t0 + tl], in_=x1f[i % 2][:, :tl]))

        @block.tensor
        def _(tensor):
            for og in range(4):
                W(S3, tensor, "d_cv", og + 1)
                for it in range(NT9):
                    t0, tl = NTB9[it]
                    i5 = og * NT9 + it
                    W(S3, tensor, f"d_h{i5 % 2}", 16 * (i5 // 2 + 1))
                    if i5 >= 2:
                        W(S3, tensor, "d_v", 4 * (i5 - 1))
                    for ic in range(64):
                        for j in range(4):
                            bank = (it % 2) * 4 + j
                            ins = tensor.matmul(pb8[bank][:, :tl],
                                                wd_bf[:, ic * 512 + j * 128:ic * 512 + (j + 1) * 128],
                                                hv[i5 % 2][:, ic, :tl],
                                                start=(ic == 0), stop=(ic == 63))
                            if ic == 63:
                                S3["d_mm"].inc(ins)

        @block.sync
        def _(sync):
            sync.wait_ge(S3["d_o0"].h, S3["d_o0"].base + S3["d_o0"].n)
            sync.wait_ge(S3["d_o1"].h, S3["d_o1"].base + S3["d_o1"].n)
    st3.close()

    st4 = ExitStack()
    sb4 = lambda *a: st4.enter_context(nc.sbuf_tensor(_u(a[0]), *a[1:]))
    ps4 = lambda *a: st4.enter_context(nc.psum_tensor(_u(a[0]), *a[1:]))
    x1s = [sb4(f"x1s{i}", [128, 16 * 128], F32) for i in range(2)]
    xtm = sb4("xtm", [128, D], F32)
    xnt = sb4("xnt", [128, D], BF16)
    xnf = [sb4(f"xnf{i}", [128, D], BF16) for i in range(2)]
    stat = sb4("stat", [128, 8], F32)
    sqs = sb4("sqs", [128, D], BF16)
    ident_sb = sb4("ident_s", [128, 128], BF16)
    identf_sb = sb4("identf_s", [128, 128], F32)
    fc4 = sb4("fc4", [128, 2], F32)
    pF = ps4("pF", [128, 512], F32)
    pB = ps4("pB", [128, 1024], BF16)
    S4 = mk_sems(nc, stack, ["l_cw", "l_x0", "l_x1", "l_tr", "l_te", "l_sq", "l_rs",
                             "l_st", "l_rd", "l_ap", "l_tb", "l_tbe", "l_oo0", "l_oo1",
                             "l_v", "l_a", "l_fc"])

    with nc.Block() as block:
        @block.sync
        def _(sync):
            S4["l_cw"].inc_dma(load_const(sync, ident_sb[:], ident_bf, c4))
            S4["l_cw"].inc_dma(load_const(sync, identf_sb[:], ident_f32, c6))
            for tt in range(NTT):
                W(S4, sync, "l_tr", 4 * (tt - 1))
                src = x1_out[:, tt * 128:(tt + 1) * 128].rearrange("(a p) c -> p a c", p=128)
                S4[f"l_x{tt % 2}"].inc_dma(sync.dma_start(
                    out=x1s[tt % 2][:].rearrange("p (a c) -> p a c", a=16), in_=src))

        @block.tensor
        def _(tensor):
            W(S4, tensor, "l_cw", 32)
            for tt in range(NTT):
                W(S4, tensor, f"l_x{tt % 2}", 16 * (tt // 2 + 1))
                for b in range(4):
                    W(S4, tensor, "l_te", 4 * tt + b)
                    for j in range(4):
                        fcx = 4 * b + j
                        ins = tensor.transpose(pF[:, j * 128:(j + 1) * 128],
                                               x1s[tt % 2][:, fcx * 128:(fcx + 1) * 128], identf_sb[:])
                    S4["l_tr"].inc(ins)
                W(S4, tensor, "l_ap", tt + 1)
                for half in range(2):
                    W(S4, tensor, "l_tbe", 2 * tt + half)
                    for j in range(8):
                        fcx = half * 8 + j
                        ins = tensor.transpose(pB[:, j * 128:(j + 1) * 128],
                                               xnt[:, fcx * 128:(fcx + 1) * 128], ident_sb[:])
                    S4["l_tb"].inc(ins)

        @block.vector
        def _(vector):
            vector.memset(fc4[:, 0:1], 1e-5)
            S4["l_fc"].inc(vector.memset(fc4[:, 1:2], -0.5))
            for tt in range(NTT):
                for b in range(4):
                    W(S4, vector, "l_tr", 4 * tt + b + 1)
                    S4["l_te"].inc(vector.tensor_copy(out=xtm[:, b * 512:(b + 1) * 512], in_=pF[:]))
                W(S4, vector, "l_te", 4 * (tt + 1))
                W(S4, vector, "l_st", 4 * tt)
                S4["l_rs"].inc(vector.tensor_reduce(out=stat[:, 0:1], in_=xtm[:], axis=AX.X, op=ALU.add))
                W(S4, vector, "l_rs", S4["l_rs"].n)
                S4["l_v"].inc(vector.tensor_scalar_mul(out=stat[:, 2:3], in0=stat[:, 0:1], scalar1=1.0 / D))
                W(S4, vector, "l_v", S4["l_v"].n)
                S4["l_v"].inc(vector.tensor_tensor(out=stat[:, 4:5], in0=stat[:, 2:3], in1=stat[:, 2:3], op=ALU.mult))
                W(S4, vector, "l_v", S4["l_v"].n)
                W(S4, vector, "l_sq", tt + 1)
                S4["l_st"].inc(vector.scalar_tensor_tensor(out=stat[:, 5:6], in0=stat[:, 1:2], scalar=1.0 / D,
                                                           in1=stat[:, 4:5], op0=ALU.mult, op1=ALU.subtract), 4)
                W(S4, vector, "l_rd", 2 * (tt + 1))
                W(S4, vector, "l_tb", 2 * tt)
                S4["l_ap"].inc(vector.tensor_scalar(out=xnt[:], in0=xtm[:],
                                                    scalar1=stat[:, 2:3], scalar2=stat[:, 7:8],
                                                    op0=ALU.subtract, op1=ALU.mult))
                for half in range(2):
                    W(S4, vector, "l_tb", 2 * tt + half + 1)
                    W(S4, vector, f"l_oo{tt % 2}", 16 * 16 * (tt // 2))
                    S4["l_tbe"].inc(vector.tensor_copy(out=xnf[tt % 2][:, half * 1024:(half + 1) * 1024], in_=pB[:]))

        @block.scalar
        def _(scalar):
            for tt in range(NTT):
                W(S4, scalar, "l_te", 4 * (tt + 1))
                W(S4, scalar, "l_st", 4 * tt)
                W(S4, scalar, "l_ap", tt)
                if tt == 0:
                    W(S4, scalar, "l_fc", 1)
                S4["l_sq"].inc(scalar.activation(out=sqs[:], in_=xtm[:], func=AF.Square,
                                                 accum_out=stat[:, 1:2]))
                W(S4, scalar, "l_st", 4 * (tt + 1))
                S4["l_a"].inc(scalar.activation(out=stat[:, 6:7], in_=stat[:, 5:6], func=AF.Ln, bias=fc4[:, 0:1]))
                W(S4, scalar, "l_a", S4["l_a"].n)
                S4["l_rd"].inc(scalar.activation(out=stat[:, 7:8], in_=stat[:, 6:7], func=AF.Exp, scale=fc4[:, 1:2]), 2)

        @block.gpsimd
        def _(gpsimd):
            for tt in range(NTT):
                for fcx in range(NFC):
                    W(S4, gpsimd, "l_tbe", 2 * tt + fcx // 8 + 1)
                    S4[f"l_oo{tt % 2}"].inc_dma(gpsimd.dma_start(
                        out=xn1_out[fcx * 128:(fcx + 1) * 128, tt * 128:(tt + 1) * 128],
                        in_=xnf[tt % 2][:, fcx * 128:(fcx + 1) * 128]))

        @block.sync
        def _(sync):
            sync.wait_ge(S4["l_oo0"].h, S4["l_oo0"].base + S4["l_oo0"].n)
            sync.wait_ge(S4["l_oo1"].h, S4["l_oo1"].base + S4["l_oo1"].n)
    st4.close()
    return nc



D = 2048
NFC = 16
CONVD = 4224
NCC = 33
DIN = 4096


def emit_b1(nc, TW, stack, io=None):
    TOWN = TW - 128
    NTB5 = [(i * 512, min(512, TW - i * 512)) for i in range((TW + 511) // 512)]
    NT5 = len(NTB5)
    NOB = TOWN // 512
    NOT = TOWN // 128

    xn1_in = dram_io(nc, io, "xn1", [D, TW], BF16, "ExternalInput")
    winT = dram_io(nc, io, "winT", [D, 8352], F32, "ExternalInput")
    convw = dram_io(nc, io, "convw", [128, NCC * 4], F32, "ExternalInput")
    convb = dram_io(nc, io, "convb", [128, NCC], F32, "ExternalInput")
    padmask = dram_io(nc, io, "padmask", [128, 1], F32, "ExternalInput")
    dtb = dram_io(nc, io, "dtb", [32, 1], F32, "ExternalInput")
    xcT_out = dram_io(nc, io, "xcT", [TOWN, CONVD], BF16, "ExternalOutput")
    z_out = dram_io(nc, io, "z_tm", [TOWN, DIN], BF16, "ExternalOutput")
    dt_out = dram_io(nc, io, "dt_f", [32, TW], F32, "ExternalOutput")

    ident_bf, c4 = inline_const(nc, "ident", np.eye(128, dtype=bf16))

    def W(S_, eng, sem, v):
        if v > 0:
            eng.wait_ge(S_[sem].h, S_[sem].base + v)

    xn1_sb = stack.enter_context(nc.sbuf_tensor(_u("xn1_sb"), [128, NFC * TW], BF16))
    xn1v = xn1_sb[:].rearrange("p (a c) -> p a c", a=NFC)

    S0 = mk_sems(nc, stack, ["z_xn"])
    with nc.Block() as block:
        @block.sync
        def _(sync):
            for fcx in range(NFC):
                S0["z_xn"].inc_dma(sync.dma_start(out=xn1v[:, fcx, :], in_=xn1_in[fcx * 128:(fcx + 1) * 128, :]))
            sync.wait_ge(S0["z_xn"].h, S0["z_xn"].base + 16 * NFC)

    st1 = ExitStack()
    sb1 = lambda *a: st1.enter_context(nc.sbuf_tensor(_u(a[0]), *a[1:]))
    ps1 = lambda *a: st1.enter_context(nc.psum_tensor(_u(a[0]), *a[1:]))
    w_f32 = [sb1(f"w_f32{i}", [128, NFC * 128], F32) for i in range(2)]
    w_bf = [sb1(f"w_bf{i}", [128, NFC * 128], BF16) for i in range(2)]
    cw_sb = sb1("cw_sb", [128, NCC * 4], F32)
    cb_sb = sb1("cb_sb", [128, NCC], F32)
    pmask = sb1("pmask", [128, 1], F32)
    dtb_sb = sb1("dtb_sb", [32, 1], F32)
    ident_sb = sb1("ident_s", [128, 128], BF16)
    xbc_w = sb1("xbc_w", [128, TW], BF16)
    convin = sb1("convin", [128, 3 + TOWN], BF16)
    dg = sb1("dg", [128, 4 * 128], BF16)
    xc_f = sb1("xc_f", [128, 512], F32)
    xc_s = sb1("xc_s", [128, 512], BF16)
    xct_s = [sb1(f"xct_s{i}", [128, 512], BF16) for i in range(2)]
    dtraw = sb1("dtraw", [32, TW], F32)
    dt_a = sb1("dt_a", [32, TW], F32)
    dt_r = sb1("dt_r", [32, TW], F32)
    dt_o = sb1("dt_o", [32, TW], F32)
    fc1 = sb1("fc1", [128, 2], F32)
    pp = [ps1(f"pp{i}", [128, 512], F32) for i in range(2)]
    pc = [ps1(f"pc{i}", [128, 512], F32) for i in range(2)]
    pt = ps1("pt", [128, 512], BF16)
    cwv = cw_sb[:].rearrange("p (a c) -> p a c", a=NCC)
    S1 = mk_sems(nc, stack, ["b_cw", "b_w0", "b_w1", "b_cv", "b_mm", "b_pe", "b_ci",
                             "b_dgv", "b_cmm", "b_cb", "b_ce", "b_tr", "b_te",
                             "b_o0", "b_o1", "b_fc", "b_v", "b_a", "b_dt"])

    with nc.Block() as block:
        @block.sync
        def _(sync):
            S1["b_cw"].inc_dma(load_const(sync, ident_sb[:], ident_bf, c4))
            S1["b_cw"].inc_dma(sync.dma_start(out=cw_sb[:], in_=convw))
            S1["b_cw"].inc_dma(sync.dma_start(out=cb_sb[:], in_=convb))
            S1["b_cw"].inc_dma(sync.dma_start(out=pmask[:], in_=padmask))
            S1["b_cw"].inc_dma(sync.dma_start(out=dtb_sb[:], in_=dtb))
            for jc in range(NCC + 1):
                W(S1, sync, "b_cv", jc - 1)
                if jc < NCC:
                    src = winT[:, DIN + 128 * jc: DIN + 128 * (jc + 1)].rearrange("(a p) c -> p a c", p=128)
                    S1[f"b_w{jc % 2}"].inc_dma(sync.dma_start(
                        out=w_f32[jc % 2][:].rearrange("p (a c) -> p a c", a=NFC), in_=src))
                else:
                    src = winT[:, 8320:8352].rearrange("(a p) c -> p a c", p=128)
                    S1[f"b_w{jc % 2}"].inc_dma(sync.dma_start(
                        out=w_f32[jc % 2][:].rearrange("p (a c) -> p a c", a=NFC)[:, :, 0:32], in_=src))

        @block.vector
        def _(vector):
            vector.memset(fc1[:, 0:1], -1.0)
            S1["b_fc"].inc(vector.memset(fc1[:, 1:2], 1.0))
            W(S1, vector, "b_cw", 16 * 5)
            def emit_conv_ops(jc):
                for tap in range(4):
                    if jc >= 1 and tap == 0:
                        W(S1, vector, "b_cmm", NOB * jc)
                    S1["b_dgv"].inc(vector.tensor_scalar(out=dg[:, tap * 128:(tap + 1) * 128],
                                                         in0=ident_sb[:], scalar1=cwv[:, jc, tap:tap + 1],
                                                         scalar2=None, op0=ALU.mult))
                W(S1, vector, "b_ce", NT5 * (jc + 1))
                if jc >= 1:
                    W(S1, vector, "b_cmm", NOB * jc)
                vector.tensor_copy(out=convin[:, 3:], in_=xbc_w[:, 0:TOWN])
                S1["b_ci"].inc(vector.tensor_scalar(out=convin[:, 0:3], in0=xbc_w[:, TOWN + 125:TOWN + 128],
                                                    scalar1=pmask[:, 0:1], scalar2=None, op0=ALU.mult))
                for ob in range(NOB):
                    i = jc * NOB + ob
                    W(S1, vector, "b_cmm", i + 1)
                    W(S1, vector, "b_a", i)
                    S1["b_cb"].inc(vector.tensor_scalar_add(out=xc_f[:], in0=pc[ob % 2][:],
                                                            scalar1=cb_sb[:, jc:jc + 1]))
                    W(S1, vector, "b_tr", i + 1)
                    W(S1, vector, f"b_o{i % 2}", 16 * 4 * (i // 2))
                    S1["b_te"].inc(vector.tensor_copy(out=xct_s[i % 2][:], in_=pt[:]))
            for jc in range(NCC + 2):
                if jc < NCC + 1:
                    W(S1, vector, f"b_w{jc % 2}", 16 * (jc // 2 + 1))
                    if jc >= 2:
                        W(S1, vector, "b_mm", NT5 * (jc - 1))
                    S1["b_cv"].inc(vector.tensor_copy(out=w_bf[jc % 2][:], in_=w_f32[jc % 2][:]))
                if 1 <= jc <= NCC:
                    emit_conv_ops(jc - 1)
            W(S1, vector, "b_ce", NT5 * (NCC + 1))
            S1["b_v"].inc(vector.tensor_scalar_add(out=dt_o[:], in0=dtraw[:], scalar1=dtb_sb[:, 0:1]))
            W(S1, vector, "b_v", S1["b_v"].n)
            W(S1, vector, "b_a", NOB * NCC + 3)
            S1["b_v"].inc(vector.tensor_scalar_max(out=dt_r[:], in0=dt_o[:], scalar1=0.0))
            W(S1, vector, "b_v", S1["b_v"].n)
            S1["b_dt"].inc(vector.tensor_tensor(out=dt_o[:], in0=dt_r[:], in1=dt_a[:], op=ALU.add), 1)

        @block.scalar
        def _(scalar):
            for jc in range(NCC):
                for it in range(NT5):
                    t0, tl = NTB5[it]
                    i = jc * NT5 + it
                    W(S1, scalar, "b_mm", i + 1)
                    if it == 0 and jc >= 1:
                        W(S1, scalar, "b_ci", jc)
                    S1["b_ce"].inc(scalar.activation(out=xbc_w[:, t0:t0 + tl], in_=pp[i % 2][:, :tl],
                                                     func=AF.Copy))
                for ob in range(NOB):
                    i = jc * NOB + ob
                    W(S1, scalar, "b_cb", i + 1)
                    W(S1, scalar, "b_tr", i)
                    S1["b_a"].inc(scalar.activation(out=xc_s[:], in_=xc_f[:], func=AF.Silu))
            for it in range(NT5):
                t0, tl = NTB5[it]
                i = NCC * NT5 + it
                W(S1, scalar, "b_mm", NCC * NT5 + it + 1)
                S1["b_ce"].inc(scalar.activation(out=dtraw[0:32, t0:t0 + tl],
                                                 in_=pp[i % 2][0:32, :tl], func=AF.Copy))
            W(S1, scalar, "b_v", 1)
            S1["b_a"].inc(scalar.activation(out=dt_a[:], in_=dt_o[:], func=AF.Abs))
            W(S1, scalar, "b_a", S1["b_a"].n)
            W(S1, scalar, "b_fc", 1)
            S1["b_a"].inc(scalar.activation(out=dt_a[:], in_=dt_a[:], func=AF.Exp, scale=fc1[0:32, 0:1]))
            W(S1, scalar, "b_a", S1["b_a"].n)
            S1["b_a"].inc(scalar.activation(out=dt_a[:], in_=dt_a[:], func=AF.Ln, bias=fc1[0:32, 1:2]))

        @block.tensor
        def _(tensor):
            W(S1, tensor, "b_cw", 16 * 5)
            for jc in range(NCC + 1):
                W(S1, tensor, "b_cv", jc + 1)
                nb = NT5 if jc < NCC else NT5
                for it in range(NT5):
                    t0, tl = NTB5[it]
                    i = jc * NT5 + it
                    W(S1, tensor, "b_ce", i - 1)
                    for fcx in range(NFC):
                        ins = tensor.matmul(pp[i % 2][:, :tl] if jc < NCC else pp[i % 2][0:32, :tl],
                                            w_bf[jc % 2][:, fcx * 128:(fcx + 1) * 128] if jc < NCC
                                            else w_bf[jc % 2][:].rearrange("p (a c) -> p a c", a=NFC)[:, fcx, 0:32],
                                            xn1v[:, fcx, t0:t0 + tl],
                                            start=(fcx == 0), stop=(fcx == NFC - 1))
                    S1["b_mm"].inc(ins)
                if jc >= NCC:
                    break
                W(S1, tensor, "b_dgv", 4 * (jc + 1))
                W(S1, tensor, "b_ci", jc + 1)
                for ob in range(NOB):
                    i = jc * NOB + ob
                    W(S1, tensor, "b_cb", i)
                    for tap in range(4):
                        ins = tensor.matmul(pc[ob % 2][:], dg[:, tap * 128:(tap + 1) * 128],
                                            convin[:, ob * 512 + tap: ob * 512 + tap + 512],
                                            start=(tap == 0), stop=(tap == 3))
                    S1["b_cmm"].inc(ins)
                    W(S1, tensor, "b_a", i + 1)
                    W(S1, tensor, "b_te", i)
                    for j in range(4):
                        ins = tensor.transpose(pt[:, j * 128:(j + 1) * 128],
                                               xc_s[:, j * 128:(j + 1) * 128], ident_sb[:])
                    S1["b_tr"].inc(ins)

        @block.gpsimd
        def _(gpsimd):
            for jc in range(NCC):
                for ob in range(NOB):
                    i = jc * NOB + ob
                    W(S1, gpsimd, "b_te", i + 1)
                    for j in range(4):
                        S1[f"b_o{i % 2}"].inc_dma(gpsimd.dma_start(
                            out=xcT_out[ob * 512 + j * 128: ob * 512 + (j + 1) * 128,
                                        jc * 128:(jc + 1) * 128],
                            in_=xct_s[i % 2][:, j * 128:(j + 1) * 128]))
            W(S1, gpsimd, "b_dt", 1)
            S1["b_o0"].inc_dma(gpsimd.dma_start(out=dt_out, in_=dt_o[:]))

        @block.sync
        def _(sync):
            sync.wait_ge(S1["b_o0"].h, S1["b_o0"].base + S1["b_o0"].n)
            sync.wait_ge(S1["b_o1"].h, S1["b_o1"].base + S1["b_o1"].n)
    st1.close()

    st2 = ExitStack()
    sb2 = lambda *a: st2.enter_context(nc.sbuf_tensor(_u(a[0]), *a[1:]))
    ps2 = lambda *a: st2.enter_context(nc.psum_tensor(_u(a[0]), *a[1:]))
    wz_f = [sb2(f"wz_f{i}", [128, NFC * 512], F32) for i in range(2)]
    wz_b = [sb2(f"wz_b{i}", [128, NFC * 512], BF16) for i in range(2)]
    z_s = [sb2(f"z_s{i}", [128, 512], BF16) for i in range(2)]
    pz = [ps2(f"pz{i}", [128, 512], F32) for i in range(2)]
    S2 = mk_sems(nc, stack, ["c_w0", "c_w1", "c_cv", "c_mm", "c_ev", "c_o0", "c_o1"])
    NOT_ = TOWN // 128

    with nc.Block() as block:
        @block.sync
        def _(sync):
            for zb in range(8):
                W(S2, sync, "c_cv", zb - 1)
                src = winT[:, zb * 512:(zb + 1) * 512].rearrange("(a p) c -> p a c", p=128)
                S2[f"c_w{zb % 2}"].inc_dma(sync.dma_start(
                    out=wz_f[zb % 2][:].rearrange("p (a c) -> p a c", a=NFC), in_=src))

        @block.vector
        def _(vector):
            for zb in range(8):
                W(S2, vector, f"c_w{zb % 2}", 16 * (zb // 2 + 1))
                if zb >= 2:
                    W(S2, vector, "c_mm", NOT_ * (zb - 1))
                S2["c_cv"].inc(vector.tensor_copy(out=wz_b[zb % 2][:], in_=wz_f[zb % 2][:]))

        @block.tensor
        def _(tensor):
            wzv = [wz_b[i][:].rearrange("p (a c) -> p a c", a=NFC) for i in range(2)]
            for zb in range(8):
                W(S2, tensor, "c_cv", zb + 1)
                for tt in range(NOT_):
                    i = zb * NOT_ + tt
                    W(S2, tensor, "c_ev", i - 1)
                    for fcx in range(NFC):
                        ins = tensor.matmul(pz[i % 2][:], xn1v[:, fcx, tt * 128:(tt + 1) * 128],
                                            wzv[zb % 2][:, fcx, :],
                                            start=(fcx == 0), stop=(fcx == NFC - 1))
                    S2["c_mm"].inc(ins)

        @block.scalar
        def _(scalar):
            for zb in range(8):
                for tt in range(NOT_):
                    i = zb * NOT_ + tt
                    W(S2, scalar, "c_mm", i + 1)
                    W(S2, scalar, f"c_o{i % 2}", 16 * (i // 2))
                    S2["c_ev"].inc(scalar.activation(out=z_s[i % 2][:], in_=pz[i % 2][:], func=AF.Copy))

        @block.gpsimd
        def _(gpsimd):
            for zb in range(8):
                for tt in range(NOT_):
                    i = zb * NOT_ + tt
                    W(S2, gpsimd, "c_ev", i + 1)
                    S2[f"c_o{i % 2}"].inc_dma(gpsimd.dma_start(
                        out=z_out[tt * 128:(tt + 1) * 128, zb * 512:(zb + 1) * 512],
                        in_=z_s[i % 2][:]))

        @block.sync
        def _(sync):
            sync.wait_ge(S2["c_o0"].h, S2["c_o0"].base + S2["c_o0"].n)
            sync.wait_ge(S2["c_o1"].h, S2["c_o1"].base + S2["c_o1"].n)
    st2.close()
    return nc



DIN = 4096
NST = 64
CONVD = 4224
NGX = 8
EPS_RMS = 1.1920929e-07


def sel_matrix(dtype):
    s = np.zeros((32, 4096), dtype=dtype)
    for h in range(32):
        s[h, h * 128:(h + 1) * 128] = 1
    return s


def emit_b2a(nc, TOWN, stack, io=None):
    NCH = TOWN // 128

    xcT = dram_io(nc, io, "xcT", [TOWN, CONVD], BF16, "ExternalInput")
    dt_in = dram_io(nc, io, "dt_f", [32, TOWN + 128], F32, "ExternalInput")
    alog = dram_io(nc, io, "A_log", [32, 1], F32, "ExternalInput")
    Dexp_in = dram_io(nc, io, "D_exp", [1, DIN], F32, "ExternalInput")
    Yp = dram_io(nc, io, "Yp", [TOWN, DIN], BF16, "ExternalOutput")
    carry_out = dram_io(nc, io, "carry", [NST, DIN], F32, "ExternalOutput")
    acs_out = dram_io(nc, io, "acs_all", [NCH * 32, 128], F32, "ExternalOutput")
    cums_out = dram_io(nc, io, "cums", [32, NCH], F32, "ExternalOutput")
    acs_scr = dram_io(nc, io, "acs_scr", [32, 128], F32, "Internal")

    selb, cs1 = inline_const(nc, "selb", sel_matrix(bf16))
    seln, cs2 = inline_const(nc, "seln", -sel_matrix(np.float32))
    ones1f, cs3 = inline_const(nc, "ones1f", np.ones((1, 128), dtype=np.float32))
    masku, cs6 = inline_const(nc, "masku", np.triu(np.ones((128, 128), dtype=np.float32)))
    ones3264, cs7 = inline_const(nc, "ones3264", np.ones((32, 64), dtype=bf16))
    identb, cs8 = inline_const(nc, "identb", np.eye(128, dtype=bf16))

    sb = lambda *a: stack.enter_context(nc.sbuf_tensor(_u(a[0]), *a[1:]))
    ps = lambda *a: stack.enter_context(nc.psum_tensor(_u(a[0]), *a[1:]))

    sel_sb = sb("sel_sb", [32, 4096], BF16)
    seln_sb = sb("seln_sb", [32, 4096], F32)
    ones1f_sb = sb("ones1f_sb", [1, 128], F32)
    masku_sb = sb("masku_sb", [128, 128], F32)
    ones3264_sb = sb("o3264_sb", [32, 64], BF16)
    ident_sb = sb("ident_sb", [128, 128], BF16)
    A_col = sb("A_col", [32, 1], F32)
    Dexp_sb = sb("Dexp_sb", [1, DIN], F32)
    D_full = sb("D_full", [128, 4096], BF16)
    xcs = [sb(f"xcs{i}", [128, CONVD], BF16) for i in range(2)]
    dtf = sb("dtf", [32, 128], F32)
    dtf_bf = sb("dtf_bf", [32, 128], BF16)
    dAf = sb("dAf", [32, 128], F32)
    acs_a = sb("acs_a", [32, 128], F32)
    acs_b = sb("acs_b", [32, 128], F32)
    dec_fm = sb("dec_fm", [32, 128], F32)
    dtdec_bf = sb("dtdec_bf", [32, 128], BF16)
    eA_fm = sb("eA_fm", [32, 128], BF16)
    dec_col = sb("dec_col", [32, 1], F32)
    dec_rep = sb("dec_rep", [32, 64], BF16)
    acc_col = sb("acc_col", [32, 1], F32)
    cums_sb = sb("cums_sb", [32, 16], F32)
    acs_rows = sb("acs_rows", [1, 4096], F32)
    B_fm = sb("B_fm", [64, 128], BF16)
    C_fm = sb("C_fm", [64, 128], BF16)
    sT = sb("sT", [128, 128], BF16)
    Lm = sb("Lm", [128, 4096], BF16)
    PTl = sb("PTl", [128, 4096], BF16)
    xd = sb("xd", [128, 4096], BF16)
    xdd = sb("xdd", [128, 4096], BF16)
    yo_s = sb("yo_s", [128, 512], F32)
    ygrp = sb("ygrp", [128, 512], F32)
    xdD = sb("xdD", [128, 512], BF16)
    y_all = [sb(f"y_all{i}", [128, 4096], BF16) for i in range(2)]
    carry = sb("carry_s", [64, 4096], F32)
    carry_bf = sb("carry_bf", [64, 4096], BF16)
    tdec = sb("tdec", [64, 512], F32)

    pS = ps("pS", [128, 512], F32)
    pL = ps("pL", [128, 512], F32)
    pX = ps("pX", [128, 512], F32)
    pYd = ps("pYd", [128, 512], F32)
    pYo = ps("pYo", [128, 512], F32)
    pSt = ps("pSt", [64, 512], F32)
    pDc = ps("pDc", [64, 512], F32)
    pB16 = ps("pB16", [128, 512], BF16)

    S_ = mk_sems(nc, stack, [
        "a_cw", "a_x0", "a_x1", "a_dt", "a_A", "a_Dfm", "a_Df", "a_dA", "a_acs",
        "a_decv", "a_deca", "a_dd", "a_eA", "a_drep", "a_cum", "a_acc", "a_bc0",
        "a_bc1", "a_tr", "a_bce", "a_scm", "a_stm", "a_lmm", "a_lme", "a_lmx",
        "a_pt", "a_xm", "a_xe", "a_ym", "a_eam", "a_yoe", "a_yos", "a_ygr",
        "a_xdd2", "a_ya", "a_sm", "a_dcm", "a_rc1", "a_rc2", "a_cbf",
        "a_o0", "a_o1", "a_oc", "a_dtbf"])

    def W(eng, sem, v):
        if v > 0:
            eng.wait_ge(S_[sem].h, S_[sem].base + v)

    with nc.Block() as block:
        @block.sync
        def _(sync):
            S_["a_cw"].inc_dma(load_const(sync, sel_sb[:], selb, cs1))
            S_["a_cw"].inc_dma(load_const(sync, seln_sb[:], seln, cs2))
            S_["a_cw"].inc_dma(load_const(sync, ones1f_sb[:], ones1f, cs3))
            S_["a_cw"].inc_dma(load_const(sync, masku_sb[:], masku, cs6))
            S_["a_cw"].inc_dma(load_const(sync, ones3264_sb[:], ones3264, cs7))
            S_["a_cw"].inc_dma(load_const(sync, ident_sb[:], identb, cs8))
            S_["a_cw"].inc_dma(sync.dma_start(out=A_col[:], in_=alog))
            S_["a_cw"].inc_dma(sync.dma_start(out=Dexp_sb[:], in_=Dexp_in))
            for c in range(NCH):
                W(sync, "a_ya", NGX * (c - 1))
                W(sync, "a_sm", NGX * (c - 1))
                W(sync, "a_tr", 2 * (c - 1))
                S_[f"a_x{c % 2}"].inc_dma(sync.dma_start(out=xcs[c % 2][:],
                                                         in_=xcT[c * 128:(c + 1) * 128, :]))
                W(sync, "a_dtbf", c)
                S_["a_dt"].inc_dma(sync.dma_start(out=dtf[:], in_=dt_in[:, c * 128:(c + 1) * 128]))

        @block.gpsimd
        def _(gpsimd):
            for c in range(NCH):
                W(gpsimd, "a_acs", 15 * c + 14)
                S_["a_bc0"].inc_dma(gpsimd.dma_start(out=acs_scr, in_=acs_a[:]))
                gpsimd.wait_ge(S_["a_bc0"].h, S_["a_bc0"].base + 16 * (c + 1))
                W(gpsimd, "a_lmm", 2 * NGX * c)
                S_["a_bc1"].inc_dma(gpsimd.dma_start(
                    out=acs_rows[0:1, :],
                    in_=acs_scr.rearrange("(o a) c -> o (a c)", o=1)))
                S_["a_oc"].inc_dma(gpsimd.dma_start(out=acs_out[32 * c:32 * (c + 1), :], in_=acs_a[:]))
                W(gpsimd, "a_ya", NGX * (c + 1))
                S_[f"a_o{c % 2}"].inc_dma(gpsimd.dma_start(out=Yp[c * 128:(c + 1) * 128, :],
                                                           in_=y_all[c % 2][:]))
            W(gpsimd, "a_rc2", NCH * NGX)
            S_["a_oc"].inc_dma(gpsimd.dma_start(out=carry_out, in_=carry[:]))
            W(gpsimd, "a_cum", NCH)
            S_["a_oc"].inc_dma(gpsimd.dma_start(out=cums_out, in_=cums_sb[:, 0:NCH]))

        @block.vector
        def _(vector):
            vector.memset(carry[:], 0.0)
            vector.memset(carry_bf[:], 0.0)
            S_["a_cbf"].inc(vector.memset(acc_col[:], 0.0))
            for sl in range(NGX):
                W(vector, "a_Dfm", sl + 1)
                S_["a_Df"].inc(vector.tensor_copy(out=D_full[:, sl * 512:(sl + 1) * 512], in_=pX[:]))
            for c in range(NCH):
                W(vector, "a_dt", 16 * (c + 1))
                W(vector, "a_A", 2)
                S_["a_dA"].inc(vector.tensor_scalar(out=dAf[:], in0=dtf[:], scalar1=A_col[:, 0:1],
                                                    scalar2=None, op0=ALU.mult))
                W(vector, "a_dA", S_["a_dA"].n)
                W(vector, "a_bc0", 16 * c)
                W(vector, "a_oc", 16 * c)
                W(vector, "a_eA", 2 * c)
                W(vector, "a_lmm", 2 * NGX * c)
                cur = dAf
                for si, s in enumerate([1, 2, 4, 8, 16, 32, 64]):
                    nxt = acs_a if si % 2 == 0 else acs_b
                    vector.tensor_tensor(out=nxt[:, s:], in0=cur[:, s:], in1=cur[:, :128 - s], op=ALU.add)
                    S_["a_acs"].inc(vector.tensor_copy(out=nxt[:, :s], in_=cur[:, :s]), 2)
                    W(vector, "a_acs", S_["a_acs"].n)
                    cur = nxt
                S_["a_acs"].inc(vector.tensor_copy(out=acs_b[:, 0:1], in_=acs_a[:, 0:1]))
                W(vector, "a_deca", c)
                S_["a_decv"].inc(vector.tensor_scalar(out=dec_fm[:], in0=acs_a[:],
                                                      scalar1=acs_a[:, 127:128], scalar2=-1.0,
                                                      op0=ALU.subtract, op1=ALU.mult))
                W(vector, "a_acc", c)
                S_["a_cum"].inc(vector.tensor_copy(out=cums_sb[:, c:c + 1], in_=acc_col[:]))
                W(vector, "a_cum", c + 1)
                S_["a_acc"].inc(vector.tensor_tensor(out=acc_col[:], in0=acc_col[:],
                                                     in1=acs_a[:, 127:128], op=ALU.add))
                W(vector, "a_deca", c + 1)
                S_["a_dd"].inc(vector.tensor_tensor(out=dtdec_bf[:], in0=dtf[:], in1=dec_fm[:], op=ALU.mult))
                W(vector, "a_dd", S_["a_dd"].n)
                S_["a_dtbf"].inc(vector.tensor_copy(out=dtf_bf[:], in_=dtf[:]))
                W(vector, "a_eA", 2 * (c + 1))
                W(vector, "a_dcm", NGX * c)
                S_["a_drep"].inc(vector.tensor_scalar(out=dec_rep[:], in0=ones3264_sb[:],
                                                      scalar1=dec_col[:, 0:1], scalar2=None, op0=ALU.mult))
                W(vector, "a_tr", 2 * (c + 1))
                W(vector, "a_scm", c)
                W(vector, "a_ym", 2 * NGX * c)
                vector.tensor_copy(out=B_fm[:], in_=pB16[0:64, 0:128])
                S_["a_bce"].inc(vector.tensor_copy(out=C_fm[:], in_=pB16[0:64, 128:256]))
                W(vector, "a_stm", c + 1)
                W(vector, "a_pt", NGX * c)
                S_["a_scm"].inc(vector.tensor_tensor(out=sT[:], in0=pS[:, 0:128], in1=masku_sb[:], op=ALU.mult))
                for sl in range(NGX):
                    i = c * NGX + sl
                    W(vector, "a_lmm", 2 * i + 2)
                    if c >= 1:
                        W(vector, "a_pt", (c - 1) * NGX + sl + 1)
                    S_["a_lme"].inc(vector.tensor_scalar_min(out=Lm[:, sl * 512:(sl + 1) * 512],
                                                             in0=pL[:], scalar1=0.0))
                    W(vector, "a_xm", 2 * i + 1)
                    if c >= 1 and sl == 0:
                        W(vector, "a_ym", 2 * NGX * c)
                        W(vector, "a_sm", NGX * c)
                    S_["a_xe"].inc(vector.tensor_tensor(out=xd[:, sl * 512:(sl + 1) * 512],
                                                        in0=xcs[c % 2][:, sl * 512:(sl + 1) * 512],
                                                        in1=pX[:], op=ALU.mult))
                    W(vector, "a_xm", 2 * i + 2)
                    S_["a_xe"].inc(vector.tensor_tensor(out=xdd[:, sl * 512:(sl + 1) * 512],
                                                        in0=xcs[c % 2][:, sl * 512:(sl + 1) * 512],
                                                        in1=pX[:], op=ALU.mult))
                    W(vector, "a_lmx", i + 1)
                    for hh in range(4):
                        ins = vector.tensor_tensor(
                            out=PTl[:, sl * 512 + hh * 128: sl * 512 + (hh + 1) * 128],
                            in0=Lm[:, sl * 512 + hh * 128: sl * 512 + (hh + 1) * 128],
                            in1=sT[:], op=ALU.mult)
                    S_["a_pt"].inc(ins)
                for g in range(NGX):
                    i = c * NGX + g
                    W(vector, "a_yoe", i + 1)
                    W(vector, "a_eam", i + 1)
                    S_["a_yos"].inc(vector.tensor_tensor(out=yo_s[:], in0=yo_s[:], in1=pX[:], op=ALU.mult))
                    W(vector, "a_yos", S_["a_yos"].n)
                    W(vector, "a_ym", 2 * i + 2)
                    S_["a_ygr"].inc(vector.scalar_tensor_tensor(out=ygrp[:], in0=pYd[:], scalar=1.0,
                                                                in1=yo_s[:], op0=ALU.mult, op1=ALU.add))
                    W(vector, "a_Df", NGX)
                    W(vector, "a_xdd2", i)
                    S_["a_xdd2"].inc(vector.tensor_tensor(out=xdD[:],
                                                          in0=xcs[c % 2][:, g * 512:(g + 1) * 512],
                                                          in1=D_full[:, g * 512:(g + 1) * 512], op=ALU.mult))
                    W(vector, "a_ygr", S_["a_ygr"].n)
                    W(vector, "a_xdd2", S_["a_xdd2"].n)
                    W(vector, f"a_o{c % 2}", 16 * (c // 2))
                    S_["a_ya"].inc(vector.tensor_tensor(out=y_all[c % 2][:, g * 512:(g + 1) * 512],
                                                        in0=ygrp[:], in1=xdD[:], op=ALU.add))
                    W(vector, "a_dcm", i + 1)
                    W(vector, "a_rc2", i)
                    S_["a_rc1"].inc(vector.tensor_tensor(out=tdec[:], in0=carry[:, g * 512:(g + 1) * 512],
                                                         in1=pDc[:], op=ALU.mult))
                    W(vector, "a_rc1", S_["a_rc1"].n)
                    W(vector, "a_sm", i + 1)
                    S_["a_rc2"].inc(vector.tensor_tensor(out=carry[:, g * 512:(g + 1) * 512],
                                                         in0=tdec[:], in1=pSt[:], op=ALU.add))
                W(vector, "a_rc2", NGX * (c + 1))
                W(vector, "a_ym", 2 * NGX * (c + 1))
                S_["a_cbf"].inc(vector.tensor_copy(out=carry_bf[:], in_=carry[:]))

        @block.scalar
        def _(scalar):
            W(scalar, "a_cw", 16 * 8)
            S_["a_A"].inc(scalar.activation(out=A_col[:], in_=A_col[:], func=AF.Exp))
            W(scalar, "a_A", 1)
            S_["a_A"].inc(scalar.mul(out=A_col[:], in_=A_col[:], mul=-1.0))
            for c in range(NCH):
                W(scalar, "a_decv", c + 1)
                S_["a_deca"].inc(scalar.activation(out=dec_fm[:], in_=dec_fm[:], func=AF.Exp))
                W(scalar, "a_acs", 15 * c + 14)
                W(scalar, "a_eam", NGX * c)
                W(scalar, "a_drep", c)
                S_["a_eA"].inc(scalar.activation(out=eA_fm[:], in_=acs_a[:], func=AF.Exp))
                S_["a_eA"].inc(scalar.activation(out=dec_col[:], in_=acs_a[:, 127:128], func=AF.Exp))
                for sl in range(NGX):
                    i = c * NGX + sl
                    W(scalar, "a_lme", i + 1)
                    S_["a_lmx"].inc(scalar.activation(out=Lm[:, sl * 512:(sl + 1) * 512],
                                                      in_=Lm[:, sl * 512:(sl + 1) * 512], func=AF.Exp))
                for g in range(NGX):
                    i = c * NGX + g
                    W(scalar, "a_ym", 2 * i + 1)
                    W(scalar, "a_ygr", i)
                    S_["a_yoe"].inc(scalar.activation(out=yo_s[:], in_=pYo[:], func=AF.Copy))

        @block.tensor
        def _(tensor):
            W(tensor, "a_cw", 16 * 8)
            for sl in range(NGX):
                W(tensor, "a_Df", sl)
                S_["a_Dfm"].inc(tensor.matmul(pX[:], ones1f_sb[:],
                                              Dexp_sb[0:1, sl * 512:(sl + 1) * 512],
                                              start=True, stop=True))
            for c in range(NCH):
                W(tensor, f"a_x{c % 2}", 16 * (c // 2 + 1))
                W(tensor, "a_bce", c)
                tensor.transpose(pB16[0:64, 0:128], xcs[c % 2][:, 4096:4160], ident_sb[:])
                S_["a_tr"].inc(tensor.transpose(pB16[0:64, 128:256], xcs[c % 2][:, 4160:4224], ident_sb[:]), 2)
                W(tensor, "a_bce", c + 1)
                W(tensor, "a_scm", c)
                S_["a_stm"].inc(tensor.matmul(pS[:, 0:128], B_fm[:], C_fm[:], start=True, stop=True))
                for sl in range(NGX):
                    i = c * NGX + sl
                    W(tensor, "a_bc1", 16 * (c + 1))
                    W(tensor, "a_lme", i)
                    tensor.matmul(pL[:], ones1f_sb[:], acs_rows[0:1, sl * 512:(sl + 1) * 512],
                                  start=True, stop=False)
                    S_["a_lmm"].inc(tensor.matmul(pL[:], acs_a[:], seln_sb[:, sl * 512:(sl + 1) * 512],
                                                  start=False, stop=True), 2)
                    W(tensor, "a_dtbf", c + 1)
                    W(tensor, "a_xe", 2 * i)
                    if i == 0:
                        W(tensor, "a_Df", NGX)
                    S_["a_xm"].inc(tensor.matmul(pX[:], dtf_bf[:], sel_sb[:, sl * 512:(sl + 1) * 512],
                                                 start=True, stop=True))
                    W(tensor, "a_dd", c + 1)
                    W(tensor, "a_xe", 2 * i + 1)
                    S_["a_xm"].inc(tensor.matmul(pX[:], dtdec_bf[:], sel_sb[:, sl * 512:(sl + 1) * 512],
                                                 start=True, stop=True))
                for g in range(NGX):
                    i = c * NGX + g
                    W(tensor, "a_yoe", i)
                    W(tensor, "a_cbf", c + 1)
                    for hh in range(4):
                        ins = tensor.matmul(pYo[:, hh * 128:(hh + 1) * 128], C_fm[:],
                                            carry_bf[:, (4 * g + hh) * 128:(4 * g + hh + 1) * 128],
                                            start=True, stop=True)
                    S_["a_ym"].inc(ins)
                    W(tensor, "a_ygr", i)
                    W(tensor, "a_pt", i + 1)
                    W(tensor, "a_xe", 2 * i + 1)
                    for hh in range(4):
                        off = g * 512 + hh * 128
                        ins = tensor.matmul(pYd[:, hh * 128:(hh + 1) * 128],
                                            PTl[:, off:off + 128], xd[:, off:off + 128],
                                            start=True, stop=True)
                    S_["a_ym"].inc(ins)
                    W(tensor, "a_eA", 2 * (c + 1))
                    W(tensor, "a_yos", i)
                    if g == 0:
                        W(tensor, "a_xe", 2 * NGX * (c + 1))
                    S_["a_eam"].inc(tensor.matmul(pX[:], eA_fm[:], sel_sb[:, g * 512:(g + 1) * 512],
                                                  start=True, stop=True))
                    W(tensor, "a_rc2", i)
                    W(tensor, "a_xe", 2 * i + 2)
                    for hh in range(4):
                        ins = tensor.matmul(pSt[:, hh * 128:(hh + 1) * 128],
                                            xcs[c % 2][:, 4096:4160],
                                            xdd[:, (4 * g + hh) * 128:(4 * g + hh + 1) * 128],
                                            start=True, stop=True)
                    S_["a_sm"].inc(ins)
                    W(tensor, "a_drep", c + 1)
                    W(tensor, "a_rc1", i)
                    S_["a_dcm"].inc(tensor.matmul(pDc[:], dec_rep[:], sel_sb[:, g * 512:(g + 1) * 512],
                                                  start=True, stop=True))

        @block.sync
        def _(sync):
            sync.wait_ge(S_["a_o0"].h, S_["a_o0"].base + S_["a_o0"].n)
            sync.wait_ge(S_["a_o1"].h, S_["a_o1"].base + S_["a_o1"].n)
            sync.wait_ge(S_["a_oc"].h, S_["a_oc"].base + S_["a_oc"].n)
    return nc


def emit_b2b(nc, TOWN, stack, io=None, mask=None):
    NCH = TOWN // 128
    NOB = TOWN // 512

    Yp = dram_io(nc, io, "Yp", [TOWN, DIN], BF16, "ExternalInput")
    xcT = dram_io(nc, io, "xcT", [TOWN, CONVD], BF16, "ExternalInput")
    z_in = dram_io(nc, io, "z_tm", [TOWN, DIN], BF16, "ExternalInput")
    acs_in = dram_io(nc, io, "acs_all", [NCH * 32, 128], F32, "ExternalInput")
    cums_in = dram_io(nc, io, "cums", [32, NCH], F32, "ExternalInput")
    carry0_in = dram_io(nc, io, "carry0", [NST, DIN], F32, "ExternalInput")
    woutT = dram_io(nc, io, "woutT", [DIN, 2048], F32, "ExternalInput")
    y2_out = dram_io(nc, io, "y2", [2048, TOWN], F32, "ExternalOutput")

    selb, cs1 = inline_const(nc, "selb", sel_matrix(bf16))
    ones3264, cs7 = inline_const(nc, "ones3264", np.ones((32, 64), dtype=bf16))
    identb, cs8 = inline_const(nc, "identb", np.eye(128, dtype=bf16))

    sb = lambda *a: stack.enter_context(nc.sbuf_tensor(_u(a[0]), *a[1:]))
    g_fm = [sb(f"g_fm{ic}", [128, TOWN], BF16) for ic in range(32)]

    st1 = ExitStack()
    sb1 = lambda *a: st1.enter_context(nc.sbuf_tensor(_u(a[0]), *a[1:]))
    ps1 = lambda *a: st1.enter_context(nc.psum_tensor(_u(a[0]), *a[1:]))
    sel_sb = sb1("sel_sb", [32, 4096], BF16)
    ones3264_sb = sb1("o3264_sb", [32, 64], BF16)
    ident_sb = sb1("ident_sb", [128, 128], BF16)
    cums_sb = sb1("cums_sb", [32, 16], F32)
    carry0 = sb1("carry0_s", [64, 4096], F32)
    ysb = [sb1("ysb0", [128, 4096], BF16)] * 2
    zsb = [sb1("zsb0", [128, 4096], BF16)] * 2
    Csl = [sb1(f"Csl{i}", [128, 64], BF16) for i in range(2)]
    acsc = [sb1(f"acsc{i}", [32, 128], F32) for i in range(2)]
    cdc = sb1("cdc", [32, 1], F32)
    crep = sb1("crep", [32, 64], BF16)
    c0b = sb1("c0b", [64, 512], BF16)
    C_fm = sb1("C_fm", [64, 128], BF16)
    eA_fm = sb1("eA_fm", [32, 128], BF16)
    dy_s = sb1("dy_s", [128, 512], F32)
    yf32 = sb1("yf32", [128, 4096], F32)
    ms = sb1("ms", [128, 4], F32)
    zs_s = sb1("zs_s", [128, 4096], BF16)
    gbuf = sb1("gbuf", [128, 512], F32)
    gtm = sb1("gtm", [128, 4096], BF16)
    fcb = sb1("fcb", [128, 2], F32)
    pC = ps1("pC", [64, 512], F32)
    pD = ps1("pD", [128, 512], F32)
    pE = ps1("pE", [128, 512], F32)
    pT = ps1("pT", [128, 512], BF16)
    S_ = mk_sems(nc, stack, [
        "b_cw", "b_y0", "b_y1", "b_z0", "b_z1", "b_C0", "b_C1", "b_a0", "b_a1",
        "b_cd", "b_crep", "b_cbm", "b_c0b", "b_tr", "b_ce", "b_eA", "b_em",
        "b_de", "b_ds", "b_yac", "b_sq", "b_ms", "b_rstd", "b_sil",
        "b_g1", "b_g2", "b_gt", "b_gte", "b_fc", "b_v", "b_a", "b_msk"])

    def W(eng, sem, v):
        if v > 0:
            eng.wait_ge(S_[sem].h, S_[sem].base + v)

    with nc.Block() as block:
        @block.sync
        def _(sync):
            S_["b_cw"].inc_dma(load_const(sync, sel_sb[:], selb, cs1))
            S_["b_cw"].inc_dma(load_const(sync, ones3264_sb[:], ones3264, cs7))
            S_["b_cw"].inc_dma(load_const(sync, ident_sb[:], identb, cs8))
            S_["b_cw"].inc_dma(sync.dma_start(out=cums_sb[:, 0:NCH], in_=cums_in))
            S_["b_cw"].inc_dma(sync.dma_start(out=carry0[:], in_=carry0_in))
            for c in range(NCH):
                W(sync, "b_yac", NGX * c)
                S_[f"b_y{c % 2}"].inc_dma(sync.dma_start(out=ysb[c % 2][:], in_=Yp[c * 128:(c + 1) * 128, :]))
                W(sync, "b_sil", c)
                S_[f"b_z{c % 2}"].inc_dma(sync.dma_start(out=zsb[c % 2][:], in_=z_in[c * 128:(c + 1) * 128, :]))
                W(sync, "b_tr", c - 1)
                S_[f"b_C{c % 2}"].inc_dma(sync.dma_start(out=Csl[c % 2][:],
                                                         in_=xcT[c * 128:(c + 1) * 128, 4160:4224]))
                W(sync, "b_eA", c - 1)
                S_[f"b_a{c % 2}"].inc_dma(sync.dma_start(out=acsc[c % 2][:],
                                                         in_=acs_in[32 * c:32 * (c + 1), :]))

        @block.vector
        def _(vector):
            vector.memset(fcb[:, 0:1], EPS_RMS)
            S_["b_fc"].inc(vector.memset(fcb[:, 1:2], -0.5))
            if mask is not None:
                W(vector, "b_cw", 16 * 5)
                S_["b_msk"].inc(vector.tensor_scalar_mul(out=carry0[:], in0=carry0[:], scalar1=mask))
            for c in range(NCH):
                W(vector, "b_cd", c + 1)
                W(vector, "b_cbm", NGX * c)
                S_["b_crep"].inc(vector.tensor_scalar(out=crep[:], in0=ones3264_sb[:],
                                                      scalar1=cdc[:, 0:1], scalar2=None, op0=ALU.mult))
                W(vector, "b_tr", c + 1)
                W(vector, "b_ds", NGX * c)
                S_["b_ce"].inc(vector.tensor_copy(out=C_fm[:], in_=pT[0:64, 0:128]))
                for g in range(NGX):
                    i = c * NGX + g
                    W(vector, "b_cbm", i + 1)
                    W(vector, "b_ds", i)
                    W(vector, "b_msk", 1 if mask is not None else 0)
                    S_["b_c0b"].inc(vector.tensor_tensor(out=c0b[:], in0=carry0[:, g * 512:(g + 1) * 512],
                                                         in1=pC[:], op=ALU.mult))
                    W(vector, "b_de", i + 1)
                    W(vector, "b_em", i + 1)
                    S_["b_v"].inc(vector.tensor_tensor(out=dy_s[:], in0=dy_s[:], in1=pE[:], op=ALU.mult))
                    W(vector, "b_v", S_["b_v"].n)
                    W(vector, f"b_y{c % 2}", 16 * (c // 2 + 1))
                    W(vector, "b_sq", c)
                    S_["b_yac"].inc(vector.tensor_tensor(out=yf32[:, g * 512:(g + 1) * 512],
                                                         in0=ysb[c % 2][:, g * 512:(g + 1) * 512],
                                                         in1=dy_s[:], op=ALU.add))
                W(vector, "b_sq", c + 1)
                S_["b_ms"].inc(vector.tensor_scalar_mul(out=ms[:, 1:2], in0=ms[:, 0:1], scalar1=1.0 / DIN))
                for g in range(NGX):
                    i = c * NGX + g
                    W(vector, "b_sil", c + 1)
                    W(vector, "b_rstd", 2 * (c + 1))
                    W(vector, "b_g2", i)
                    S_["b_g1"].inc(vector.tensor_tensor(out=gbuf[:], in0=yf32[:, g * 512:(g + 1) * 512],
                                                        in1=zs_s[:, g * 512:(g + 1) * 512], op=ALU.mult))
                    W(vector, "b_g1", S_["b_g1"].n)
                    W(vector, "b_gt", 8 * c)
                    W(vector, "b_sq", c + 1)
                    S_["b_g2"].inc(vector.tensor_scalar(out=gtm[:, g * 512:(g + 1) * 512], in0=gbuf[:],
                                                        scalar1=ms[:, 3:4], scalar2=None, op0=ALU.mult))
                for b in range(8):
                    W(vector, "b_gt", 8 * c + b + 1)
                    for j in range(4):
                        ic = 4 * b + j
                        ins = vector.tensor_copy(out=g_fm[ic][:, c * 128:(c + 1) * 128],
                                                 in_=pT[:, j * 128:(j + 1) * 128])
                    S_["b_gte"].inc(ins)

        @block.scalar
        def _(scalar):
            W(scalar, "b_cw", 16 * 5)
            for c in range(NCH):
                W(scalar, "b_crep", c)
                S_["b_cd"].inc(scalar.activation(out=cdc[:], in_=cums_sb[:, c:c + 1], func=AF.Exp))
                W(scalar, f"b_a{c % 2}", 16 * (c // 2 + 1))
                W(scalar, "b_em", NGX * c)
                S_["b_eA"].inc(scalar.activation(out=eA_fm[:], in_=acsc[c % 2][:], func=AF.Exp))
                for g in range(NGX):
                    i = c * NGX + g
                    W(scalar, "b_ds", i + 1)
                    W(scalar, "b_v", i)
                    S_["b_de"].inc(scalar.activation(out=dy_s[:], in_=pD[:], func=AF.Copy))
                W(scalar, "b_yac", NGX * (c + 1))
                W(scalar, "b_ms", c)
                W(scalar, "b_gt", 8 * c)
                S_["b_sq"].inc(scalar.activation(out=gtm[:], in_=yf32[:], func=AF.Square,
                                                 accum_out=ms[:, 0:1]))
                W(scalar, "b_ms", c + 1)
                if c == 0:
                    W(scalar, "b_fc", 1)
                S_["b_a"].inc(scalar.activation(out=ms[:, 2:3], in_=ms[:, 1:2], func=AF.Ln, bias=fcb[:, 0:1]))
                W(scalar, "b_a", S_["b_a"].n)
                S_["b_rstd"].inc(scalar.activation(out=ms[:, 3:4], in_=ms[:, 2:3], func=AF.Exp,
                                                   scale=fcb[:, 1:2]), 2)
                W(scalar, f"b_z{c % 2}", 16 * (c // 2 + 1))
                W(scalar, "b_g1", NGX * c)
                S_["b_sil"].inc(scalar.activation(out=zs_s[:], in_=zsb[c % 2][:], func=AF.Silu))

        @block.tensor
        def _(tensor):
            W(tensor, "b_cw", 16 * 5)
            for c in range(NCH):
                W(tensor, f"b_C{c % 2}", 16 * (c // 2 + 1))
                W(tensor, "b_ce", c)
                W(tensor, "b_gte", 8 * c)
                S_["b_tr"].inc(tensor.transpose(pT[0:64, 0:128], Csl[c % 2][:], ident_sb[:]))
                for g in range(NGX):
                    i = c * NGX + g
                    W(tensor, "b_crep", c + 1)
                    W(tensor, "b_c0b", i)
                    S_["b_cbm"].inc(tensor.matmul(pC[:], crep[:], sel_sb[:, g * 512:(g + 1) * 512],
                                                  start=True, stop=True))
                    W(tensor, "b_c0b", i + 1)
                    W(tensor, "b_ce", c + 1)
                    W(tensor, "b_de", i)
                    for hh in range(4):
                        ins = tensor.matmul(pD[:, hh * 128:(hh + 1) * 128], C_fm[:],
                                            c0b[:, hh * 128:(hh + 1) * 128], start=True, stop=True)
                    S_["b_ds"].inc(ins)
                    W(tensor, "b_eA", c + 1)
                    W(tensor, "b_v", i)
                    S_["b_em"].inc(tensor.matmul(pE[:], eA_fm[:], sel_sb[:, g * 512:(g + 1) * 512],
                                                 start=True, stop=True))
                for b in range(8):
                    W(tensor, "b_g2", c * NGX + b + 1)
                    W(tensor, "b_gte", 8 * c + b)
                    W(tensor, "b_ce", c + 1)
                    for j in range(4):
                        ic = 4 * b + j
                        ins = tensor.transpose(pT[:, j * 128:(j + 1) * 128],
                                               gtm[:, ic * 128:(ic + 1) * 128], ident_sb[:])
                    S_["b_gt"].inc(ins)

        @block.sync
        def _(sync):
            sync.wait_ge(S_["b_gte"].h, S_["b_gte"].base + 8 * NCH)
    st1.close()

    st2 = ExitStack()
    sb2 = lambda *a: st2.enter_context(nc.sbuf_tensor(_u(a[0]), *a[1:]))
    ps2 = lambda *a: st2.enter_context(nc.psum_tensor(_u(a[0]), *a[1:]))
    wst = [sb2(f"wst{i}", [128, 32 * 128], F32) for i in range(2)]
    wbf = [sb2(f"wbf{i}", [128, 32 * 128], BF16) for i in range(2)]
    y2s = [sb2(f"y2s{i}", [128, 512], F32) for i in range(2)]
    po = [ps2(f"po{i}", [128, 512], F32) for i in range(2)]
    S2 = mk_sems(nc, stack, ["o_w0", "o_w1", "o_cv", "o_mm", "o_ev", "o_o0", "o_o1"])

    def W2(eng, sem, v):
        if v > 0:
            eng.wait_ge(S2[sem].h, S2[sem].base + v)

    with nc.Block() as block:
        @block.sync
        def _(sync):
            for dc in range(16):
                W2(sync, "o_cv", dc - 1)
                src = woutT[:, dc * 128:(dc + 1) * 128].rearrange("(a p) c -> p a c", p=128)
                S2[f"o_w{dc % 2}"].inc_dma(sync.dma_start(
                    out=wst[dc % 2][:].rearrange("p (a c) -> p a c", a=32), in_=src))

        @block.vector
        def _(vector):
            for dc in range(16):
                W2(vector, f"o_w{dc % 2}", 16 * (dc // 2 + 1))
                if dc >= 2:
                    W2(vector, "o_mm", NOB * (dc - 1))
                S2["o_cv"].inc(vector.tensor_copy(out=wbf[dc % 2][:], in_=wst[dc % 2][:]))

        @block.tensor
        def _(tensor):
            for dc in range(16):
                W2(tensor, "o_cv", dc + 1)
                for tb in range(NOB):
                    i = dc * NOB + tb
                    W2(tensor, "o_ev", i - 1)
                    for ic in range(32):
                        ins = tensor.matmul(po[i % 2][:], wbf[dc % 2][:, ic * 128:(ic + 1) * 128],
                                            g_fm[ic][:, tb * 512:(tb + 1) * 512],
                                            start=(ic == 0), stop=(ic == 31))
                    S2["o_mm"].inc(ins)

        @block.scalar
        def _(scalar):
            for dc in range(16):
                for tb in range(NOB):
                    i = dc * NOB + tb
                    W2(scalar, "o_mm", i + 1)
                    W2(scalar, f"o_o{i % 2}", 16 * (i // 2))
                    S2["o_ev"].inc(scalar.activation(out=y2s[i % 2][:], in_=po[i % 2][:], func=AF.Copy))

        @block.gpsimd
        def _(gpsimd):
            for dc in range(16):
                for tb in range(NOB):
                    i = dc * NOB + tb
                    W2(gpsimd, "o_ev", i + 1)
                    S2[f"o_o{i % 2}"].inc_dma(gpsimd.dma_start(
                        out=y2_out[dc * 128:(dc + 1) * 128, tb * 512:(tb + 1) * 512],
                        in_=y2s[i % 2][:]))

        @block.sync
        def _(sync):
            sync.wait_ge(S2["o_o0"].h, S2["o_o0"].base + S2["o_o0"].n)
            sync.wait_ge(S2["o_o1"].h, S2["o_o1"].base + S2["o_o1"].n)
    st2.close()
    return nc



D = 2048
NFC = 16
NI = 8192


def emit_c(nc, TW, stack, io=None):
    TOWN = TW - 128
    NTT = TOWN // 128
    NGB = 4
    NT9 = TOWN // 256

    xn1_in = dram_io(nc, io, "xn1", [D, TW], BF16, "ExternalInput")
    x1_in = dram_io(nc, io, "x1", [D, TW], F32, "ExternalInput")
    y2_in = dram_io(nc, io, "y2", [D, TOWN], F32, "ExternalInput")
    wg_in = dram_io(nc, io, "wg1p", [D, NI], F32, "ExternalInput")
    wu_in = dram_io(nc, io, "wu1p", [D, NI], F32, "ExternalInput")
    wd_in = dram_io(nc, io, "wd1p", [NI, D], F32, "ExternalInput")
    x2_out = dram_io(nc, io, "x2", [D, TOWN], F32, "ExternalOutput")
    hsp = dram_io(nc, io, "hsp", [NI, TOWN], BF16, "Internal")

    identb, cs8 = inline_const(nc, "identb", np.eye(128, dtype=bf16))

    st1 = ExitStack()
    sb1 = lambda *a: st1.enter_context(nc.sbuf_tensor(_u(a[0]), *a[1:]))
    ps1 = lambda *a: st1.enter_context(nc.psum_tensor(_u(a[0]), *a[1:]))
    xn1_sb = sb1("xn1_sb", [128, NFC * TOWN], BF16)
    xnv = xn1_sb[:].rearrange("p (a c) -> p a c", a=NFC)
    wstage = sb1("wstage", [128, NFC * 256], F32)
    wg_bf = sb1("wg_bf", [128, NFC * 512], BF16)
    wu_bf = sb1("wu_bf", [128, NFC * 512], BF16)
    ident_sb = sb1("ident_s", [128, 128], BF16)
    hqf = [sb1(f"hqf{j}", [128, NTT * 512], BF16) for j in range(4)]
    aq = [sb1(f"aq{j}", [128, 512], F32) for j in range(4)]
    t1 = sb1("t1", [128, 512], F32)
    b1 = sb1("b1", [128, 512], F32)
    m2 = sb1("m2", [128, 512], F32)
    kq = sb1("kq", [128, 512], F32)
    hs = sb1("hs", [128, 512], BF16)
    hto = [sb1(f"hto{i}", [128, 512], BF16) for i in range(2)]
    fcq = sb1("fcq", [128, 4], F32)
    pg = [ps1(f"pg{i}", [128, 512], F32) for i in range(2)]
    pu = [ps1(f"pu{i}", [128, 512], F32) for i in range(2)]
    pT = [ps1(f"pT{i}", [128, 512], BF16) for i in range(2)]
    S1 = mk_sems(nc, stack, ["c_xn", "c_wl0", "c_wl1", "c_cv", "c_gmm", "c_umm",
                             "c_ge", "c_ue", "c_hq", "c_sp", "c_tr", "c_te",
                             "c_o0", "c_o1", "c_v", "c_a"])

    def W(S_, eng, sem, v):
        if v > 0:
            eng.wait_ge(S_[sem].h, S_[sem].base + v)

    with nc.Block() as block:
        @block.sync
        def _(sync):
            S1["c_xn"].inc_dma(load_const(sync, ident_sb[:], identb, cs8))
            for fcx in range(NFC):
                S1["c_xn"].inc_dma(sync.dma_start(out=xnv[:, fcx, :],
                                                  in_=xn1_in[fcx * 128:(fcx + 1) * 128, 0:TOWN]))
            for gb in range(NGB):
                for q in range(4):
                    k = gb * 4 + q
                    col = q * 2048 + gb * 512
                    for hf in range(4):
                        m = 8 * k + 2 * hf
                        W(S1, sync, "c_cv", m - 1)
                        srcw = wg_in[:, col + hf * 128: col + (hf + 1) * 128].rearrange("(a p) c -> p a c", p=128)
                        S1[f"c_wl{m % 2}"].inc_dma(sync.dma_start(
                            out=wstage[:].rearrange("p (a c) -> p a c", a=NFC)[:, :, 0:128], in_=srcw))
                        m = 8 * k + 2 * hf + 1
                        W(S1, sync, "c_cv", m - 1)
                        srcw = wu_in[:, col + hf * 128: col + (hf + 1) * 128].rearrange("(a p) c -> p a c", p=128)
                        S1[f"c_wl{m % 2}"].inc_dma(sync.dma_start(
                            out=wstage[:].rearrange("p (a c) -> p a c", a=NFC)[:, :, 128:256], in_=srcw))

        @block.vector
        def _(vector):
            vector.memset(fcq[:, 0:1], 1.0)
            vector.memset(fcq[:, 1:2], float(np.sqrt(1.0 - 2.0 ** -11)))
            vector.memset(fcq[:, 2:3], float(np.sqrt(1.0 - 2.0 * 2.0 ** -11)))
            S1["c_hq"].inc(vector.memset(fcq[:, 3:4], float(np.sqrt(1.0 - 3.0 * 2.0 ** -11))))

            def vv(ins):
                S1["c_v"].inc(ins)
                W(S1, vector, "c_v", S1["c_v"].n)

            def sparsify(gb, tt):
                it = gb * NTT + tt
                sl = slice(tt * 512, (tt + 1) * 512)
                W(S1, vector, "c_a", 4 * (it + 1))
                vv(vector.tensor_tensor(out=t1[:], in0=aq[0][:], in1=aq[1][:], op=ALU.max))
                vv(vector.tensor_tensor(out=b1[:], in0=aq[0][:], in1=aq[1][:], op=ALU.min))
                vv(vector.tensor_tensor(out=m2[:], in0=aq[2][:], in1=aq[3][:], op=ALU.max))
                vv(vector.tensor_tensor(out=kq[:], in0=aq[2][:], in1=aq[3][:], op=ALU.min))
                vv(vector.tensor_tensor(out=t1[:], in0=t1[:], in1=m2[:], op=ALU.min))
                vv(vector.tensor_tensor(out=b1[:], in0=b1[:], in1=kq[:], op=ALU.max))
                vv(vector.tensor_tensor(out=m2[:], in0=t1[:], in1=b1[:], op=ALU.max))
                for j in range(4):
                    W(S1, vector, "c_sp", S1["c_sp"].n)
                    vv(vector.tensor_tensor(out=kq[:], in0=aq[j][:], in1=m2[:], op=ALU.is_ge))
                    S1["c_sp"].inc(vector.tensor_tensor(out=hqf[j][:, sl], in0=hqf[j][:, sl],
                                                        in1=kq[:], op=ALU.mult), 2)

            for gb in range(NGB):
                for q in range(4):
                    k = gb * 4 + q
                    for hf in range(4):
                        m = 8 * k + 2 * hf
                        W(S1, vector, f"c_wl{m % 2}", 16 * (m // 2 + 1))
                        if k >= 1 and hf == 0:
                            W(S1, vector, "c_gmm", NTT * k)
                        S1["c_cv"].inc(vector.tensor_copy(
                            out=wg_bf[:].rearrange("p (a c) -> p a c", a=NFC)[:, :, hf * 128:(hf + 1) * 128],
                            in_=wstage[:].rearrange("p (a c) -> p a c", a=NFC)[:, :, 0:128]))
                        m = 8 * k + 2 * hf + 1
                        W(S1, vector, f"c_wl{m % 2}", 16 * (m // 2 + 1))
                        if k >= 1 and hf == 0:
                            W(S1, vector, "c_umm", NTT * k)
                        S1["c_cv"].inc(vector.tensor_copy(
                            out=wu_bf[:].rearrange("p (a c) -> p a c", a=NFC)[:, :, hf * 128:(hf + 1) * 128],
                            in_=wstage[:].rearrange("p (a c) -> p a c", a=NFC)[:, :, 128:256]))
                    for tt in range(NTT):
                        j = q * NTT + tt
                        gj = gb * 4 * NTT + j
                        W(S1, vector, "c_umm", gj + 1)
                        W(S1, vector, "c_ge", gj + 1)
                        if gb >= 1 and j == 0:
                            W(S1, vector, "c_tr", 4 * NTT * gb)
                        ins = vector.tensor_tensor(out=hqf[q][:, tt * 512:(tt + 1) * 512],
                                                   in0=hs[:], in1=pu[j % 2][:], op=ALU.mult)
                        S1["c_ue"].inc(ins)
                        if q == 3 and tt >= 1:
                            sparsify(gb, tt - 1)
                sparsify(gb, NTT - 1)

        @block.scalar
        def _(scalar):
            W(S1, scalar, "c_hq", 1)

            def squares(gb, tt):
                it = gb * NTT + tt
                sl = slice(tt * 512, (tt + 1) * 512)
                W(S1, scalar, "c_ue", gb * 4 * NTT + 3 * NTT + tt + 1)
                if it >= 1:
                    W(S1, scalar, "c_sp", 8 * it)
                for j in range(4):
                    S1["c_a"].inc(scalar.activation(out=aq[j][:], in_=hqf[j][:, sl],
                                                    func=AF.Square, scale=fcq[:, j:j + 1]))

            def evacs(gb, tt):
                it = gb * NTT + tt
                for j in range(4):
                    i = it * 4 + j
                    W(S1, scalar, "c_tr", i + 1)
                    W(S1, scalar, f"c_o{i % 2}", 16 * (i // 2))
                    S1["c_te"].inc(scalar.activation(out=hto[i % 2][:], in_=pT[i % 2][:], func=AF.Copy))

            for gb in range(NGB):
                for q in range(4):
                    for tt in range(NTT):
                        j = q * NTT + tt
                        gj = gb * 4 * NTT + j
                        W(S1, scalar, "c_gmm", gj + 1)
                        W(S1, scalar, "c_ue", gj)
                        S1["c_ge"].inc(scalar.activation(out=hs[:], in_=pg[j % 2][:], func=AF.Silu))
                        if q == 3 and tt >= 1:
                            squares(gb, tt - 1)
                squares(gb, NTT - 1)
                for tt in range(NTT):
                    evacs(gb, tt)

        @block.tensor
        def _(tensor):
            W(S1, tensor, "c_xn", 16 * (NFC + 1))
            for gb in range(NGB):
                for q in range(4):
                    k = gb * 4 + q
                    W(S1, tensor, "c_cv", 8 * (k + 1))
                    for tt in range(NTT):
                        j = q * NTT + tt
                        gj = gb * 4 * NTT + j
                        W(S1, tensor, "c_ge", gj - 1)
                        for fcx in range(NFC):
                            ins = tensor.matmul(pg[j % 2][:], xnv[:, fcx, tt * 128:(tt + 1) * 128],
                                                wg_bf[:].rearrange("p (a c) -> p a c", a=NFC)[:, fcx, :],
                                                start=(fcx == 0), stop=(fcx == NFC - 1))
                        S1["c_gmm"].inc(ins)
                        W(S1, tensor, "c_ue", gj - 1)
                        for fcx in range(NFC):
                            ins = tensor.matmul(pu[j % 2][:], xnv[:, fcx, tt * 128:(tt + 1) * 128],
                                                wu_bf[:].rearrange("p (a c) -> p a c", a=NFC)[:, fcx, :],
                                                start=(fcx == 0), stop=(fcx == NFC - 1))
                        S1["c_umm"].inc(ins)
                for tt in range(NTT):
                    it = gb * NTT + tt
                    W(S1, tensor, "c_sp", 8 * (it + 1))
                    for j in range(4):
                        i = it * 4 + j
                        W(S1, tensor, "c_te", i - 1)
                        for sub in range(4):
                            ins = tensor.transpose(pT[i % 2][:, sub * 128:(sub + 1) * 128],
                                                   hqf[j][:, tt * 512 + sub * 128: tt * 512 + (sub + 1) * 128],
                                                   ident_sb[:])
                        S1["c_tr"].inc(ins)

        @block.gpsimd
        def _(gpsimd):
            for gb in range(NGB):
                for tt in range(NTT):
                    it = gb * NTT + tt
                    for j in range(4):
                        i = it * 4 + j
                        W(S1, gpsimd, "c_te", i + 1)
                        row0 = j * 2048 + gb * 512
                        dst = hsp[row0:row0 + 512, tt * 128:(tt + 1) * 128]
                        dst = dst.rearrange("(a p) c -> p a c", p=128)
                        S1[f"c_o{i % 2}"].inc_dma(gpsimd.dma_start(
                            out=dst, in_=hto[i % 2][:].rearrange("p (a c) -> p a c", a=4)))

        @block.sync
        def _(sync):
            sync.wait_ge(S1["c_o0"].h, S1["c_o0"].base + S1["c_o0"].n)
            sync.wait_ge(S1["c_o1"].h, S1["c_o1"].base + S1["c_o1"].n)
    st1.close()

    st2 = ExitStack()
    sb2 = lambda *a: st2.enter_context(nc.sbuf_tensor(_u(a[0]), *a[1:]))
    ps2 = lambda *a: st2.enter_context(nc.psum_tensor(_u(a[0]), *a[1:]))
    wst = sb2("wst", [128, 32 * 512], F32)
    wd_bf = sb2("wd_bf", [128, 64 * 512], BF16)
    h_sb = [sb2(f"h_sb{i}", [128, 64 * 256], BF16) for i in range(2)]
    x1t = [sb2(f"x1t{i}", [128, 256], F32) for i in range(2)]
    y2t = [sb2(f"y2t{i}", [128, 256], F32) for i in range(2)]
    acc = sb2("acc_c2", [128, 256], F32)
    x2t = [sb2(f"x2t{i}", [128, 256], F32) for i in range(2)]
    pb8 = [ps2(f"pb8_{i}", [128, 256], F32) for i in range(8)]
    S2 = mk_sems(nc, stack, ["d_w", "d_cv", "d_h0", "d_h1", "d_mm", "d_x0", "d_x1",
                             "d_y0", "d_y1", "d_add", "d_fin", "d_o0", "d_o1", "d_v"])
    hv = [h_sb[i][:].rearrange("p (a c) -> p a c", a=64) for i in range(2)]

    with nc.Block() as block:
        @block.sync
        def _(sync):
            for og in range(4):
                for hf in range(2):
                    W(S2, sync, "d_cv", 2 * og + hf)
                    src = wd_in[4096 * hf:4096 * (hf + 1), 512 * og:512 * (og + 1)].rearrange("(a p) c -> p a c", p=128)
                    S2["d_w"].inc_dma(sync.dma_start(out=wst[:].rearrange("p (a c) -> p a c", a=32), in_=src))
                for it in range(NT9):
                    t0 = it * 256
                    i5 = og * NT9 + it
                    W(S2, sync, "d_mm", 4 * (i5 - 1))
                    src = hsp[:, t0:t0 + 256].rearrange("(a p) c -> p a c", p=128)
                    S2[f"d_h{i5 % 2}"].inc_dma(sync.dma_start(out=hv[i5 % 2][:, :, :], in_=src))

        @block.gpsimd
        def _(gpsimd):
            for og in range(4):
                for it in range(NT9):
                    t0 = it * 256
                    for j in range(4):
                        oc = 4 * og + j
                        i = (og * NT9 + it) * 4 + j
                        W(S2, gpsimd, "d_add", i - 1)
                        S2[f"d_x{i % 2}"].inc_dma(gpsimd.dma_start(
                            out=x1t[i % 2][:], in_=x1_in[oc * 128:(oc + 1) * 128, t0:t0 + 256]))
                        W(S2, gpsimd, "d_fin", i - 1)
                        S2[f"d_y{i % 2}"].inc_dma(gpsimd.dma_start(
                            out=y2t[i % 2][:], in_=y2_in[oc * 128:(oc + 1) * 128, t0:t0 + 256]))

        @block.vector
        def _(vector):
            for og in range(4):
                for hf in range(2):
                    W(S2, vector, "d_w", 16 * (2 * og + hf + 1))
                    if og >= 1 and hf == 0:
                        W(S2, vector, "d_mm", 4 * og * NT9)
                    S2["d_cv"].inc(vector.tensor_copy(out=wd_bf[:, hf * 16384:(hf + 1) * 16384], in_=wst[:]))
                for it in range(NT9):
                    for j in range(4):
                        oc = 4 * og + j
                        i = (og * NT9 + it) * 4 + j
                        W(S2, vector, "d_mm", i + 1)
                        W(S2, vector, f"d_x{i % 2}", 16 * (i // 2 + 1))
                        W(S2, vector, "d_fin", i)
                        S2["d_add"].inc(vector.scalar_tensor_tensor(out=acc[:], in0=pb8[(it % 2) * 4 + j][:],
                                                                    scalar=1.0, in1=x1t[i % 2][:],
                                                                    op0=ALU.mult, op1=ALU.add))
                        W(S2, vector, "d_add", S2["d_add"].n)
                        W(S2, vector, f"d_y{i % 2}", 16 * (i // 2 + 1))
                        W(S2, vector, f"d_o{i % 2}", 16 * (i // 2))
                        S2["d_fin"].inc(vector.tensor_tensor(out=x2t[i % 2][:], in0=acc[:],
                                                             in1=y2t[i % 2][:], op=ALU.add))

        @block.tensor
        def _(tensor):
            for og in range(4):
                W(S2, tensor, "d_cv", 2 * (og + 1))
                for it in range(NT9):
                    i5 = og * NT9 + it
                    W(S2, tensor, f"d_h{i5 % 2}", 16 * (i5 // 2 + 1))
                    if i5 >= 2:
                        W(S2, tensor, "d_add", 4 * (i5 - 1))
                    for ic in range(64):
                        for j in range(4):
                            bank = (it % 2) * 4 + j
                            ins = tensor.matmul(pb8[bank][:],
                                                wd_bf[:, ic * 512 + j * 128:ic * 512 + (j + 1) * 128],
                                                hv[i5 % 2][:, ic, :],
                                                start=(ic == 0), stop=(ic == 63))
                            if ic == 63:
                                S2["d_mm"].inc(ins)

        @block.scalar
        def _(scalar):
            for og in range(4):
                for it in range(NT9):
                    t0 = it * 256
                    for j in range(4):
                        oc = 4 * og + j
                        i = (og * NT9 + it) * 4 + j
                        W(S2, scalar, "d_fin", i + 1)
                        S2[f"d_o{i % 2}"].inc_dma(scalar.dma_start(
                            out=x2_out[oc * 128:(oc + 1) * 128, t0:t0 + 256], in_=x2t[i % 2][:]))

        @block.sync
        def _(sync):
            sync.wait_ge(S2["d_o0"].h, S2["d_o0"].base + S2["d_o0"].n)
            sync.wait_ge(S2["d_o1"].h, S2["d_o1"].base + S2["d_o1"].n)
    st2.close()
    return nc



def perm_index(NIw):
    ar = np.arange(NIw)
    return 4 * (ar % (NIw // 4)) + ar // (NIw // 4)


def perm_cols(w):
    return np.ascontiguousarray(w[:, perm_index(w.shape[1])])


def perm_rows(w):
    return np.ascontiguousarray(w[perm_index(w.shape[0]), :])






S_FULL = 4096
TOWN = 2048
TW = 2176
_PAIRS = [[0, 1], [2, 3], [4, 5], [6, 7]]
_CACHE = {}


def build_fused():
    nc = bass.Bass("TRN2", target_bir_lowering=False, debug=False)
    gstack = ExitStack()
    nc._ag_sem_pool = []
    nc._ag_sem_stack = gstack

    attn_loc = nc.dram_tensor("attn_loc", [1024, 4096], BF16, kind="Internal")
    xn_full = nc.dram_tensor("xn_full", [2048, 4096], BF16, kind="Internal")
    attn_gat = nc.dram_tensor("attn_gat", [2048, 4096], BF16, kind="Internal")
    x1_t = nc.dram_tensor("x1_t", [2048, TW], F32, kind="Internal")
    xn1_t = nc.dram_tensor("xn1_t", [2048, TW], BF16, kind="Internal")
    xcT_t = nc.dram_tensor("xcT_t", [TOWN, 4224], BF16, kind="Internal")
    z_t = nc.dram_tensor("z_t", [TOWN, 4096], BF16, kind="Internal")
    dtf_t = nc.dram_tensor("dtf_t", [32, TW], F32, kind="Internal")
    Yp_t = nc.dram_tensor("Yp_t", [TOWN, 4096], BF16, kind="Internal")
    carry_t = nc.dram_tensor("carry_t", [64, 4096], F32, kind="Internal")
    carry_gat = nc.dram_tensor("carry_gat", [128, 4096], F32, kind="Internal")
    acs_t = nc.dram_tensor("acs_t", [512, 128], F32, kind="Internal")
    cums_t = nc.dram_tensor("cums_t", [32, 16], F32, kind="Internal")
    y2_t = nc.dram_tensor("y2_t", [2048, TOWN], F32, kind="Internal")
    x2_o = nc.dram_tensor("x2", [2048, TOWN], F32, kind="ExternalOutput")
    offs_in = nc.dram_tensor("offs", [1, 2], I32, kind="ExternalInput")
    mask_in = nc.dram_tensor("mask64", [64, 1], F32, kind="ExternalInput")

    io = {"attn": attn_loc.ap(), "xn": xn_full.ap(), "x1": x1_t.ap(), "xn1": xn1_t.ap(),
          "xcT": xcT_t.ap(), "z_tm": z_t.ap(), "dt_f": dtf_t.ap(), "Yp": Yp_t.ap(),
          "carry": carry_t.ap(), "acs_all": acs_t.ap(), "cums": cums_t.ap(),
          "carry0": carry_gat.ap()[0:64, :], "y2": y2_t.ap(), "x2": x2_o.ap()}

    offs_sb = gstack.enter_context(nc.sbuf_tensor("offs_sb", [1, 2], I32))
    mask_sb = gstack.enter_context(nc.sbuf_tensor("mask_sb", [64, 1], F32))
    psem = gstack.enter_context(nc.semaphore("pre_sem"))
    ccsem = gstack.enter_context(nc.semaphore("cc_sem"))

    with nc.Block() as block:
        @block.sync
        def _(sync):
            sync.dma_start(out=offs_sb[:], in_=offs_in.ap()).then_inc(psem, 16)
            sync.dma_start(out=mask_sb[:], in_=mask_in.ap()).then_inc(psem, 16)
            sync.wait_ge(psem, 32)

    st = ExitStack(); emit_a1(nc, S_FULL, st, io=io); st.close()

    with nc.Block() as block:
        @block.gpsimd
        def _(g):
            g.collective_compute("AllGather", ALU.bypass, replica_groups=_PAIRS,
                                 ins=[attn_loc.ap().opt()],
                                 outs=[attn_gat.ap().opt()]).then_inc(ccsem, 1)
            g.wait_ge(ccsem, 1)

    st = ExitStack()
    emit_mid(nc, TW, st, io=io, win={"offs": offs_sb[:], "attn_full": attn_gat.ap(),
                                     "xn_full": xn_full.ap()})
    st.close()
    st = ExitStack(); emit_b1(nc, TW, st, io=io); st.close()
    st = ExitStack(); emit_b2a(nc, TOWN, st, io=io); st.close()

    with nc.Block() as block:
        @block.gpsimd
        def _(g):
            g.collective_compute("AllGather", ALU.bypass, replica_groups=_PAIRS,
                                 ins=[carry_t.ap().opt()],
                                 outs=[carry_gat.ap().opt()]).then_inc(ccsem, 1)
            g.wait_ge(ccsem, 2)

    st = ExitStack(); emit_b2b(nc, TOWN, st, io=io, mask=mask_sb[:, 0:1]); st.close()
    st = ExitStack(); emit_c(nc, TW, st, io=io); st.close()
    nc._ag_keep = gstack
    return nc


def _fused_nc():
    if "f" not in _CACHE:
        _CACHE["f"] = build_fused()
    return _CACHE["f"]


def kernel(**ins):
    f32 = np.float32
    ac = np.ascontiguousarray
    nc = _fused_nc()
    x = np.asarray(ins["L0_attn_x"], f32)

    bo = np.asarray(ins["L0_attn_bo"], f32)
    sg = np.asarray(ins["L0_ffn_s_gate"], f32)
    su = np.asarray(ins["L0_ffn_s_up"], f32)
    sd = np.asarray(ins["L0_ffn_s_down"], f32)
    winT = ac(np.asarray(ins["L1_attn_W_in"], f32).T)
    cw = np.asarray(ins["L1_attn_conv_w"], f32)
    cb = np.asarray(ins["L1_attn_conv_b"], f32)
    dtb = np.asarray(ins["L1_attn_dt_bias"], f32)
    alog = ac(np.asarray(ins["L1_attn_A_log"], f32).reshape(32, 1))
    dexp = ac(np.repeat(np.asarray(ins["L1_attn_D"], f32), 128).reshape(1, 4096))
    woutT = ac(np.asarray(ins["L1_attn_W_out"], f32).T)
    wg1p = perm_cols(np.asarray(ins["L1_ffn_w_gate"], f32))
    wu1p = perm_cols(np.asarray(ins["L1_ffn_w_up"], f32))
    wd1p = perm_rows(np.asarray(ins["L1_ffn_w_down"], f32))
    convw = ac(cw.reshape(33, 128, 4).transpose(1, 0, 2).reshape(128, 132))
    convb = ac(cb.reshape(33, 128).T)
    xTf = [ac(x[b].T) for b in range(4)]

    maps = []
    for c in range(8):
        b, e = c // 2, c % 2
        T0, E0 = 2048 * e, 1920 * e
        wcols = np.r_[T0:T0 + 2048, E0:E0 + 128]
        m = {"x": ac(x[b]), "xT": ac(xTf[b][:, wcols]),
             "offs": np.array([[T0, E0]], np.int32),
             "mask64": np.full((64, 1), float(e), f32),
             "wo_c": np.asarray(ins["L0_attn_wo_codes"], f32),
             "wo_s": np.asarray(ins["L0_attn_wo_scales"], f32),
             "bo_r": ac(bo.reshape(16, 128).T),
             "wg8": np.asarray(ins["L0_ffn_w_gate"], np.int8),
             "wu8": np.asarray(ins["L0_ffn_w_up"], np.int8),
             "wd8": np.asarray(ins["L0_ffn_w_down"], np.int8),
             "sg_r": ac(sg.reshape(64, 128).T),
             "su_r": ac(su.reshape(64, 128).T),
             "sd_r": ac(sd.reshape(16, 128).T),
             "winT": winT, "convw": convw, "convb": convb,
             "padmask": np.full((128, 1), float(e), f32),
             "dtb": ac(dtb.reshape(32, 1)),
             "A_log": alog, "D_exp": dexp, "woutT": woutT,
             "wg1p": wg1p, "wu1p": wu1p, "wd1p": wd1p}
        sl = slice(1024 * e, 1024 * (e + 1))
        for nm in "qkv":
            m["w" + nm] = ac(np.asarray(ins[f"L0_attn_w{nm}_codes"], f32)[:, sl])
            m["s" + nm] = ac(np.asarray(ins[f"L0_attn_w{nm}_scales"], f32)[:, sl])
            bb = np.asarray(ins[f"L0_attn_b{nm}"], f32)[sl]
            m["b" + nm] = ac(bb.reshape(8, 128).T)
        m["bvr"] = ac(np.asarray(ins["L0_attn_bv"], f32)[sl].reshape(1, 1024))
        maps.append(m)

    res = run_bass_kernel_spmd(nc, maps, core_ids=list(range(8))).results
    out = np.empty((4, 4096, 2048), f32)
    for c in range(8):
        b, e = c // 2, c % 2
        T0 = 2048 * e
        out[b, T0:T0 + 2048, :] = np.asarray(res[c]["x2"], f32).T
    return out
